# version 14
# speedup vs baseline: 1.0380x; 1.0080x over previous
import contextlib
import numpy as np
import concourse.bass as bass
import concourse.mybir as mybir

F32 = mybir.dt.float32
BF16 = mybir.dt.bfloat16
AF = mybir.ActivationFunctionType
ALU = mybir.AluOpType
AX = mybir.AxisListType

D = 1024
NCH = D // 128
NH = 16
HD = 64


class ConstTable:
    def __init__(self, nc, sb_tensor, cap):
        self.nc = nc
        self.sb = sb_tensor
        self.cap = cap
        self.vals = {}
    def f32(self, v):
        v = float(np.float32(v))
        if v not in self.vals:
            idx = len(self.vals)
            assert idx < self.cap, "const table full"
            self.vals[v] = idx
            self.nc.const_aps.register(mybir.dt.float32, v, self.sb[:, idx:idx + 1])
        return v
    def host_array(self):
        arr = np.zeros((128, self.cap), dtype=np.float32)
        for v, i in self.vals.items():
            arr[:, i] = v
        return arr


class Res:
    __slots__ = ("name", "writer", "readers", "sem", "sem_count")
    def __init__(self, name, sem=None):
        self.name = name
        self.writer = None
        self.readers = []
        self.sem = sem
        self.sem_count = 0


class Emitter:
    def __init__(self, nc, stack):
        self.nc = nc
        self.stack = stack
        self.engines = {}
        self.sems = {}
        self.counts = {}
        self.watermarks = {}
        self.n_waits = 0
        self.n_inst = 0
        for name, handle in (("pe", nc.tensor), ("act", nc.scalar),
                             ("dve", nc.vector), ("pool", nc.gpsimd),
                             ("sync", nc.sync)):
            self.engines[name] = handle
            self.watermarks[name] = {}
            if name != "sync":
                self._new_sem(name)

    def _new_sem(self, name):
        if name in self.sems:
            i = 2
            while f"{name}{i}" in self.sems:
                i += 1
            name = f"{name}{i}"
        sem = self.stack.enter_context(self.nc.semaphore(f"sem_{name}"))
        self.sems[name] = sem
        self.counts[name] = 0
        return name

    def res(self, name):
        return Res(name)

    def dma_res(self, name):
        return Res(name, sem=self._new_sem(name))


    def _wait(self, eng, sem_name, value):
        if value <= 0:
            return
        wm = self.watermarks[eng]
        if wm.get(sem_name, 0) >= value:
            return
        self.engines[eng].wait_ge(self.sems[sem_name], value)
        wm[sem_name] = value
        self.n_waits += 1

    def _deps(self, eng, reads, writes):
        def need(sname):
            return not (eng == "pe" and sname == "pe")
        for r in reads:
            if r.writer is not None and need(r.writer[0]):
                self._wait(eng, *r.writer)
        for w in writes:
            if w.writer is not None and need(w.writer[0]):
                self._wait(eng, *w.writer)
            for (sname, c) in w.readers:
                if need(sname):
                    self._wait(eng, sname, c)

    def emit(self, eng, fn, reads=(), writes=()):
        self._deps(eng, reads, writes)
        inst = fn(self.engines[eng])
        if isinstance(inst, list):
            inst = inst[-1]
        self.counts[eng] += 1
        cnt = self.counts[eng]
        inst.then_inc(self.sems[eng], 1)
        self.n_inst += 1
        for r in reads:
            r.readers.append((eng, cnt))
        for w in writes:
            w.writer = (eng, cnt)
            w.readers = []
        return inst

    def dma(self, queue, fn, reads=(), writes=(), dest=None):
        if dest is None:
            assert len(writes) == 1
            dest = writes[0]
        assert dest.sem is not None, f"{dest.name}: DMA dest needs dma_res()"
        self._deps(queue, reads, writes)
        inst = fn(self.engines[queue])
        self.counts[dest.sem] += 16
        cnt = self.counts[dest.sem]
        dest.sem_count = cnt
        inst.then_inc(self.sems[dest.sem], 16)
        self.n_inst += 1
        for r in reads:
            r.readers.append((dest.sem, cnt))
        for w in writes:
            w.writer = (dest.sem, cnt)
            w.readers = []
        return inst

    def wait_all(self, eng):
        for sname, c in self.counts.items():
            self._wait(eng, sname, c)


class Ctx:
    def __init__(self, T):
        self.T = T
        self.NT = T // 512
        self.nc = bass.Bass("TRN2", target_bir_lowering=False)
        self.stack = contextlib.ExitStack()
        self.host_consts = {}

    def input(self, name, shape):
        ap = self.nc.dram_tensor(name, list(shape), F32, kind="ExternalInput").ap()
        return ap, self.E.res("in_" + name)

    def open(self):
        nc = self.nc
        st = self.stack
        self.E = Emitter(nc, st)
        self.prefix = ""
        self.scope = None
        def _sb(name, shape, dt):
            tgt = self.scope if self.scope is not None else st
            return tgt.enter_context(nc.sbuf_tensor(self.prefix + name, shape, dt))
        self.sb = _sb
        self.PA = st.enter_context(nc.psum_tensor("PA", [128, 1024], F32))
        self.PB = st.enter_context(nc.psum_tensor("PB", [128, 1024], F32))
        self.PC = st.enter_context(nc.psum_tensor("PC", [128, 1024], F32))
        self.PL = st.enter_context(nc.psum_tensor("PL", [128, 512], F32))
        self.PT = st.enter_context(nc.psum_tensor("PT", [128, 128], BF16))
        E = self.E
        self.rPA = [E.res("PAh0"), E.res("PAh1")]
        self.rPB = [E.res("PBh0"), E.res("PBh1")]
        self.rPC = [E.res("PCh0"), E.res("PCh1")]
        self.rPL = E.res("PL")
        self.rPT = E.res("PT")
        self.const_sb = self.sb("const_sb", [128, 48], F32)
        self.CT = ConstTable(nc, self.const_sb, 48)
        self.r_const = E.dma_res("consts")
        self.consts_in = nc.dram_tensor("consts", [128, 48], F32, kind="ExternalInput").ap()
        self.ident_bf = self.sb("ident_bf", [128, 128], BF16)
        self.ident_f32 = self.sb("ident_f32", [128, 128], F32)
        self.r_ident = E.dma_res("ident")
        self.r_ident_b = E.dma_res("ident_b")
        self.ident_in = nc.dram_tensor("ident_in", [128, 128], F32, kind="ExternalInput").ap()
        self.ones_f32 = self.sb("ones_f32", [128, 2], F32)
        self.r_ones = E.dma_res("ones")
        self.ones_in = nc.dram_tensor("ones_in", [128, 2], F32, kind="ExternalInput").ap()
        self.ones_row_f32 = self.sb("ones_row_f32", [1, 128], F32)
        self.ones_row_bf = self.sb("ones_row_bf", [1, 128], BF16)
        self.r_ones_row = E.dma_res("ones_row")
        self.r_ones_row_b = E.dma_res("ones_row_b")
        self.ones_row_in = nc.dram_tensor("ones_row_in", [1, 128], F32, kind="ExternalInput").ap()
        E.dma("sync", lambda s: s.dma_start(out=self.const_sb[:], in_=self.consts_in), writes=[self.r_const])
        E.dma("sync", lambda s: s.dma_start(out=self.ident_f32[:], in_=self.ident_in), writes=[self.r_ident])
        E.dma("pool", lambda g: g.dma_start(out=self.ident_bf[:], in_=self.ident_in), writes=[self.r_ident_b])
        E.dma("sync", lambda s: s.dma_start(out=self.ones_f32[:], in_=self.ones_in), writes=[self.r_ones])
        E.dma("sync", lambda s: s.dma_start(out=self.ones_row_f32[:], in_=self.ones_row_in), writes=[self.r_ones_row])
        E.dma("pool", lambda g: g.dma_start(out=self.ones_row_bf[:], in_=self.ones_row_in), writes=[self.r_ones_row_b])
        self.host_consts["ident_in"] = np.eye(128, dtype=np.float32)
        self.host_consts["ones_in"] = np.ones((128, 2), dtype=np.float32)
        self.host_consts["ones_row_in"] = np.ones((1, 128), dtype=np.float32)
        self.slabA = self.sb("slabA", [128, NCH * 512], F32)
        self.slabB = self.sb("slabB", [128, NCH * 512], F32)
        self.r_slabA = E.dma_res("slabA")
        self.r_slabB = E.res("slabB")
        self.tslab = [self.sb(f"tslab{i}", [128, NCH * 512], BF16) for i in range(2)]
        self.r_tslab = [E.dma_res(f"tslab{i}") for i in range(2)]
        self.tepb = self.sb("tepb", [128, NCH * 512], BF16)
        self.r_tepb = self.E.res("tepb")
        self.stat = self.sb("stat", [1, 2048], F32)
        self.r_stat = E.res("stat")
        self.masks = None
        self.hsel = None
        self.ensure_masks()
        self.ensure_hsel()

    def ensure_masks(self):
        if self.masks is not None:
            return
        _sc, _pf = self.scope, self.prefix
        self.scope, self.prefix = None, ""
        try:
            self._ensure_masks_inner()
        finally:
            self.scope, self.prefix = _sc, _pf

    def _ensure_masks_inner(self):
        E, nc = self.E, self.nc
        self.masks = self.sb("masks", [128, 4 * 512], BF16)
        self.r_masks = E.dma_res("masks")
        self.masks_in = nc.dram_tensor("masks_in", [128, 2048], F32, kind="ExternalInput").ap()
        E.dma("pool", lambda g: g.dma_start(out=self.masks[:], in_=self.masks_in), writes=[self.r_masks])
        mk = np.zeros((128, 4, 512), np.float32)
        p = np.arange(128)[:, None]
        f = np.arange(512)[None, :]
        for ci, coff in enumerate((0, 128, 256, 384)):
            mk[:, ci, :] = (f >= p + coff).astype(np.float32)
        self.host_consts["masks_in"] = np.ascontiguousarray(mk.reshape(128, 2048))

    def push_scope(self, prefix):
        import contextlib as _ctx
        assert self.scope is None
        self.scope = _ctx.ExitStack()
        self.prefix = prefix

    def pop_scope(self):
        self.scope.close()
        self.scope = None
        self.prefix = ""

    def barrier(self):
        for eng in ("pe", "act", "dve", "pool", "sync"):
            self.E.wait_all(eng)

    def ensure_hsel(self):
        if getattr(self, "hsel", None) is not None:
            return
        _sc, _pf = self.scope, self.prefix
        self.scope, self.prefix = None, ""
        try:
            self._ensure_hsel_inner()
        finally:
            self.scope, self.prefix = _sc, _pf

    def _ensure_hsel_inner(self):
        E, nc = self.E, self.nc
        self.hsel = self.sb("hsel", [128, 2], F32)
        self.hsel2 = self.sb("hsel2", [2, 128], F32)
        self.hsel2b = self.sb("hsel2b", [66, 128], F32)
        self.r_hsel = E.dma_res("hsel")
        self.hsel_in = nc.dram_tensor("hsel_in", [128, 2], F32, kind="ExternalInput").ap()
        self.hsel2_in = nc.dram_tensor("hsel2_in", [2, 128], F32, kind="ExternalInput").ap()
        self.hsel2b_in = nc.dram_tensor("hsel2b_in", [66, 128], F32, kind="ExternalInput").ap()
        E.dma("sync", lambda s: s.dma_start(out=self.hsel[:], in_=self.hsel_in), writes=[self.r_hsel])
        E.dma("sync", lambda s: s.dma_start(out=self.hsel2[:], in_=self.hsel2_in), writes=[self.r_hsel])
        E.dma("sync", lambda s: s.dma_start(out=self.hsel2b[:], in_=self.hsel2b_in), writes=[self.r_hsel])
        h = np.zeros((128, 2), np.float32)
        h[0:64, 0] = 1.0
        h[64:128, 1] = 1.0
        self.host_consts["hsel_in"] = h
        self.host_consts["hsel2_in"] = np.ascontiguousarray(h.T)
        h2b = np.zeros((66, 128), np.float32)
        h2b[64, 0:64] = 1.0
        h2b[65, 64:128] = 1.0
        self.host_consts["hsel2b_in"] = h2b

    def finish(self):
        self.host_consts["consts"] = self.CT.host_array()
        return self.host_consts

    def close(self):
        self.stack.close()


def chunked_vec(v):
    return np.ascontiguousarray(v.reshape(-1, 128).T)




def ap_slab(hbm, j, w=512, p=128):
    return hbm[:, j * w:(j + 1) * w].rearrange("(c p) t -> p c t", p=p)


def sb_slab(sb_ap, w=512):
    return sb_ap.rearrange("p (c t) -> p c t", t=w)


def emit_lnstats(C, buf, r_buf, eps, chunks=None):
    E, CT = C.E, C.CT
    PL, PB = C.PL, C.PB
    sq, r_sq = C.slabB, C.r_slabB
    if chunks is None:
        chunks = [buf[:, c * 512:(c + 1) * 512] for c in range(NCH)]
    for c in range(NCH):
        E.emit("act", lambda s, c=c: s.activation(out=sq[:, c * 512:(c + 1) * 512], in_=chunks[c],
                                                  func=AF.Square),
               reads=[r_buf], writes=[r_sq])
    E.emit("pe", lambda t: [t.matmul(PL[0:1, :], C.ones_f32[:, 0:1], chunks[c],
                                     start=(c == 0), stop=(c == NCH - 1)) for c in range(NCH)],
           reads=[r_buf, C.r_ones], writes=[C.rPL])
    E.emit("pe", lambda t: [t.matmul(PL[32:33, :], C.ones_f32[:, 0:1], sq[:, c * 512:(c + 1) * 512],
                                     start=(c == 0), stop=(c == NCH - 1)) for c in range(NCH)],
           reads=[r_sq, C.r_ones], writes=[C.rPL])
    st_, r_st = C.stat, C.r_stat
    E.emit("dve", lambda v: v.tensor_scalar(out=st_[:, 0:512], in0=PL[0:1, :],
                                            scalar1=1.0 / D, scalar2=None, op0=ALU.mult),
           reads=[C.rPL], writes=[r_st])
    E.emit("dve", lambda v: v.tensor_tensor(out=st_[:, 512:1024], in0=st_[:, 0:512],
                                            in1=st_[:, 0:512], op=ALU.mult),
           reads=[r_st], writes=[r_st])
    E.emit("dve", lambda v: v.scalar_tensor_tensor(out=st_[:, 1024:1536], in0=PL[32:33, :],
                                                   scalar=1.0 / D, in1=st_[:, 512:1024],
                                                   op0=ALU.mult, op1=ALU.subtract),
           reads=[C.rPL, r_st], writes=[r_st])
    E.emit("act", lambda s: s.activation(out=st_[:, 512:1024], in_=st_[:, 1024:1536], func=AF.Ln,
                                         bias=CT.f32(eps)),
           reads=[r_st, C.r_const], writes=[r_st])
    E.emit("act", lambda s: s.activation(out=st_[:, 1536:2048], in_=st_[:, 512:1024], func=AF.Exp,
                                         scale=CT.f32(-0.5)),
           reads=[r_st, C.r_const], writes=[r_st])
    E.emit("pe", lambda t: t.matmul(PB[:, 0:512], C.ones_row_f32[0:1, :], st_[:, 0:512],
                                    start=True, stop=True),
           reads=[r_st, C.r_ones_row], writes=[C.rPB[0]])
    E.emit("pe", lambda t: t.matmul(PB[:, 512:1024], C.ones_row_f32[0:1, :], st_[:, 1536:2048],
                                    start=True, stop=True),
           reads=[r_st, C.r_ones_row], writes=[C.rPB[1]])


def emit_ln_to_tT(C, xT_in, r_xin, tT_hbm, r_tT, eps=1e-5):
    E, T, NT = C.E, C.T, C.NT
    xs, r_xs = C.slabA, C.r_slabA
    tb, r_tb = C.tslab[0], C.r_tslab[0]
    for j in range(NT):
        E.dma("sync", lambda s, j=j: s.dma_start(out=sb_slab(xs[:]), in_=ap_slab(xT_in, j)),
              reads=[r_xin], writes=[r_xs])
        emit_lnstats(C, xs, r_xs, eps)
        for c in range(NCH):
            sl = slice(c * 512, (c + 1) * 512)
            E.emit("dve", lambda v, sl=sl: v.tensor_tensor(out=C.slabB[:, sl], in0=xs[:, sl],
                                                           in1=C.PB[:, 0:512], op=ALU.subtract),
                   reads=[r_xs, C.rPB[0]], writes=[C.r_slabB])
            E.emit("dve", lambda v, sl=sl: v.tensor_tensor(out=tb[:, sl], in0=C.slabB[:, sl],
                                                           in1=C.PB[:, 512:1024], op=ALU.mult),
                   reads=[C.r_slabB, C.rPB[1]], writes=[r_tb])
        E.dma("pool", lambda g, j=j: g.dma_start(out=ap_slab(tT_hbm, j), in_=sb_slab(tb[:])),
              reads=[r_tb], writes=[r_tT], dest=r_tT)


def emit_ep_ln(C, ybuf, r_ybuf, jw, tT_next, r_tTn, eps=1e-5, chunks=None, tb_idx=0):
    E, T = C.E, C.T
    if tT_next is None:
        return
    emit_lnstats(C, ybuf, r_ybuf, eps, chunks=chunks)
    if chunks is None:
        chunks = [ybuf[:, c * 512:(c + 1) * 512] for c in range(NCH)]
    tb, r_tb = C.tepb, C.r_tepb
    for c in range(NCH):
        sl = slice(c * 512, (c + 1) * 512)
        E.emit("dve", lambda v, sl=sl, c=c: v.tensor_tensor(out=C.slabB[:, sl], in0=chunks[c],
                                                            in1=C.PB[:, 0:512], op=ALU.subtract),
               reads=[r_ybuf, C.rPB[0]], writes=[C.r_slabB])
        E.emit("dve", lambda v, sl=sl: v.tensor_tensor(out=tb[:, sl], in0=C.slabB[:, sl],
                                                       in1=C.PB[:, 512:1024], op=ALU.mult),
               reads=[C.r_slabB, C.rPB[1]], writes=[r_tb])
    E.dma("pool", lambda g, jw=jw: g.dma_start(out=ap_slab(tT_next, jw), in_=sb_slab(tb[:])),
          reads=[r_tb], writes=[r_tTn], dest=r_tTn)

def precast(C, W, name, queue="pool"):
    if name + "_bf" in W:
        return
    E, nc = C.E, C.nc
    ap = W[name]
    bf = nc.dram_tensor(name + "_bfs", list(ap.shape), BF16, kind="Internal").ap()
    r = E.dma_res(name + "_bfr")
    E.dma(queue, lambda g: g.dma_start(out=bf, in_=ap), reads=[W["r_" + name]], writes=[r])
    W[name + "_bf"] = bf
    W["r_" + name + "_bf"] = r


def load_weight_lhsT(C, w_hbm, r_w, dst_sb, r_dst, cols, queue="pool"):
    E = C.E
    m = cols.stop - cols.start
    nrow = w_hbm.shape[0] // 128
    src = w_hbm[:, cols].rearrange("(c p) m -> p c m", p=128)
    dst3 = dst_sb[:, 0:nrow * m].rearrange("p (c m) -> p c m", m=m)
    E.dma(queue, lambda g: g.dma_start(out=dst3, in_=src),
          reads=[r_w], writes=[r_dst])


def load_w(C, W, name, dst_sb, r_dst, cols, queue="act"):
    load_weight_lhsT(C, W[name + "_bf"], W["r_" + name + "_bf"], dst_sb, r_dst, cols, queue)


def emit_attn_a0(C, xT_in, r_xin, xT_out, r_xout, tT_hbm, r_tT, W, tT_next=None, r_tTn=None, skip_ln=False):
    E, nc, CT, T, NT = C.E, C.nc, C.CT, C.T, C.NT
    NQ = T // 512
    NTK = T // 128
    SHIFT = 8.0
    PA, PB, PC, PL, PT = C.PA, C.PB, C.PC, C.PL, C.PT
    C.ensure_masks()

    precast(C, W, "w_qkv")
    precast(C, W, "w_proj")
    if not skip_ln:
        emit_ln_to_tT(C, xT_in, r_xin, tT_hbm, r_tT)

    QT = C.sb("aa_QT", [128, T], BF16); r_QT = E.res("aa_QT")
    KT = C.sb("aa_KT", [128, T], BF16); r_KT = E.res("aa_KT")
    VTs = C.sb("aa_VTs", [128, 512], BF16); r_VTs = E.res("aa_VTs")
    Vp = C.sb("aa_Vp", [128, NTK * 130], BF16); r_Vp = E.res("aa_Vp")
    Pt = [C.sb(f"aa_P{i}", [128, 1024], BF16) for i in range(2)]
    r_Pt = [E.res(f"aa_P{i}") for i in range(2)]
    attnO = C.sb("aa_attnO", [128, NCH * T], BF16); r_attnO = E.res("aa_attnO")
    recrow = C.sb("aa_recrow", [1, 1024], F32); r_rec = E.res("aa_recrow")
    rb = C.sb("aa_rb", [128, 1024], F32); r_rb = E.res("aa_rb")
    wstage = C.sb("wstage", [128, NCH * D], BF16); r_wstage = E.dma_res("wstage")

    for i in range(NTK):
        E.emit("dve", lambda v, i=i: v.memset(Vp[:, i * 130 + 64:i * 130 + 65], 1.0), writes=[r_Vp])
        E.emit("dve", lambda v, i=i: v.memset(Vp[:, i * 130 + 129:i * 130 + 130], 1.0), writes=[r_Vp])

    w_qkv, r_wqkv = W["w_qkv"], W["r_w_qkv"]
    wq = wstage[:, 0:NCH * 128]
    wk = wstage[:, NCH * 128:2 * NCH * 128]
    wv = wstage[:, 2 * NCH * 128:3 * NCH * 128]

    for hp in range(NH // 2):
        c0 = hp * 128
        load_w(C, W, "w_qkv", wq, r_wstage, slice(c0, c0 + 128), "act")
        load_w(C, W, "w_qkv", wk, r_wstage, slice(D + c0, D + c0 + 128), "act")
        load_w(C, W, "w_qkv", wv, r_wstage, slice(2 * D + c0, 2 * D + c0 + 128), "act")
        for j in range(NT):
            tb, r_tb = C.tslab[j % 2], C.r_tslab[j % 2]
            E.dma("sync", lambda s, j=j: s.dma_start(out=sb_slab(tb[:]), in_=ap_slab(tT_hbm, j)),
                  reads=[r_tT], writes=[r_tb])
            for wi, (wsb, dst, r_dst) in enumerate(((wq, QT, r_QT), (wk, KT, r_KT))):
                E.emit("pe", lambda t, wsb=wsb, tb=tb, wi=wi: [t.matmul(
                           PB[:, wi * 512:(wi + 1) * 512], wsb[:, c * 128:(c + 1) * 128],
                           tb[:, c * 512:(c + 1) * 512],
                           start=(c == 0), stop=(c == NCH - 1)) for c in range(NCH)],
                       reads=[r_wstage, r_tb], writes=[C.rPB[wi]])
                E.emit("dve", lambda v, dst=dst, j=j, wi=wi: v.tensor_copy(
                           out=dst[:, j * 512:(j + 1) * 512], in_=PB[:, wi * 512:(wi + 1) * 512]),
                       reads=[C.rPB[wi]], writes=[r_dst])
            E.emit("pe", lambda t, tb=tb: [t.matmul(
                       PC[:, 0:512], wv[:, c * 128:(c + 1) * 128], tb[:, c * 512:(c + 1) * 512],
                       start=(c == 0), stop=(c == NCH - 1)) for c in range(NCH)],
                   reads=[r_wstage, r_tb], writes=[C.rPC[0]])
            E.emit("dve", lambda v: v.tensor_copy(out=VTs[:], in_=PC[:, 0:512]),
                   reads=[C.rPC[0]], writes=[r_VTs])
            for k in range(4):
                i = j * 4 + k
                E.emit("pe", lambda t, k=k: t.transpose(PT[:], VTs[:, k * 128:(k + 1) * 128], C.ident_bf[:]),
                       reads=[r_VTs, C.r_ident_b], writes=[C.rPT])
                E.emit("dve", lambda v, i=i: v.tensor_copy(out=Vp[:, i * 130:i * 130 + 64],
                                                           in_=PT[:, 0:64]),
                       reads=[C.rPT], writes=[r_Vp])
                E.emit("dve", lambda v, i=i: v.tensor_copy(out=Vp[:, i * 130 + 65:i * 130 + 129],
                                                           in_=PT[:, 64:128]),
                       reads=[C.rPT], writes=[r_Vp])
        for qb in range(NQ):
            ntk = (qb + 1) * 4
            qc = slice(qb * 512, (qb + 1) * 512)
            def emit_S(i, qc=qc):
                ib = i % 2
                Ps = PB if ib == 0 else PC
                rS = C.rPB if ib == 0 else C.rPC
                E.emit("pe", lambda t, i=i, Ps=Ps, qc=qc: [t.matmul(
                           Ps[:, 0:512], KT[0:64, i * 128:(i + 1) * 128], QT[0:64, qc],
                           start=True, stop=True, tile_position=(0, 0)),
                       t.matmul(
                           Ps[:, 512:1024], KT[64:128, i * 128:(i + 1) * 128], QT[64:128, qc],
                           start=True, stop=True, tile_position=(64, 0))],
                       reads=[r_KT, r_QT], writes=[rS[0], rS[1]])
            def emit_elem(i, qb=qb):
                ib = i % 2
                Ps = PB if ib == 0 else PC
                rS = C.rPB if ib == 0 else C.rPC
                E.emit("act", lambda s, Ps=Ps, ib=ib: s.activation(
                           out=Pt[ib][:], in_=Ps[:, 0:1024], func=AF.Exp,
                           scale=CT.f32(0.125), bias=CT.f32(-SHIFT)),
                       reads=[rS[0], rS[1], C.r_const], writes=[r_Pt[ib]])
                coff = i * 128 - qb * 512
                if coff >= 0:
                    ci = coff // 128
                    for half in range(2):
                        E.emit("dve", lambda v, ib=ib, ci=ci, half=half: v.tensor_tensor(
                                   out=Pt[ib][:, half * 512:(half + 1) * 512],
                                   in0=Pt[ib][:, half * 512:(half + 1) * 512],
                                   in1=C.masks[:, ci * 512:(ci + 1) * 512], op=ALU.mult),
                               reads=[C.r_masks], writes=[r_Pt[ib]])
            def emit_PV(i, ntk=ntk):
                ib = i % 2
                E.emit("pe", lambda t, i=i, ib=ib, ntk=ntk: t.matmul(
                           PA[0:65, 0:512], Vp[:, i * 130:i * 130 + 65], Pt[ib][:, 0:512],
                           start=(i == 0), stop=(i == ntk - 1)),
                       reads=[r_Vp, r_Pt[ib]], writes=[C.rPA[0]])
                E.emit("pe", lambda t, i=i, ib=ib, ntk=ntk: t.matmul(
                           PA[0:65, 512:1024], Vp[:, i * 130 + 65:i * 130 + 130], Pt[ib][:, 512:1024],
                           start=(i == 0), stop=(i == ntk - 1)),
                       reads=[r_Vp, r_Pt[ib]], writes=[C.rPA[1]])
            for i in range(ntk):
                emit_S(i)
                if i > 0:
                    emit_elem(i - 1)
                    emit_PV(i - 1)
            emit_elem(ntk - 1)
            emit_PV(ntk - 1)
            E.emit("act", lambda s: s.activation(out=recrow[:], in_=PA[64:65, 0:1024], func=AF.Ln),
                   reads=[C.rPA[0], C.rPA[1]], writes=[r_rec])
            E.emit("act", lambda s: s.activation(out=recrow[:], in_=recrow[:], func=AF.Exp,
                                                 scale=CT.f32(-1.0)),
                   reads=[r_rec, C.r_const], writes=[r_rec])
            E.emit("pe", lambda t: t.matmul(PB[:, 0:512], C.ones_row_f32[0:1, :], recrow[0:1, 0:512],
                                            start=True, stop=True),
                   reads=[r_rec, C.r_ones_row], writes=[C.rPB[0]])
            E.emit("pe", lambda t: t.matmul(PB[:, 512:1024], C.ones_row_f32[0:1, :], recrow[0:1, 512:1024],
                                            start=True, stop=True),
                   reads=[r_rec, C.r_ones_row], writes=[C.rPB[1]])
            E.emit("dve", lambda v: v.tensor_copy(out=rb[:], in_=PB[:, 0:1024]),
                   reads=[C.rPB[0], C.rPB[1]], writes=[r_rb])
            E.emit("dve", lambda v, hp=hp, qb=qb: v.tensor_tensor(
                       out=attnO[0:64, hp * T + qb * 512: hp * T + (qb + 1) * 512],
                       in0=PA[0:64, 0:512], in1=rb[0:64, 0:512], op=ALU.mult),
                   reads=[C.rPA[0], r_rb], writes=[r_attnO])
            E.emit("dve", lambda v, hp=hp, qb=qb: v.tensor_tensor(
                       out=attnO[64:128, hp * T + qb * 512: hp * T + (qb + 1) * 512],
                       in0=PA[0:64, 512:1024], in1=rb[0:64, 512:1024], op=ALU.mult),
                   reads=[C.rPA[1], r_rb], writes=[r_attnO])

    load_w(C, W, "w_proj", wstage, r_wstage, slice(0, D), "act")
    ybuf, r_ybuf = C.slabA, C.r_slabA
    sq, r_sq = C.slabB, C.r_slabB
    for j in range(NT):
        tb, r_tb = C.tslab[j % 2], C.r_tslab[j % 2]
        E.dma("sync", lambda s, j=j: s.dma_start(out=sb_slab(tb[:]), in_=ap_slab(tT_hbm, j)),
              reads=[r_tT], writes=[r_tb])
        for dc in range(NCH):
            h = dc % 2
            E.emit("pe", lambda t, dc=dc, j=j, h=h: [t.matmul(
                       PC[:, h * 512:(h + 1) * 512],
                       wstage[:, c * D + dc * 128: c * D + (dc + 1) * 128],
                       attnO[:, c * T + j * 512: c * T + (j + 1) * 512],
                       start=(c == 0), stop=(c == NCH - 1)) for c in range(NCH)],
                   reads=[r_wstage, r_attnO], writes=[C.rPC[h]])
            E.emit("dve", lambda v, dc=dc, h=h, tb=tb: v.scalar_tensor_tensor(
                       out=ybuf[:, dc * 512:(dc + 1) * 512], in0=PC[:, h * 512:(h + 1) * 512],
                       scalar=W["bpc"][:, dc:dc + 1], in1=tb[:, dc * 512:(dc + 1) * 512],
                       op0=ALU.add, op1=ALU.add),
                   reads=[C.rPC[h], W["r_bpc"], r_tb], writes=[r_ybuf])
        emit_lnstats(C, ybuf, r_ybuf, 1e-5)
        for dc in range(NCH):
            sl = slice(dc * 512, (dc + 1) * 512)
            E.emit("dve", lambda v, sl=sl: v.tensor_tensor(out=sq[:, sl], in0=ybuf[:, sl],
                                                           in1=PB[:, 0:512], op=ALU.subtract),
                   reads=[r_ybuf, C.rPB[0]], writes=[r_sq])
            E.emit("dve", lambda v, sl=sl: v.tensor_tensor(out=sq[:, sl], in0=sq[:, sl],
                                                           in1=PB[:, 512:1024], op=ALU.mult),
                   reads=[r_sq, C.rPB[1]], writes=[r_sq])
            E.emit("dve", lambda v, sl=sl, dc=dc: v.tensor_scalar(
                       out=sq[:, sl], in0=sq[:, sl], scalar1=W["gammac"][:, dc:dc + 1],
                       scalar2=W["betac"][:, dc:dc + 1], op0=ALU.mult, op1=ALU.add),
                   reads=[r_sq, W["r_gammac"], W["r_betac"]], writes=[r_sq])
        E.dma("sync", lambda s, j=j: s.dma_start(out=sb_slab(ybuf[:]), in_=ap_slab(xT_in, j)),
              reads=[r_xin], writes=[r_ybuf])
        for dc in range(NCH):
            sl = slice(dc * 512, (dc + 1) * 512)
            E.emit("dve", lambda v, sl=sl: v.tensor_tensor(out=ybuf[:, sl], in0=sq[:, sl],
                                                           in1=ybuf[:, sl], op=ALU.add),
                   reads=[r_sq, r_ybuf], writes=[r_ybuf])
        E.dma("pool", lambda g, j=j: g.dma_start(out=ap_slab(xT_out, j), in_=sb_slab(ybuf[:])),
              reads=[r_ybuf], writes=[r_xout], dest=r_xout)
        emit_ep_ln(C, ybuf, r_ybuf, j, tT_next, r_tTn)


def emit_moe(C, xT_in, r_xin, xT_out, r_xout, tT_hbm, r_tT, W, tag, tT_next=None, r_tTn=None, skip_ln=False):
    E, nc, CT, T, NT = C.E, C.nc, C.CT, C.T, C.NT
    M, NM = 1408, 11
    NE = 8
    PA, PB, PC, PL = C.PA, C.PB, C.PC, C.PL
    if not skip_ln:
        emit_ln_to_tT(C, xT_in, r_xin, tT_hbm, r_tT)

    wgate = C.sb(tag + "wgate", [128, NCH * 8], BF16); r_wgate = E.dma_res(tag + "wgate")
    load_weight_lhsT(C, W["w_gate"], W["r_w_gate"], wgate, r_wgate, slice(0, 8), "pool")
    NTT = T // 128
    gbuf = C.sb(tag + "gbuf", [128, NTT * 8], F32); r_gbuf = E.res(tag + "gbuf")
    gT = C.sb(tag + "gT", [8, T], BF16); r_gT = E.res(tag + "gT")
    selmat = C.sb(tag + "sel", [8, 8 * 128], BF16); r_sel = E.dma_res(tag + "sel")
    sel_in = C.nc.dram_tensor(tag + "sel_in", [8, 8 * 128], F32, kind="ExternalInput").ap()
    E.dma("pool", lambda g: g.dma_start(out=selmat[:], in_=sel_in), writes=[r_sel])
    selh = np.zeros((8, 8, 128), np.float32)
    for e in range(8):
        selh[e, e, :] = 1.0
    C.host_consts[tag + "sel_in"] = np.ascontiguousarray(selh.transpose(1, 0, 2).reshape(8, 8 * 128))
    grow = C.sb(tag + "grow", [128, 8 * 4], F32); r_grow = E.res(tag + "grow")
    g_bc = C.sb(tag + "g_bc", [128, 512], F32); r_gbc = E.res(tag + "g_bc")
    for j in range(NT):
        tb, r_tb = C.tslab[j % 2], C.r_tslab[j % 2]
        E.dma("sync", lambda s, j=j: s.dma_start(out=sb_slab(tb[:]), in_=ap_slab(tT_hbm, j)),
              reads=[r_tT], writes=[r_tb])
        for k in range(4):
            tt = j * 4 + k
            E.emit("pe", lambda t, tb=tb, k=k: [t.matmul(
                       PL[:, 0:8], tb[:, c * 512 + k * 128: c * 512 + (k + 1) * 128],
                       wgate[:, c * 8:(c + 1) * 8],
                       start=(c == 0), stop=(c == NCH - 1)) for c in range(NCH)],
                   reads=[r_tb, r_wgate], writes=[C.rPL])
            sc = gbuf[:, tt * 8:(tt + 1) * 8]
            E.emit("dve", lambda v, sc=sc: v.tensor_copy(out=sc, in_=PL[:, 0:8]),
                   reads=[C.rPL], writes=[r_gbuf])
            m1 = grow[:, 0:1]; m2 = grow[:, 1:2]; p1 = grow[:, 2:3]; p2 = grow[:, 3:4]
            mask1 = grow[:, 8:16]; sp = grow[:, 16:24]
            E.emit("dve", lambda v, sc=sc, m1=m1: v.tensor_reduce(out=m1, in_=sc, axis=AX.X, op=ALU.max),
                   reads=[r_gbuf], writes=[r_grow])
            E.emit("dve", lambda v, sc=sc: v.tensor_scalar(out=mask1, in0=sc, scalar1=m1,
                                                           scalar2=None, op0=ALU.is_equal),
                   reads=[r_gbuf, r_grow], writes=[r_grow])
            E.emit("dve", lambda v, sc=sc: v.scalar_tensor_tensor(out=sp, in0=mask1, scalar=-1e30,
                                                                  in1=sc, op0=ALU.mult, op1=ALU.add),
                   reads=[r_grow, r_gbuf], writes=[r_grow])
            E.emit("dve", lambda v: v.tensor_reduce(out=m2, in_=sp, axis=AX.X, op=ALU.max),
                   reads=[r_grow], writes=[r_grow])
            E.emit("dve", lambda v: v.tensor_tensor(out=grow[:, 4:5], in0=m1, in1=m2, op=ALU.subtract),
                   reads=[r_grow], writes=[r_grow])
            E.emit("act", lambda s: s.activation(out=p1, in_=grow[:, 4:5], func=AF.Sigmoid),
                   reads=[r_grow], writes=[r_grow])
            E.emit("act", lambda s: s.activation(out=p2, in_=grow[:, 4:5], func=AF.Sigmoid,
                                                 scale=CT.f32(-1.0)),
                   reads=[r_grow, C.r_const], writes=[r_grow])
            E.emit("dve", lambda v: v.tensor_scalar(out=sp, in0=sp, scalar1=m2, scalar2=None,
                                                    op0=ALU.is_equal),
                   reads=[r_grow], writes=[r_grow])
            E.emit("dve", lambda v: v.tensor_scalar(out=mask1, in0=mask1, scalar1=p1, scalar2=None,
                                                    op0=ALU.mult),
                   reads=[r_grow], writes=[r_grow])
            E.emit("dve", lambda v, sc=sc: v.scalar_tensor_tensor(out=sc, in0=sp, scalar=p2,
                                                                  in1=mask1, op0=ALU.mult, op1=ALU.add),
                   reads=[r_grow], writes=[r_gbuf])
            E.emit("pe", lambda t, sc=sc: t.transpose(PL[0:8, 0:128], sc, C.ident_f32[:]),
                   reads=[r_gbuf, C.r_ident], writes=[C.rPL])
            E.emit("dve", lambda v, tt=tt: v.tensor_copy(out=gT[:, tt * 128:(tt + 1) * 128],
                                                         in_=PL[0:8, 0:128]),
                   reads=[C.rPL], writes=[r_gT])

    use_bf = "w_g_bf" in W
    wg = C.sb(tag + "wg", [128, NCH * M], BF16); r_wg = E.dma_res(tag + "wg")
    wu = C.sb(tag + "wu", [128, NCH * M], BF16); r_wu = E.dma_res(tag + "wu")
    wd = C.sb(tag + "wd", [128, NM * D], BF16); r_wd = E.dma_res(tag + "wd")
    hbuf = C.sb(tag + "hbuf", [128, NM * 512], BF16); r_hbuf = E.res(tag + "hbuf")
    sg = C.sb(tag + "sg", [128, 512], BF16); r_sg = E.res(tag + "sg")
    tmp = C.sb(tag + "tmp", [128, 512], BF16); r_tmp = E.res(tag + "tmp")
    HTOK = max(512, T // 4)
    NHALF = T // HTOK
    NJ = HTOK // 512
    yacc = C.sb(tag + "yacc", [128, NCH * HTOK], F32); r_yacc = E.dma_res(tag + "yacc")

    for H in range(NHALF):
        src = xT_in[:, H * HTOK:(H + 1) * HTOK].rearrange("(c p) t -> p c t", p=128)
        E.dma("sync", lambda s: s.dma_start(out=yacc[:, 0:NCH * HTOK].rearrange("p (c t) -> p c t", t=HTOK),
                                            in_=src),
              reads=[r_xin], writes=[r_yacc])
        for e in range(NE):
            if use_bf:
                E.dma("act", lambda g, e=e: g.dma_start(
                          out=wg[:].rearrange("p (c m) -> p c m", m=M),
                          in_=W["w_g_bf"][e].rearrange("(c p) m -> p c m", p=128)),
                      reads=[W["r_w_g_bf"]], writes=[r_wg])
                E.dma("sync", lambda g, e=e: g.dma_start(
                          out=wu[:].rearrange("p (c m) -> p c m", m=M),
                          in_=W["w_u_bf"][e].rearrange("(c p) m -> p c m", p=128)),
                      reads=[W["r_w_u_bf"]], writes=[r_wu])
                E.dma("act", lambda g, e=e: g.dma_start(
                          out=wd[:].rearrange("p (c m) -> p c m", m=D),
                          in_=W["w_d_bf"][e].rearrange("(c p) m -> p c m", p=128)),
                      reads=[W["r_w_d_bf"]], writes=[r_wd])
            else:
                E.dma("pool", lambda g, e=e: g.dma_start(
                          out=wg[:].rearrange("p (c m) -> p c m", m=M),
                          in_=W["w_g"][e].rearrange("(c p) m -> p c m", p=128)),
                      reads=[W["r_w_g"]], writes=[r_wg])
                E.dma("pool", lambda g, e=e: g.dma_start(
                          out=wu[:].rearrange("p (c m) -> p c m", m=M),
                          in_=W["w_u"][e].rearrange("(c p) m -> p c m", p=128)),
                      reads=[W["r_w_u"]], writes=[r_wu])
                E.dma("pool", lambda g, e=e: g.dma_start(
                          out=wd[:].rearrange("p (c m) -> p c m", m=D),
                          in_=W["w_d"][e].rearrange("(c p) m -> p c m", p=128)),
                      reads=[W["r_w_d"]], writes=[r_wd])
            for j in range(NJ):
                jj = H * NJ + j
                tb, r_tb = C.tslab[jj % 2], C.r_tslab[jj % 2]
                E.dma("sync", lambda s, jj=jj: s.dma_start(out=sb_slab(tb[:]), in_=ap_slab(tT_hbm, jj)),
                      reads=[r_tT], writes=[r_tb])
                E.emit("pe", lambda t, e=e, jj=jj: t.matmul(
                           PL[:, 0:512], selmat[:, e * 128:(e + 1) * 128],
                           gT[0:8, jj * 512:(jj + 1) * 512],
                           start=True, stop=True),
                       reads=[r_gT, r_sel], writes=[C.rPL])
                E.emit("dve", lambda v: v.tensor_copy(out=g_bc[:], in_=PL[:, 0:512]),
                       reads=[C.rPL], writes=[r_gbc])
                for mi in range(NM):
                    Ps = PA if mi % 2 == 0 else PC
                    rPs = C.rPA if mi % 2 == 0 else C.rPC
                    E.emit("pe", lambda t, mi=mi, tb=tb, Ps=Ps: [t.matmul(
                               Ps[:, 0:512], wg[:, c * M + mi * 128: c * M + (mi + 1) * 128],
                               tb[:, c * 512:(c + 1) * 512],
                               start=(c == 0), stop=(c == NCH - 1)) for c in range(NCH)],
                           reads=[r_wg, r_tb], writes=[rPs[0]])
                    E.emit("pe", lambda t, mi=mi, tb=tb, Ps=Ps: [t.matmul(
                               Ps[:, 512:1024], wu[:, c * M + mi * 128: c * M + (mi + 1) * 128],
                               tb[:, c * 512:(c + 1) * 512],
                               start=(c == 0), stop=(c == NCH - 1)) for c in range(NCH)],
                           reads=[r_wu, r_tb], writes=[rPs[1]])
                    E.emit("act", lambda s, Ps=Ps: s.activation(out=sg[:], in_=Ps[:, 0:512], func=AF.Silu),
                           reads=[rPs[0]], writes=[r_sg])
                    E.emit("dve", lambda v, Ps=Ps: v.tensor_tensor(out=tmp[:], in0=Ps[:, 512:1024],
                                                            in1=g_bc[:], op=ALU.mult),
                           reads=[rPs[1], r_gbc], writes=[r_tmp])
                    E.emit("dve", lambda v, mi=mi: v.tensor_tensor(
                               out=hbuf[:, mi * 512:(mi + 1) * 512], in0=sg[:], in1=tmp[:], op=ALU.mult),
                           reads=[r_sg, r_tmp], writes=[r_hbuf])
                for dc in range(NCH):
                    h2 = dc % 2
                    E.emit("pe", lambda t, dc=dc, h2=h2: [t.matmul(
                               PB[:, h2 * 512:(h2 + 1) * 512],
                               wd[:, mi * D + dc * 128: mi * D + (dc + 1) * 128],
                               hbuf[:, mi * 512:(mi + 1) * 512],
                               start=(mi == 0), stop=(mi == NM - 1)) for mi in range(NM)],
                           reads=[r_wd, r_hbuf], writes=[C.rPB[h2]])
                    ysl = slice(dc * HTOK + j * 512, dc * HTOK + (j + 1) * 512)
                    E.emit("dve", lambda v, ysl=ysl, h2=h2: v.tensor_tensor(
                               out=yacc[:, ysl], in0=yacc[:, ysl], in1=PB[:, h2 * 512:(h2 + 1) * 512],
                               op=ALU.add),
                           reads=[C.rPB[h2]], writes=[r_yacc])
        dst = xT_out[:, H * HTOK:(H + 1) * HTOK].rearrange("(c p) t -> p c t", p=128)
        E.dma("pool", lambda g: g.dma_start(out=dst,
                                            in_=yacc[:, 0:NCH * HTOK].rearrange("p (c t) -> p c t", t=HTOK)),
              reads=[r_yacc], writes=[r_xout], dest=r_xout)
        for js in range(NJ):
            chunks = [yacc[:, c * HTOK + js * 512: c * HTOK + (js + 1) * 512] for c in range(NCH)]
            emit_ep_ln(C, yacc, r_yacc, H * NJ + js, tT_next, r_tTn, chunks=chunks)


def emit_attn_b0(C, xT_in, r_xin, xT_out, r_xout, tT_hbm, r_tT, W, tT_next=None, r_tTn=None, skip_ln=False):
    E, nc, CT, T, NT = C.E, C.nc, C.CT, C.T, C.NT
    NQ = T // 512
    NTK = T // 128
    SHIFT = 50.0
    EPS = 1.1920929e-07
    PA, PB, PC, PL, PT = C.PA, C.PB, C.PC, C.PL, C.PT
    C.ensure_masks()
    C.ensure_hsel()
    for nm in ("w_qT", "w_kT", "w_vT", "w_oT"):
        precast(C, W, nm)
    if not skip_ln:
        emit_ln_to_tT(C, xT_in, r_xin, tT_hbm, r_tT)

    QT2 = [C.sb(f"ab_QT{i}", [128, T], BF16) for i in range(2)]
    r_QT2 = [E.res(f"ab_QT{i}") for i in range(2)]
    KT2 = [C.sb(f"ab_KT{i}", [128, T], BF16) for i in range(2)]
    r_KT2 = [E.res(f"ab_KT{i}") for i in range(2)]
    Vp2 = [C.sb(f"ab_Vp{i}", [128, NTK * 130], BF16) for i in range(2)]
    r_Vp2 = [E.res(f"ab_Vp{i}") for i in range(2)]
    VTs = C.sb("ab_VTs", [128, 512], BF16); r_VTs = E.res("ab_VTs")
    qraw = C.sb("ab_qraw", [128, 512], BF16); r_qraw = E.res("ab_qraw")
    Pt = [C.sb(f"ab_P{i}", [128, 1024], BF16) for i in range(2)]
    r_Pt = [E.res(f"ab_P{i}") for i in range(2)]
    th = C.sb("ab_th", [128, 1024], F32); r_th = E.res("ab_th")
    th2 = C.sb("ab_th2", [128, 1024], F32); r_th2 = E.res("ab_th2")
    aO_hbm = nc.dram_tensor("ab_aO_hbm", [D, T], BF16, kind="Internal").ap()
    r_aOh = E.dma_res("ab_aOh")
    aOs = C.sb("ab_aOs", [128, 512], BF16); r_aOs = E.res("ab_aOs")
    recrow = C.sb("ab_recrow", [1, 1024], F32); r_rec = E.res("ab_recrow")
    rb = C.sb("ab_rb", [128, 1024], F32); r_rb = E.res("ab_rb")
    rmsq = C.sb("ab_rmsq", [128, 512], F32); r_rmsq = E.res("ab_rmsq")
    rmsr = C.sb("ab_rmsr", [2, 512], F32); r_rmsr = E.res("ab_rmsr")
    qksum2 = [C.sb(f"ab_qksum{i}", [66, T], F32) for i in range(2)]
    r_qksum2 = [E.res(f"ab_qksum{i}") for i in range(2)]
    for _p in range(2):
        E.emit("dve", lambda v, _p=_p: v.memset(qksum2[_p][:], 1.0), writes=[r_qksum2[_p]])
    rmsbc = C.sb("ab_rmsbc", [128, 512], F32); r_rmsbc = E.res("ab_rmsbc")
    wstage = C.sb("wstage", [128, NCH * D], BF16)
    r_ws = [E.dma_res("ab_ws0"), E.dma_res("ab_ws1")]

    for par in range(2):
        for i in range(NTK):
            E.emit("dve", lambda v, i=i, par=par: v.memset(Vp2[par][:, i * 130 + 64:i * 130 + 65], 1.0),
                   writes=[r_Vp2[par]])
            E.emit("dve", lambda v, i=i, par=par: v.memset(Vp2[par][:, i * 130 + 129:i * 130 + 130], 1.0),
                   writes=[r_Vp2[par]])

    def prep_gen(hp):
        par = hp % 2
        c0 = hp * 128
        ws = wstage[:, par * NCH * 512:(par + 1) * NCH * 512]
        r_w = r_ws[par]
        QT, KT, Vp = QT2[par], KT2[par], Vp2[par]
        r_QT, r_KT, r_Vp = r_QT2[par], r_KT2[par], r_Vp2[par]
        wq = ws[:, 0:NCH * 128]
        wk = ws[:, NCH * 128:2 * NCH * 128]
        wv = ws[:, 2 * NCH * 128:3 * NCH * 128]
        for nm, dsts in (("w_qT", wq), ("w_kT", wk), ("w_vT", wv)):
            bf = W[nm + "_bf"]
            sap = bf[:, c0:c0 + 128].rearrange("(c p) m -> p c m", p=128)
            E.dma("act", lambda g, dsts=dsts, sap=sap: g.dma_start(
                      out=dsts.rearrange("p (c m) -> p c m", m=128), in_=sap),
                  reads=[W["r_" + nm + "_bf"]], writes=[r_w], dest=r_w)
        for j in range(NT):
            tb, r_tb = C.tslab[j % 2], C.r_tslab[j % 2]
            E.dma("sync", lambda s, j=j: s.dma_start(out=sb_slab(tb[:]), in_=ap_slab(tT_hbm, j)),
                  reads=[r_tT], writes=[r_tb])
            for wi, (wsb, dst, r_dst, sro) in enumerate((
                    (wq, QT, r_QT, 0), (wk, KT, r_KT, 64))):
                smb, r_smb = qksum2[par], r_qksum2[par]
                E.emit("pe", lambda t, wsb=wsb, tb=tb: [t.matmul(
                           PL[:, 0:512], wsb[:, c * 128:(c + 1) * 128],
                           tb[:, c * 512:(c + 1) * 512],
                           start=(c == 0), stop=(c == NCH - 1)) for c in range(NCH)],
                       reads=[r_w, r_tb], writes=[C.rPL])
                E.emit("dve", lambda v, dst=dst, j=j: v.tensor_copy(
                           out=dst[:, j * 512:(j + 1) * 512], in_=PL[:, 0:512]),
                       reads=[C.rPL], writes=[r_dst])
                E.emit("act", lambda s, dst=dst, j=j: s.activation(
                           out=rmsq[:], in_=dst[:, j * 512:(j + 1) * 512], func=AF.Square),
                       reads=[r_dst], writes=[r_rmsq])
                E.emit("pe", lambda t: t.matmul(PL[0:2, 0:512], C.hsel[:, 0:2], rmsq[:],
                                                start=True, stop=True),
                       reads=[r_rmsq, C.r_hsel], writes=[C.rPL])
                E.emit("dve", lambda v, smb=smb, j=j, sro=sro: v.tensor_copy(
                           out=smb[sro:sro + 2, j * 512:(j + 1) * 512], in_=PL[0:2, 0:512]),
                       reads=[C.rPL], writes=[r_smb])
                yield
            E.emit("pe", lambda t, tb=tb: [t.matmul(
                       PL[:, 0:512], wv[:, c * 128:(c + 1) * 128], tb[:, c * 512:(c + 1) * 512],
                       start=(c == 0), stop=(c == NCH - 1)) for c in range(NCH)],
                   reads=[r_w, r_tb], writes=[C.rPL])
            E.emit("dve", lambda v: v.tensor_copy(out=VTs[:], in_=PL[:, 0:512]),
                   reads=[C.rPL], writes=[r_VTs])
            for k in range(4):
                i = j * 4 + k
                E.emit("pe", lambda t, k=k: t.transpose(PT[:], VTs[:, k * 128:(k + 1) * 128], C.ident_bf[:]),
                       reads=[r_VTs, C.r_ident_b], writes=[C.rPT])
                E.emit("dve", lambda v, i=i, Vp=Vp: v.tensor_copy(out=Vp[:, i * 130:i * 130 + 64],
                                                           in_=PT[:, 0:64]),
                       reads=[C.rPT], writes=[r_Vp])
                E.emit("dve", lambda v, i=i, Vp=Vp: v.tensor_copy(out=Vp[:, i * 130 + 65:i * 130 + 129],
                                                           in_=PT[:, 64:128]),
                       reads=[C.rPT], writes=[r_Vp])
            yield
        smb, r_smb = qksum2[par], r_qksum2[par]
        E.emit("act", lambda s, smb=smb: s.activation(out=smb[:], in_=smb[:], func=AF.Ln,
                                                      scale=CT.f32(1.0 / 64.0), bias=CT.f32(EPS)),
               reads=[r_smb], writes=[r_smb])
        E.emit("act", lambda s, smb=smb: s.activation(out=smb[:], in_=smb[:], func=AF.Exp,
                                                      scale=CT.f32(-0.5)),
               reads=[r_smb], writes=[r_smb])
        yield
        for j in range(NT):
            for (dst, r_dst, sro) in ((QT, r_QT, 0), (KT, r_KT, 64)):
                smb, r_smb = qksum2[par], r_qksum2[par]
                E.emit("pe", lambda t, smb=smb, j=j, sro=sro: t.matmul(
                           PL[:, 0:512], C.hsel2[0:2, :] if sro == 0 else C.hsel2b[64:66, :],
                           smb[sro:sro + 2, j * 512:(j + 1) * 512],
                           start=True, stop=True),
                       reads=[r_smb, C.r_hsel], writes=[C.rPL])
                E.emit("dve", lambda v: v.tensor_copy(out=rmsbc[:], in_=PL[:, 0:512]),
                       reads=[C.rPL], writes=[r_rmsbc])
                E.emit("dve", lambda v, dst=dst, j=j: v.tensor_tensor(
                           out=dst[:, j * 512:(j + 1) * 512], in0=dst[:, j * 512:(j + 1) * 512],
                           in1=rmsbc[:], op=ALU.mult),
                       reads=[r_rmsbc], writes=[r_dst])
            yield

    def attention(hp, feeder):
        par = hp % 2
        QT, KT, Vp = QT2[par], KT2[par], Vp2[par]
        r_QT, r_KT, r_Vp = r_QT2[par], r_KT2[par], r_Vp2[par]
        for qb in range(NQ):
            ntk = (qb + 1) * 4
            qc = slice(qb * 512, (qb + 1) * 512)
            def emit_S(i, qc=qc):
                ib = i % 2
                Ps = PB if ib == 0 else PC
                rS = C.rPB if ib == 0 else C.rPC
                E.emit("pe", lambda t, i=i, Ps=Ps, qc=qc: [t.matmul(
                           Ps[:, 0:512], KT[0:64, i * 128:(i + 1) * 128], QT[0:64, qc],
                           start=True, stop=True, tile_position=(0, 0)),
                       t.matmul(
                           Ps[:, 512:1024], KT[64:128, i * 128:(i + 1) * 128], QT[64:128, qc],
                           start=True, stop=True, tile_position=(64, 0))],
                       reads=[r_KT, r_QT], writes=[rS[0], rS[1]])
            def emit_elem(i, qb=qb):
                ib = i % 2
                Ps = PB if ib == 0 else PC
                rS = C.rPB if ib == 0 else C.rPC
                thb = th if ib == 0 else th2
                r_thb = r_th if ib == 0 else r_th2
                E.emit("act", lambda s, Ps=Ps, thb=thb: s.activation(
                           out=thb[:], in_=Ps[:, 0:1024], func=AF.Tanh, scale=CT.f32(1.0 / 400.0)),
                       reads=[rS[0], rS[1], C.r_const], writes=[r_thb])
                E.emit("act", lambda s, ib=ib, thb=thb: s.activation(
                           out=Pt[ib][:], in_=thb[:], func=AF.Exp,
                           scale=CT.f32(50.0), bias=CT.f32(-SHIFT)),
                       reads=[r_thb, C.r_const], writes=[r_Pt[ib]])
                coff = i * 128 - qb * 512
                if coff >= 0:
                    ci = coff // 128
                    for half in range(2):
                        E.emit("dve", lambda v, ib=ib, ci=ci, half=half: v.tensor_tensor(
                                   out=Pt[ib][:, half * 512:(half + 1) * 512],
                                   in0=Pt[ib][:, half * 512:(half + 1) * 512],
                                   in1=C.masks[:, ci * 512:(ci + 1) * 512], op=ALU.mult),
                               reads=[C.r_masks], writes=[r_Pt[ib]])
            def emit_PV(i, ntk=ntk):
                ib = i % 2
                E.emit("pe", lambda t, i=i, ib=ib, ntk=ntk: t.matmul(
                           PA[0:65, 0:512], Vp[:, i * 130:i * 130 + 65], Pt[ib][:, 0:512],
                           start=(i == 0), stop=(i == ntk - 1)),
                       reads=[r_Vp, r_Pt[ib]], writes=[C.rPA[0]])
                E.emit("pe", lambda t, i=i, ib=ib, ntk=ntk: t.matmul(
                           PA[0:65, 512:1024], Vp[:, i * 130 + 65:i * 130 + 130], Pt[ib][:, 512:1024],
                           start=(i == 0), stop=(i == ntk - 1)),
                       reads=[r_Vp, r_Pt[ib]], writes=[C.rPA[1]])
            for i in range(ntk):
                emit_S(i)
                if i > 0:
                    emit_elem(i - 1)
                    emit_PV(i - 1)
                if i % 2 == 0:
                    feeder()
            emit_elem(ntk - 1)
            emit_PV(ntk - 1)
            E.emit("dve", lambda v: v.reciprocal(out=recrow[:], in_=PA[64:65, 0:1024]),
                   reads=[C.rPA[0], C.rPA[1]], writes=[r_rec])
            E.emit("pe", lambda t: t.matmul(PB[:, 0:512], C.ones_row_f32[0:1, :], recrow[0:1, 0:512],
                                            start=True, stop=True),
                   reads=[r_rec, C.r_ones_row], writes=[C.rPB[0]])
            E.emit("pe", lambda t: t.matmul(PB[:, 512:1024], C.ones_row_f32[0:1, :], recrow[0:1, 512:1024],
                                            start=True, stop=True),
                   reads=[r_rec, C.r_ones_row], writes=[C.rPB[1]])
            E.emit("dve", lambda v: v.tensor_copy(out=rb[:], in_=PB[:, 0:1024]),
                   reads=[C.rPB[0], C.rPB[1]], writes=[r_rb])
            E.emit("dve", lambda v: v.tensor_tensor(
                       out=aOs[0:64, :], in0=PA[0:64, 0:512], in1=rb[0:64, 0:512], op=ALU.mult),
                   reads=[C.rPA[0], r_rb], writes=[r_aOs])
            E.emit("dve", lambda v: v.tensor_tensor(
                       out=aOs[64:128, :], in0=PA[0:64, 512:1024], in1=rb[0:64, 512:1024], op=ALU.mult),
                   reads=[C.rPA[1], r_rb], writes=[r_aOs])
            E.dma("pool", lambda g, hp=hp, qb=qb: g.dma_start(
                      out=aO_hbm[hp * 128:(hp + 1) * 128, qb * 512:(qb + 1) * 512], in_=aOs[:]),
                  reads=[r_aOs], writes=[r_aOh], dest=r_aOh)

    g = prep_gen(0)
    for _ in g:
        pass
    for hp in range(NH // 2):
        gnext = prep_gen(hp + 1) if hp + 1 < NH // 2 else None
        state = {"cnt": 0}
        def feeder():
            state["cnt"] += 1
            if gnext is not None and state["cnt"] % 3 == 0:
                next(gnext, None)
        attention(hp, feeder)
        if gnext is not None:
            for _ in gnext:
                pass

    load_w(C, W, "w_oT", wstage, r_ws[0], slice(0, D), "act")
    r_ws[0].readers.extend(r_ws[1].readers)
    ybuf, r_ybuf = C.slabA, C.r_slabA
    for j in range(NT):
        E.dma("sync", lambda s, j=j: s.dma_start(out=sb_slab(ybuf[:]), in_=ap_slab(xT_in, j)),
              reads=[r_xin], writes=[r_ybuf])
        aos2, r_aos2 = C.tslab[j % 2], C.r_tslab[j % 2]
        E.dma("sync", lambda s, j=j, aos2=aos2: s.dma_start(out=sb_slab(aos2[:]), in_=ap_slab(aO_hbm, j)),
              reads=[r_aOh], writes=[r_aos2])
        for dc in range(NCH):
            h = dc % 2
            E.emit("pe", lambda t, dc=dc, h=h, aos2=aos2: [t.matmul(
                       PC[:, h * 512:(h + 1) * 512],
                       wstage[:, c * D + dc * 128: c * D + (dc + 1) * 128],
                       aos2[:, c * 512:(c + 1) * 512],
                       start=(c == 0), stop=(c == NCH - 1)) for c in range(NCH)],
                   reads=[r_ws[0], r_aos2], writes=[C.rPC[h]])
            E.emit("dve", lambda v, dc=dc, h=h: v.tensor_tensor(
                       out=ybuf[:, dc * 512:(dc + 1) * 512], in0=ybuf[:, dc * 512:(dc + 1) * 512],
                       in1=PC[:, h * 512:(h + 1) * 512], op=ALU.add),
                   reads=[C.rPC[h]], writes=[r_ybuf])
        E.dma("pool", lambda g, j=j: g.dma_start(out=ap_slab(xT_out, j), in_=sb_slab(ybuf[:])),
              reads=[r_ybuf], writes=[r_xout], dest=r_xout)
        emit_ep_ln(C, ybuf, r_ybuf, j, tT_next, r_tTn)


def emit_attn_a1(C, xT_in, r_xin, xT_out, r_xout, tT_hbm, r_tT, W, tT_next=None, r_tTn=None, skip_ln=False):
    E, nc, CT, T, NT = C.E, C.nc, C.CT, C.T, C.NT
    NQ = T // 512
    NTK = T // 128
    EPS = 1e-4
    PA, PB, PC, PL, PT = C.PA, C.PB, C.PC, C.PL, C.PT
    C.ensure_hsel()
    precast(C, W, "W_in")
    precast(C, W, "W_out")
    if not skip_ln:
        emit_ln_to_tT(C, xT_in, r_xin, tT_hbm, r_tT)

    KTp2 = [C.sb(f"a1_KTp{i}", [128, T], BF16) for i in range(2)]
    r_KTp2 = [E.res(f"a1_KTp{i}") for i in range(2)]
    QTp2 = [C.sb(f"a1_QTp{i}", [128, T], BF16) for i in range(2)]
    r_QTp2 = [E.res(f"a1_QTp{i}") for i in range(2)]
    G_hbm = nc.dram_tensor("a1_G_hbm", [D, T], BF16, kind="Internal").ap()
    r_Gh = E.dma_res("a1_Gh")
    gsl2 = [C.sb(f"a1_gsl{i}", [128, 512], BF16) for i in range(2)]
    r_gsl2 = [E.dma_res(f"a1_gsl{i}") for i in range(2)]
    Vp2 = [C.sb(f"a1_Vp{i}", [128, NTK * 130], BF16) for i in range(2)]
    r_Vp2 = [E.res(f"a1_Vp{i}") for i in range(2)]
    q2A2 = [C.sb(f"a1_q2A{i}", [1, T], BF16) for i in range(2)]
    r_q2A2 = [E.res(f"a1_q2A{i}") for i in range(2)]
    q2B2 = [C.sb(f"a1_q2B{i}", [1, T], BF16) for i in range(2)]
    r_q2B2 = [E.res(f"a1_q2B{i}") for i in range(2)]
    k2m2 = [C.sb(f"a1_k2m{i}", [128, 2 * (T // 128)], F32) for i in range(2)]
    r_k2m2 = [E.res(f"a1_k2m{i}") for i in range(2)]
    q2bc = C.sb("a1_q2bc", [128, 1024], BF16); r_q2bc = E.res("a1_q2bc")
    ktok = C.sb("a1_ktok", [128, 128], BF16); r_ktok = E.res("a1_ktok")
    k2sq = C.sb("a1_k2sq", [128, 128], BF16); r_k2sq = E.res("a1_k2sq")
    qkslab = C.sb("a1_qks", [128, 512], BF16); r_qks = E.res("a1_qks")
    sqs = C.sb("a1_sqs", [128, 512], F32); r_sqs = E.res("a1_sqs")
    HO_hbm = nc.dram_tensor("a1_HO_hbm", [D, T], BF16, kind="Internal").ap()
    r_HOh = E.dma_res("a1_HOh")
    HO = C.sb("a1_HO", [128, 512], BF16); r_HO = E.res("a1_HO")

    Pt = [C.sb(f"a1_P{i}", [128, 2048], BF16) for i in range(2)]
    r_Pt = [E.res(f"a1_P{i}") for i in range(2)]
    d2c = [C.sb(f"a1_d2c{i}", [128, 2048], BF16) for i in range(2)]
    r_d2c = [E.res(f"a1_d2c{i}") for i in range(2)]
    Lb = C.sb("a1_Lb", [128, 2048], F32); r_Lb = E.res("a1_Lb")
    Db = C.sb("a1_Db", [128, 512], BF16); r_Db = E.res("a1_Db")
    recrow = C.sb("a1_recrow", [1, 512], F32); r_rec = E.res("a1_recrow")
    rb = C.sb("a1_rb", [128, 1024], F32); r_rb = E.res("a1_rb")
    wstage = C.sb("wstage", [128, NCH * D], BF16)
    r_ws = [E.dma_res("a1_ws0"), E.dma_res("a1_ws1")]
    binc = C.sb("a1_binc", [128, 32], F32); r_binc = E.dma_res("a1_binc")
    boutc = C.sb("a1_boutc", [128, NCH], F32); r_boutc = E.dma_res("a1_boutc")
    binc_in = nc.dram_tensor("a1_binc_in", [128, 32], F32, kind="ExternalInput").ap()
    boutc_in = nc.dram_tensor("a1_boutc_in", [128, NCH], F32, kind="ExternalInput").ap()
    E.dma("sync", lambda s: s.dma_start(out=binc[:], in_=binc_in), writes=[r_binc])
    E.dma("sync", lambda s: s.dma_start(out=boutc[:], in_=boutc_in), writes=[r_boutc])

    for par in range(2):
        for i in range(NTK):
            E.emit("dve", lambda v, i=i, par=par: v.memset(Vp2[par][:, i * 130 + 64:i * 130 + 65], 1.0),
                   writes=[r_Vp2[par]])
            E.emit("dve", lambda v, i=i, par=par: v.memset(Vp2[par][:, i * 130 + 129:i * 130 + 130], 1.0),
                   writes=[r_Vp2[par]])

    W_in, r_Win = W["W_in"], W["r_W_in"]

    def prep_gen(hp):
        par = hp % 2
        ws = wstage[:, par * NCH * 512:(par + 1) * NCH * 512]
        r_w = r_ws[par]
        KTp, QTp, Vp = KTp2[par], QTp2[par], Vp2[par]
        r_KTp, r_QTp, r_Vp = r_KTp2[par], r_QTp2[par], r_Vp2[par]
        q2A, q2B, k2m = q2A2[par], q2B2[par], k2m2[par]
        r_q2A, r_q2B, r_k2m = r_q2A2[par], r_q2B2[par], r_k2m2[par]
        src_ap = W["W_in_bf"][:, 512 * hp:512 * hp + 512].rearrange("(c p) m -> p c m", p=128)
        E.dma("act", lambda g: g.dma_start(out=ws.rearrange("p (c m) -> p c m", m=512), in_=src_ap),
              reads=[W["r_W_in_bf"]], writes=[r_w])
        for j in range(NT):
            tb, r_tb = C.tslab[j % 2], C.r_tslab[j % 2]
            E.dma("sync", lambda s, j=j: s.dma_start(out=sb_slab(tb[:]), in_=ap_slab(tT_hbm, j)),
                  reads=[r_tT], writes=[r_tb])
            for sub in range(4):
                cg = 4 * hp + sub
                hb = 0 if sub < 2 else 64
                E.emit("pe", lambda t, tb=tb, sub=sub, ws=ws: [t.matmul(
                           PL[:, 0:512], ws[:, c * 512 + sub * 128: c * 512 + (sub + 1) * 128],
                           tb[:, c * 512:(c + 1) * 512],
                           start=(c == 0), stop=(c == NCH - 1)) for c in range(NCH)],
                       reads=[r_w, r_tb], writes=[C.rPL])
                E.emit("dve", lambda v, cg=cg: v.tensor_scalar(
                           out=qkslab[:], in0=PL[:, 0:512], scalar1=binc[:, cg:cg + 1],
                           scalar2=None, op0=ALU.add),
                       reads=[C.rPL, r_binc], writes=[r_qks])
                if sub % 2 == 0:
                    E.emit("dve", lambda v, hb=hb, j=j, QTp=QTp: v.tensor_copy(
                               out=QTp[hb:hb + 64, j * 512:(j + 1) * 512], in_=qkslab[0:64, :]),
                           reads=[r_qks], writes=[r_QTp])
                    E.emit("dve", lambda v, hb=hb, j=j, KTp=KTp: v.tensor_copy(
                               out=KTp[hb:hb + 64, j * 512:(j + 1) * 512], in_=qkslab[64:128, :]),
                           reads=[r_qks], writes=[r_KTp])
                    E.emit("act", lambda s: s.activation(out=sqs[0:64, :], in_=qkslab[0:64, :], func=AF.Square),
                           reads=[r_qks], writes=[r_sqs])
                    E.emit("pe", lambda t: t.matmul(PL[0:1, 0:512], C.hsel[0:64, 0:1], sqs[0:64, :],
                                                    start=True, stop=True),
                           reads=[r_sqs, C.r_hsel], writes=[C.rPL])
                    q2buf, r_q2x = (q2A, r_q2A) if hb == 0 else (q2B, r_q2B)
                    E.emit("dve", lambda v, j=j, q2buf=q2buf: v.tensor_scalar(
                               out=q2buf[0:1, j * 512:(j + 1) * 512], in0=PL[0:1, 0:512],
                               scalar1=-0.5, scalar2=None, op0=ALU.mult),
                           reads=[C.rPL], writes=[r_q2x])
                    for k in range(4):
                        i = j * 4 + k
                        kcol = (0 if hb == 0 else (T // 128)) + i
                        E.emit("pe", lambda t, k=k: t.transpose(PT[:, 0:64],
                                                                qkslab[64:128, k * 128:(k + 1) * 128],
                                                                C.ident_bf[64:128, 64:128]),
                               reads=[r_qks, C.r_ident_b], writes=[C.rPT])
                        E.emit("dve", lambda v: v.tensor_copy(out=ktok[:, 0:64], in_=PT[:, 0:64]),
                               reads=[C.rPT], writes=[r_ktok])
                        E.emit("dve", lambda v: v.scalar_tensor_tensor(out=k2sq[:, 0:64], in0=ktok[:, 0:64],
                                                                scalar=0.5, in1=ktok[:, 0:64],
                                                                op0=ALU.mult, op1=ALU.mult),
                               reads=[r_ktok], writes=[r_k2sq])
                        E.emit("dve", lambda v, kcol=kcol, k2m=k2m: v.tensor_reduce(
                                   out=k2m[:, kcol:kcol + 1],
                                   in_=k2sq[:, 0:64], axis=AX.X, op=ALU.add, negate=True),
                               reads=[r_k2sq], writes=[r_k2m])
                else:
                    E.dma("pool", lambda g, hb=hb, j=j, hp=hp: g.dma_start(
                              out=G_hbm[hp * 128 + hb:hp * 128 + hb + 64, j * 512:(j + 1) * 512],
                              in_=qkslab[64:128, :]),
                          reads=[r_qks], writes=[r_Gh], dest=r_Gh)
                    for k in range(4):
                        i = j * 4 + k
                        off = 0 if hb == 0 else 65
                        E.emit("pe", lambda t, k=k: t.transpose(PT[:, 0:64],
                                                                qkslab[0:64, k * 128:(k + 1) * 128],
                                                                C.ident_bf[0:64, 0:64]),
                               reads=[r_qks, C.r_ident_b], writes=[C.rPT])
                        E.emit("dve", lambda v, i=i, off=off, Vp=Vp: v.tensor_copy(
                                   out=Vp[:, i * 130 + off:i * 130 + off + 64], in_=PT[:, 0:64]),
                               reads=[C.rPT], writes=[r_Vp])
                yield

    def attention(hp, feeder):
        par = hp % 2
        KTp, QTp, Vp = KTp2[par], QTp2[par], Vp2[par]
        r_KTp, r_QTp, r_Vp = r_KTp2[par], r_QTp2[par], r_Vp2[par]
        q2A, q2B, k2m = q2A2[par], q2B2[par], k2m2[par]
        r_q2A, r_q2B, r_k2m = r_q2A2[par], r_q2B2[par], r_k2m2[par]
        NP = NTK // 2

        def q2head(qb):
            qc = slice(qb * 512, (qb + 1) * 512)
            gb, r_gb = gsl2[qb % 2], r_gsl2[qb % 2]
            E.dma("sync", lambda s, hp=hp, qb=qb, gb=gb: s.dma_start(
                      out=gb[:], in_=G_hbm[hp * 128:(hp + 1) * 128, qb * 512:(qb + 1) * 512]),
                  reads=[r_Gh], writes=[r_gb])
            E.emit("pe", lambda t, qc=qc: t.matmul(PL[:, 0:512], C.ones_row_bf[0:1, :], q2A[0:1, qc],
                                                   start=True, stop=True),
                   reads=[r_q2A, C.r_ones_row_b], writes=[C.rPL])
            E.emit("dve", lambda v: v.tensor_copy(out=q2bc[:, 0:512], in_=PL[:, 0:512]),
                   reads=[C.rPL], writes=[r_q2bc])
            E.emit("pe", lambda t, qc=qc: t.matmul(PL[:, 0:512], C.ones_row_bf[0:1, :], q2B[0:1, qc],
                                                   start=True, stop=True),
                   reads=[r_q2B, C.r_ones_row_b], writes=[C.rPL])
            E.emit("dve", lambda v: v.tensor_copy(out=q2bc[:, 512:1024], in_=PL[:, 0:512]),
                   reads=[C.rPL], writes=[r_q2bc])

        def emit_S(qb, i):
            qc = slice(qb * 512, (qb + 1) * 512)
            ib = i % 2
            Ps = PB if ib == 0 else PC
            rS = C.rPB if ib == 0 else C.rPC
            E.emit("pe", lambda t, i=i, Ps=Ps, qc=qc: [
                       t.matmul(Ps[:, 0:512], KTp[0:64, i * 128:(i + 1) * 128], QTp[0:64, qc],
                                start=True, stop=True, tile_position=(0, 0)),
                       t.matmul(Ps[:, 512:1024], KTp[64:128, i * 128:(i + 1) * 128], QTp[64:128, qc],
                                start=True, stop=True, tile_position=(64, 0))],
                   reads=[r_KTp, r_QTp], writes=[rS[0], rS[1]])

        def emit_clamp(n, i):
            ib = i % 2
            ip = n % 2
            quarter = (i % 2) * 1024
            Ps = PB if ib == 0 else PC
            rS = C.rPB if ib == 0 else C.rPC
            for half in range(2):
                kcol = (0 if half == 0 else (T // 128)) + i
                E.emit("dve", lambda v, Ps=Ps, half=half, kcol=kcol:
                           v.scalar_tensor_tensor(
                           out=Db[:, 0:512],
                           in0=Ps[:, half * 512:(half + 1) * 512], scalar=k2m[:, kcol:kcol + 1],
                           in1=q2bc[:, half * 512:(half + 1) * 512], op0=ALU.add, op1=ALU.add),
                       reads=[rS[half], r_k2m, r_q2bc], writes=[r_Db])
                E.emit("dve", lambda v, half=half, ip=ip, quarter=quarter: v.tensor_scalar(
                           out=d2c[ip][:, quarter + half * 512: quarter + (half + 1) * 512],
                           in0=Db[:, 0:512],
                           scalar1=-2.0, scalar2=0.0, op0=ALU.mult, op1=ALU.max),
                       reads=[r_Db], writes=[r_d2c[ip]])

        def emit_chain(n):
            ip = n % 2
            E.emit("act", lambda s, ip=ip: s.activation(out=Lb[:], in_=d2c[ip][:], func=AF.Ln),
                   reads=[r_d2c[ip]], writes=[r_Lb])
            E.emit("act", lambda s, ip=ip: s.activation(out=d2c[ip][:], in_=Lb[:], func=AF.Exp,
                                                        scale=CT.f32(0.5)),
                   reads=[r_Lb, C.r_const], writes=[r_d2c[ip]])
            E.emit("act", lambda s, ip=ip: s.activation(out=Lb[:], in_=d2c[ip][:], func=AF.Ln,
                                                        bias=CT.f32(EPS)),
                   reads=[r_d2c[ip], C.r_const], writes=[r_Lb])
            E.emit("act", lambda s, ip=ip: s.activation(out=Pt[ip][:], in_=Lb[:], func=AF.Exp,
                                                        scale=CT.f32(-2.0)),
                   reads=[r_Lb, C.r_const], writes=[r_Pt[ip]])

        def emit_PVpair(n):
            ip = n % 2
            for k in range(2):
                i = (n % NP) * 2 + k
                quarter = k * 1024
                E.emit("pe", lambda t, i=i, ip=ip, quarter=quarter: t.matmul(
                           PA[0:65, 0:512], Vp[:, i * 130:i * 130 + 65],
                           Pt[ip][:, quarter:quarter + 512],
                           start=(i == 0), stop=(i == NTK - 1)),
                       reads=[r_Vp, r_Pt[ip]], writes=[C.rPA[0]])
                E.emit("pe", lambda t, i=i, ip=ip, quarter=quarter: t.matmul(
                           PA[0:65, 512:1024], Vp[:, i * 130 + 65:i * 130 + 130],
                           Pt[ip][:, quarter + 512:quarter + 1024],
                           start=(i == 0), stop=(i == NTK - 1)),
                       reads=[r_Vp, r_Pt[ip]], writes=[C.rPA[1]])

        def norm_tail(qb):
            for hh in range(2):
                E.emit("act", lambda s, hh=hh: s.activation(out=recrow[:], in_=PA[64:65, hh * 512:(hh + 1) * 512],
                                                            func=AF.Ln, bias=CT.f32(EPS)),
                       reads=[C.rPA[hh], C.r_const], writes=[r_rec])
                E.emit("act", lambda s: s.activation(out=recrow[:], in_=recrow[:], func=AF.Exp,
                                                     scale=CT.f32(-1.0)),
                       reads=[r_rec, C.r_const], writes=[r_rec])
                E.emit("pe", lambda t, hh=hh: t.matmul(PB[:, hh * 512:(hh + 1) * 512], C.ones_row_f32[0:1, :],
                                                       recrow[0:1, 0:512], start=True, stop=True),
                       reads=[r_rec, C.r_ones_row], writes=[C.rPB[hh]])
            E.emit("dve", lambda v: v.tensor_copy(out=rb[:], in_=PB[:, 0:1024]),
                   reads=[C.rPB[0], C.rPB[1]], writes=[r_rb])
            E.emit("dve", lambda v: v.tensor_tensor(
                       out=Db[0:64, 0:512],
                       in0=PA[0:64, 0:512], in1=rb[0:64, 0:512], op=ALU.mult),
                   reads=[C.rPA[0], r_rb], writes=[r_Db])
            E.emit("dve", lambda v: v.tensor_tensor(
                       out=Db[64:128, 0:512],
                       in0=PA[0:64, 512:1024], in1=rb[0:64, 512:1024], op=ALU.mult),
                   reads=[C.rPA[1], r_rb], writes=[r_Db])
            gb, r_gb = gsl2[qb % 2], r_gsl2[qb % 2]
            E.emit("dve", lambda v, gb=gb: v.tensor_tensor(
                       out=HO[0:64, :],
                       in0=Db[0:64, 0:512], in1=gb[0:64, :], op=ALU.mult),
                   reads=[r_Db, r_gb], writes=[r_HO])
            E.emit("dve", lambda v, gb=gb: v.tensor_tensor(
                       out=HO[64:128, :],
                       in0=Db[64:128, 0:512], in1=gb[64:128, :], op=ALU.mult),
                   reads=[r_Db, r_gb], writes=[r_HO])
            E.dma("pool", lambda g, hp=hp, qb=qb: g.dma_start(
                      out=HO_hbm[hp * 128:(hp + 1) * 128, qb * 512:(qb + 1) * 512], in_=HO[:]),
                  reads=[r_HO], writes=[r_HOh], dest=r_HOh)

        NTOT = NQ * NP
        for n in range(NTOT + 2):
            if n < NTOT:
                qb = n // NP
                if n % NP == 0:
                    q2head(qb)
                emit_S(qb, (n % NP) * 2)
                emit_clamp(n, (n % NP) * 2)
                emit_S(qb, (n % NP) * 2 + 1)
                emit_clamp(n, (n % NP) * 2 + 1)
                feeder()
            if n >= 1 and n - 1 < NTOT:
                emit_chain(n - 1)
            if n >= 2 and n - 2 < NTOT:
                emit_PVpair(n - 2)
                if (n - 2) % NP == NP - 1:
                    norm_tail((n - 2) // NP)

    g = prep_gen(0)
    for _ in g:
        pass
    for hp in range(NH // 2):
        gnext = prep_gen(hp + 1) if hp + 1 < NH // 2 else None
        state = {"cnt": 0}
        def feeder():
            state["cnt"] += 1
            if gnext is not None and state["cnt"] % 4 == 0:
                next(gnext, None)
        attention(hp, feeder)
        if gnext is not None:
            for _ in gnext:
                pass

    load_w(C, W, "W_out", wstage, r_ws[0], slice(0, D), "act")
    r_ws[0].readers.extend(r_ws[1].readers)
    ybuf, r_ybuf = C.slabA, C.r_slabA
    for j in range(NT):
        E.dma("sync", lambda s, j=j: s.dma_start(out=sb_slab(ybuf[:]), in_=ap_slab(xT_in, j)),
              reads=[r_xin], writes=[r_ybuf])
        hos, r_hos = C.tslab[j % 2], C.r_tslab[j % 2]
        E.dma("sync", lambda s, j=j, hos=hos: s.dma_start(out=sb_slab(hos[:]), in_=ap_slab(HO_hbm, j)),
              reads=[r_HOh], writes=[r_hos])
        for dc in range(NCH):
            h = dc % 2
            E.emit("pe", lambda t, dc=dc, j=j, h=h: [t.matmul(
                       PC[:, h * 512:(h + 1) * 512],
                       wstage[:, c * D + dc * 128: c * D + (dc + 1) * 128],
                       hos[:, c * 512:(c + 1) * 512],
                       start=(c == 0), stop=(c == NCH - 1)) for c in range(NCH)],
                   reads=[r_ws[0], r_hos], writes=[C.rPC[h]])
            E.emit("dve", lambda v, dc=dc, h=h: v.scalar_tensor_tensor(
                       out=ybuf[:, dc * 512:(dc + 1) * 512], in0=PC[:, h * 512:(h + 1) * 512],
                       scalar=boutc[:, dc:dc + 1], in1=ybuf[:, dc * 512:(dc + 1) * 512],
                       op0=ALU.add, op1=ALU.add),
                   reads=[C.rPC[h], r_boutc], writes=[r_ybuf])
        E.dma("pool", lambda g, j=j: g.dma_start(out=ap_slab(xT_out, j), in_=sb_slab(ybuf[:])),
              reads=[r_ybuf], writes=[r_xout], dest=r_xout)
        emit_ep_ln(C, ybuf, r_ybuf, j, tT_next, r_tTn)


def emit_l1ffn(C, xT_in, r_xin, xT_out, r_xout, tT_hbm, r_tT, W, tag, tT_next=None, r_tTn=None, skip_ln=False):
    E, nc, CT, T, NT = C.E, C.nc, C.CT, C.T, C.NT
    PA, PB, PC, PL = C.PA, C.PB, C.PC, C.PL
    if not skip_ln:
        emit_ln_to_tT(C, xT_in, r_xin, tT_hbm, r_tT)

    out = C.sb(tag + "out", [128, NCH * T], BF16); r_out = E.dma_res(tag + "out")
    gcb = C.sb(tag + "gcb", [128, 16], BF16); r_outb = E.res(tag + "gcb")
    outb = gcb
    h1b = C.sb(tag + "h1", [128, NCH * 512], BF16); r_h1 = E.res(tag + "h1")
    h2b = C.sb(tag + "h2", [128, 4 * 512], BF16); r_h2 = E.res(tag + "h2")
    h3b = C.sb(tag + "h3", [128, 2 * 512], BF16); r_h3 = E.res(tag + "h3")
    tfrow = C.sb(tag + "tf", [1, T], F32); r_tf = E.res(tag + "tf")
    crow = C.sb(tag + "crow", [1, 512], F32); r_crow = E.res(tag + "crow")
    gcacc = C.sb(tag + "gcacc", [128, NCH + 2], F32); r_gcacc = E.res(tag + "gcacc")
    gfbuf = C.sb(tag + "gf", [128, 8], F32); r_gf = E.res(tag + "gf")
    wstage = C.sb("wstage", [128, NCH * D], BF16); r_wstage = E.dma_res("wstage")
    mtmp = C.sb(tag + "mtmp", [128, 512], F32); r_mtmp = E.res(tag + "mtmp")
    ttmp = C.sb(tag + "ttmp", [128, 512], F32); r_ttmp = E.res(tag + "ttmp")
    bvec = C.sb(tag + "bvec", [128, 24], F32); r_bvec = E.dma_res(tag + "bvec")
    bvec_in = nc.dram_tensor(tag + "bvec_in", [128, 24], F32, kind="ExternalInput").ap()
    E.dma("sync", lambda s: s.dma_start(out=bvec[:], in_=bvec_in), writes=[r_bvec])
    B1, B2, B3, G1, G2, SC = 0, 8, 12, 14, 18, 20

    E.emit("dve", lambda v: v.memset(gcacc[:, 0:NCH], 0.0), writes=[r_gcacc])
    for j in range(NT):
        tb, r_tb = C.tslab[j % 2], C.r_tslab[j % 2]
        E.dma("sync", lambda s, j=j: s.dma_start(out=sb_slab(tb[:]), in_=ap_slab(tT_hbm, j)),
              reads=[r_tT], writes=[r_tb])
        for c in range(NCH):
            E.emit("dve", lambda v, c=c, j=j, tb=tb: v.tensor_copy(
                       out=out[:, c * T + j * 512: c * T + (j + 1) * 512],
                       in_=tb[:, c * 512:(c + 1) * 512]),
                   reads=[r_tb], writes=[r_out])
            E.emit("dve", lambda v, c=c, tb=tb: v.tensor_reduce(
                       out=gcacc[:, NCH:NCH + 1], in_=tb[:, c * 512:(c + 1) * 512],
                       axis=AX.X, op=ALU.add),
                   reads=[r_tb], writes=[r_gcacc])
            E.emit("dve", lambda v, c=c: v.tensor_tensor(
                       out=gcacc[:, c:c + 1], in0=gcacc[:, c:c + 1], in1=gcacc[:, NCH:NCH + 1],
                       op=ALU.add) if True else None,
                   reads=[r_gcacc], writes=[r_gcacc])

    E.emit("dve", lambda v: v.tensor_scalar(out=gcacc[:, 0:NCH], in0=gcacc[:, 0:NCH],
                                            scalar1=1.0 / T, scalar2=None, op0=ALU.mult),
           reads=[r_gcacc], writes=[r_gcacc])
    E.emit("dve", lambda v: v.tensor_copy(out=gcb[:, 0:NCH], in_=gcacc[:, 0:NCH]),
           reads=[r_gcacc], writes=[r_outb])
    load_weight_lhsT(C, W["gc_w1"], W["r_gc_w1"], wstage[:, 0:NCH * 512], r_wstage, slice(0, 512), "pool")
    for dc in range(4):
        E.emit("pe", lambda t, dc=dc: [t.matmul(
                   PL[:, dc:dc + 1], wstage[:, c * 512 + dc * 128: c * 512 + (dc + 1) * 128],
                   gcb[:, c:c + 1], start=(c == 0), stop=(c == NCH - 1)) for c in range(NCH)],
               reads=[r_wstage, r_outb], writes=[C.rPL])
    for dc in range(4):
        E.emit("act", lambda s, dc=dc: s.activation(out=gfbuf[:, dc:dc + 1], in_=PL[:, dc:dc + 1],
                                                    func=AF.Gelu, bias=bvec[:, G1 + dc:G1 + dc + 1]),
               reads=[C.rPL, r_bvec], writes=[r_gf])
    load_weight_lhsT(C, W["gc_w2"], W["r_gc_w2"], wstage[:, 0:4 * 256], r_wstage, slice(0, 256), "pool")
    E.emit("dve", lambda v: v.tensor_copy(out=gcb[:, 8:12], in_=gfbuf[:, 0:4]),
           reads=[r_gf], writes=[r_outb])
    for dc in range(2):
        E.emit("pe", lambda t, dc=dc: [t.matmul(
                   PL[:, 8 + dc:9 + dc], wstage[:, c * 256 + dc * 128: c * 256 + (dc + 1) * 128],
                   gcb[:, 8 + c:9 + c], start=(c == 0), stop=(c == 3)) for c in range(4)],
               reads=[r_wstage, r_outb], writes=[C.rPL])
    for dc in range(2):
        E.emit("act", lambda s, dc=dc: s.activation(out=gfbuf[:, 4 + dc:5 + dc], in_=PL[:, 8 + dc:9 + dc],
                                                    func=AF.Gelu, bias=bvec[:, G2 + dc:G2 + dc + 1]),
               reads=[C.rPL, r_bvec], writes=[r_gf])
    E.dma("pool", lambda g: g.dma_start(out=wstage[:, 0:2], in_=W["gc_w3c"]), reads=[W["r_gc_w3c"]], writes=[r_wstage])
    E.emit("dve", lambda v: v.tensor_copy(out=gcb[:, 12:14], in_=gfbuf[:, 4:6]),
           reads=[r_gf], writes=[r_outb])
    E.emit("pe", lambda t: [t.matmul(PL[0:1, 16:17], wstage[:, c:c + 1], gcb[:, 12 + c:13 + c],
                                     start=(c == 0), stop=(c == 1)) for c in range(2)],
           reads=[r_wstage, r_outb], writes=[C.rPL])
    E.emit("act", lambda s: s.activation(out=gfbuf[0:1, 6:7], in_=PL[0:1, 16:17], func=AF.Sigmoid,
                                         bias=bvec[0:1, SC + 1:SC + 2]),
           reads=[C.rPL, r_bvec], writes=[r_gf])
    E.emit("dve", lambda v: v.tensor_scalar(out=gfbuf[0:1, 7:8], in0=gfbuf[0:1, 6:7],
                                            scalar1=0.7, scalar2=None, op0=ALU.mult),
           reads=[r_gf], writes=[r_gf])

    C.w2stage = C.sb(tag + "w2", [128, NCH * 512], BF16); C.r_w2stage = E.dma_res(tag + "w2")
    C.w3stage = C.sb(tag + "w3", [128, 4 * 256], BF16); C.r_w3stage = E.dma_res(tag + "w3")
    C.w4stage = C.sb(tag + "w4", [128, 2 * 1], BF16); C.r_w4stage = E.dma_res(tag + "w4")
    load_weight_lhsT(C, W["cn_w1"], W["r_cn_w1"], wstage, r_wstage, slice(0, D), "pool")
    load_weight_lhsT(C, W["cn_w2"], W["r_cn_w2"], C.w2stage, C.r_w2stage, slice(0, 512), "pool")
    load_weight_lhsT(C, W["cn_w3"], W["r_cn_w3"], C.w3stage, C.r_w3stage, slice(0, 256), "pool")
    E.dma("pool", lambda g: g.dma_start(out=C.w4stage[:], in_=W["cn_w4c"]), reads=[W["r_cn_w4c"]], writes=[C.r_w4stage])
    for cyc in range(3):
        for j in range(NT):
            for dc in range(NCH):
                h = dc % 2
                E.emit("pe", lambda t, dc=dc, h=h, j=j: [t.matmul(
                           PB[:, h * 512:(h + 1) * 512],
                           wstage[:, c * D + dc * 128: c * D + (dc + 1) * 128],
                           out[:, c * T + j * 512: c * T + (j + 1) * 512],
                           start=(c == 0), stop=(c == NCH - 1)) for c in range(NCH)],
                       reads=[r_wstage, r_out], writes=[C.rPB[h]])
                E.emit("act", lambda s, dc=dc, h=h: s.activation(
                           out=h1b[:, dc * 512:(dc + 1) * 512], in_=PB[:, h * 512:(h + 1) * 512],
                           func=AF.Gelu, bias=bvec[:, B1 + dc:B1 + dc + 1]),
                       reads=[C.rPB[h], r_bvec], writes=[r_h1])
            for dc in range(4):
                h = dc % 2
                E.emit("pe", lambda t, dc=dc, h=h: [t.matmul(
                           PC[:, h * 512:(h + 1) * 512],
                           C.w2stage[:, c * 512 + dc * 128: c * 512 + (dc + 1) * 128],
                           h1b[:, c * 512:(c + 1) * 512],
                           start=(c == 0), stop=(c == NCH - 1)) for c in range(NCH)],
                       reads=[C.r_w2stage, r_h1], writes=[C.rPC[h]])
                E.emit("act", lambda s, dc=dc, h=h: s.activation(
                           out=h2b[:, dc * 512:(dc + 1) * 512], in_=PC[:, h * 512:(h + 1) * 512],
                           func=AF.Gelu, bias=bvec[:, B2 + dc:B2 + dc + 1]),
                       reads=[C.rPC[h], r_bvec], writes=[r_h2])
            for dc in range(2):
                E.emit("pe", lambda t, dc=dc: [t.matmul(
                           PL[:, 0:512],
                           C.w3stage[:, c * 256 + dc * 128: c * 256 + (dc + 1) * 128],
                           h2b[:, c * 512:(c + 1) * 512],
                           start=(c == 0), stop=(c == 3)) for c in range(4)],
                       reads=[C.r_w3stage, r_h2], writes=[C.rPL])
                E.emit("act", lambda s, dc=dc: s.activation(
                           out=h3b[:, dc * 512:(dc + 1) * 512], in_=PL[:, 0:512],
                           func=AF.Gelu, bias=bvec[:, B3 + dc:B3 + dc + 1]),
                       reads=[C.rPL, r_bvec], writes=[r_h3])
            E.emit("pe", lambda t: [t.matmul(PL[0:1, 0:512], C.w4stage[:, c:c + 1],
                                             h3b[:, c * 512:(c + 1) * 512],
                                             start=(c == 0), stop=(c == 1)) for c in range(2)],
                   reads=[C.r_w4stage, r_h3], writes=[C.rPL])
            E.emit("act", lambda s, j=j: s.activation(out=tfrow[0:1, j * 512:(j + 1) * 512],
                                                      in_=PL[0:1, 0:512], func=AF.Sigmoid,
                                                      bias=bvec[0:1, SC:SC + 1]),
                   reads=[C.rPL, r_bvec], writes=[r_tf])
            E.emit("dve", lambda v, j=j: v.tensor_scalar(
                       out=crow[:], in0=tfrow[0:1, j * 512:(j + 1) * 512],
                       scalar1=0.3, scalar2=gfbuf[0:1, 7:8], op0=ALU.mult, op1=ALU.add),
                   reads=[r_tf, r_gf], writes=[r_crow])
            E.emit("dve", lambda v: v.tensor_scalar(
                       out=crow[:], in0=crow[:], scalar1=0.5, scalar2=bvec[0:1, SC + 2:SC + 3],
                       op0=ALU.subtract, op1=ALU.mult),
                   reads=[r_crow, r_bvec], writes=[r_crow])
            E.emit("dve", lambda v: v.tensor_scalar(
                       out=crow[:], in0=crow[:], scalar1=0.05, scalar2=2.0,
                       op0=ALU.add, op1=ALU.mult),
                   reads=[r_crow], writes=[r_crow])
            E.emit("pe", lambda t: t.matmul(PL[:, 0:512], C.ones_row_f32[0:1, :], crow[0:1, :],
                                            start=True, stop=True),
                   reads=[r_crow, C.r_ones_row], writes=[C.rPL])
            for a in range(4):
                b = a + 4
                s_a = +1 if a < 2 else -1
                s_b = -s_a
                sla = slice(a * T + j * 512, a * T + (j + 1) * 512)
                slb = slice(b * T + j * 512, b * T + (j + 1) * 512)
                E.emit("dve", lambda v, sla=sla: v.tensor_copy(out=ttmp[:], in_=out[:, sla]),
                       reads=[r_out], writes=[r_ttmp])
                E.emit("dve", lambda v, slb=slb: v.tensor_tensor(out=mtmp[:], in0=out[:, slb],
                                                                 in1=PL[:, 0:512], op=ALU.mult),
                       reads=[r_out, C.rPL], writes=[r_mtmp])
                E.emit("dve", lambda v, sla=sla, s_a=s_a: v.tensor_tensor(
                           out=out[:, sla], in0=out[:, sla], in1=mtmp[:],
                           op=(ALU.add if s_a > 0 else ALU.subtract)),
                       reads=[r_mtmp], writes=[r_out])
                E.emit("dve", lambda v: v.tensor_tensor(out=mtmp[:], in0=ttmp[:],
                                                        in1=PL[:, 0:512], op=ALU.mult),
                       reads=[r_ttmp, C.rPL], writes=[r_mtmp])
                E.emit("dve", lambda v, slb=slb, s_b=s_b: v.tensor_tensor(
                           out=out[:, slb], in0=out[:, slb], in1=mtmp[:],
                           op=(ALU.add if s_b > 0 else ALU.subtract)),
                       reads=[r_mtmp], writes=[r_out])

    ybuf, r_ybuf = C.slabA, C.r_slabA
    for j in range(NT):
        E.dma("sync", lambda s, j=j: s.dma_start(out=sb_slab(ybuf[:]), in_=ap_slab(xT_in, j)),
              reads=[r_xin], writes=[r_ybuf])
        for c in range(NCH):
            E.emit("dve", lambda v, c=c, j=j: v.tensor_tensor(
                       out=ybuf[:, c * 512:(c + 1) * 512], in0=ybuf[:, c * 512:(c + 1) * 512],
                       in1=out[:, c * T + j * 512: c * T + (j + 1) * 512], op=ALU.add),
                   reads=[r_out], writes=[r_ybuf])
        E.dma("pool", lambda g, j=j: g.dma_start(out=ap_slab(xT_out, j), in_=sb_slab(ybuf[:])),
              reads=[r_ybuf], writes=[r_xout], dest=r_xout)
        emit_ep_ln(C, ybuf, r_ybuf, j, tT_next, r_tTn)


def emit_mamba(C, xT_in, r_xin, xT_out, r_xout, tT_hbm, r_tT, W, tT_next=None, r_tTn=None, skip_ln=False):
    E, nc, CT, T, NT = C.E, C.nc, C.CT, C.T, C.NT
    DI, NDI = 2048, 16
    PA, PB, PC, PL = C.PA, C.PB, C.PC, C.PL
    if not skip_ln:
        emit_ln_to_tT(C, xT_in, r_xin, tT_hbm, r_tT)

    inwT = C.sb("mb_inwT", [128, NCH * DI], BF16); r_inwT = E.dma_res("mb_inwT")
    precast(C, W, "in_wT")
    load_w(C, W, "in_wT", inwT, r_inwT, slice(0, DI), "act")
    xpw = C.sb("mb_xpw", [128, 16 * 65], BF16); r_xpw = E.dma_res("mb_xpw")
    E.dma("pool", lambda g: g.dma_start(out=xpw[:].rearrange("p (c m) -> p c m", m=65),
                                        in_=W["xp_wTa"].rearrange("(c p) m -> p c m", p=128)),
          reads=[W["r_xp_wTa"]], writes=[r_xpw])
    outwT = C.sb("mb_outwT", [16, NCH * 128], BF16); r_outwT = E.dma_res("mb_outwT")
    E.dma("pool", lambda g: g.dma_start(out=outwT[:].rearrange("p (c m) -> p c m", m=128),
                                        in_=W["out_wT"].rearrange("p (c m) -> p c m", m=128)),
          reads=[W["r_out_wT"]], writes=[r_outwT])
    vecs = C.sb("mb_vecs", [128, 16 * 3 + 64 + 2], F32); r_vecs = E.dma_res("mb_vecs")
    vecs_in = nc.dram_tensor("mb_vecs_in", [128, 114], F32, kind="ExternalInput").ap()
    E.dma("sync", lambda s: s.dma_start(out=vecs[:], in_=vecs_in), writes=[r_vecs])
    IB, CB, OB, CW, XB = 0, 16, 32, 48, 112
    hraw2 = [C.sb(f"mb_hraw{i}", [128, T], BF16) for i in range(2)]
    r_hraw2 = [E.res(f"mb_hraw{i}") for i in range(2)]
    acc = C.sb("mb_acc", [128, T], BF16); r_acc = E.res("mb_acc")
    xpacc = C.sb("mb_xpacc", [65, T], F32); r_xpacc = E.res("mb_xpacc")
    r16a = C.sb("mb_r16a", [16, T], F32); r_r16a = E.res("mb_r16a")
    r16b = C.sb("mb_r16b", [16, T], F32); r_r16b = E.res("mb_r16b")
    ysb = C.sb("mb_ysb", [16, T], BF16); r_ysb = E.res("mb_ysb")
    hsrow = C.sb("mb_hsrow", [1, T], BF16); r_hsrow = E.res("mb_hsrow")

    E.emit("dve", lambda v: v.memset(xpacc[:], 0.0), writes=[r_xpacc])
    for di in range(NDI):
        hraw, r_hraw = hraw2[di % 2], r_hraw2[di % 2]
        for j in range(NT):
            tb, r_tb = C.tslab[j % 2], C.r_tslab[j % 2]
            if di == 0 or True:
                pass
            E.dma("sync", lambda s, j=j: s.dma_start(out=sb_slab(tb[:]), in_=ap_slab(tT_hbm, j)),
                  reads=[r_tT], writes=[r_tb])
            h2 = j % 2
            Ps = PB if h2 == 0 else PC
            rPs = C.rPB if h2 == 0 else C.rPC
            E.emit("pe", lambda t, di=di, tb=tb, Ps=Ps: [t.matmul(
                       Ps[:, 0:512], inwT[:, c * DI + di * 128: c * DI + (di + 1) * 128],
                       tb[:, c * 512:(c + 1) * 512],
                       start=(c == 0), stop=(c == NCH - 1)) for c in range(NCH)],
                   reads=[r_inwT, r_tb], writes=[rPs[0]])
            E.emit("dve", lambda v, j=j, di=di, Ps=Ps: v.tensor_scalar(
                       out=hraw[:, j * 512:(j + 1) * 512], in0=Ps[:, 0:512],
                       scalar1=vecs[:, IB + di:IB + di + 1], scalar2=None, op0=ALU.add),
                   reads=[rPs[0], r_vecs], writes=[r_hraw])
        E.emit("dve", lambda v, di=di: v.tensor_scalar(
                   out=acc[:], in0=hraw[:], scalar1=vecs[:, CW + di * 4 + 3:CW + di * 4 + 4],
                   scalar2=None, op0=ALU.mult),
               reads=[r_hraw, r_vecs], writes=[r_acc])
        for s_ in range(1, 4):
            E.emit("dve", lambda v, di=di, s_=s_: v.scalar_tensor_tensor(
                       out=acc[:, s_:T], in0=hraw[:, 0:T - s_],
                       scalar=vecs[:, CW + di * 4 + 3 - s_:CW + di * 4 + 4 - s_],
                       in1=acc[:, s_:T], op0=ALU.mult, op1=ALU.add),
                   reads=[r_hraw, r_vecs, r_acc], writes=[r_acc])
        E.emit("act", lambda s, di=di: s.activation(out=hraw[:], in_=acc[:], func=AF.Silu,
                                                    bias=vecs[:, CB + di:CB + di + 1]),
               reads=[r_acc, r_vecs], writes=[r_hraw])
        for j in range(NT):
            E.emit("pe", lambda t, j=j, di=di: t.matmul(
                       PL[0:65, 0:512], xpw[:, di * 65:(di + 1) * 65], hraw[:, j * 512:(j + 1) * 512],
                       start=True, stop=True),
                   reads=[r_xpw, r_hraw], writes=[C.rPL])
            E.emit("dve", lambda v, j=j: v.tensor_tensor(
                       out=xpacc[:, j * 512:(j + 1) * 512], in0=xpacc[:, j * 512:(j + 1) * 512],
                       in1=PL[0:65, 0:512], op=ALU.add),
                   reads=[C.rPL], writes=[r_xpacc])
    E.emit("act", lambda s: s.activation(out=r16a[:], in_=xpacc[0:16, :], func=AF.Exp,
                                         bias=vecs[0:16, XB:XB + 1]),
           reads=[r_xpacc, r_vecs], writes=[r_r16a])
    E.emit("act", lambda s: s.activation(out=r16a[:], in_=r16a[:], func=AF.Exp,
                                         scale=CT.f32(-1.0)),
           reads=[r_r16a, C.r_const], writes=[r_r16a])
    E.emit("act", lambda s: s.activation(out=r16b[:], in_=xpacc[32:48, :], func=AF.Silu,
                                         bias=vecs[32:48, XB + 1:XB + 2]),
           reads=[r_xpacc, r_vecs], writes=[r_r16b])
    E.emit("dve", lambda v: v.tensor_copy(out=hsrow[:], in_=xpacc[64:65, :]),
           reads=[r_xpacc], writes=[r_hsrow])
    for j in range(NT):
        E.emit("pe", lambda t, j=j: t.matmul(
                   PL[0:16, 0:512], C.ones_row_bf[0:1, 0:16], hsrow[0:1, j * 512:(j + 1) * 512],
                   start=True, stop=True),
               reads=[r_hsrow, C.r_ones_row_b], writes=[C.rPL])
        E.emit("dve", lambda v, j=j: v.tensor_tensor(
                   out=r16b[:, j * 512:(j + 1) * 512], in0=r16b[:, j * 512:(j + 1) * 512],
                   in1=PL[0:16, 0:512], op=ALU.mult),
               reads=[C.rPL], writes=[r_r16b])
    E.emit("dve", lambda v: v.tensor_tensor_scan(out=ysb[:], data0=r16a[:], data1=r16b[:],
                                                 initial=0.0, op0=ALU.mult, op1=ALU.add),
           reads=[r_r16a, r_r16b], writes=[r_ysb])
    ybuf, r_ybuf = C.slabA, C.r_slabA
    for j in range(NT):
        E.dma("sync", lambda s, j=j: s.dma_start(out=sb_slab(ybuf[:]), in_=ap_slab(xT_in, j)),
              reads=[r_xin], writes=[r_ybuf])
        for dc in range(NCH):
            h2 = dc % 2
            Ps = PB if h2 == 0 else PC
            rPs = C.rPB if h2 == 0 else C.rPC
            E.emit("pe", lambda t, dc=dc, j=j, Ps=Ps: t.matmul(
                       Ps[:, 0:512], outwT[:, dc * 128:(dc + 1) * 128],
                       ysb[:, j * 512:(j + 1) * 512], start=True, stop=True),
                   reads=[r_outwT, r_ysb], writes=[rPs[0]])
            E.emit("dve", lambda v, dc=dc, Ps=Ps: v.scalar_tensor_tensor(
                       out=ybuf[:, dc * 512:(dc + 1) * 512], in0=Ps[:, 0:512],
                       scalar=vecs[:, OB + dc:OB + dc + 1], in1=ybuf[:, dc * 512:(dc + 1) * 512],
                       op0=ALU.add, op1=ALU.add),
                   reads=[rPs[0], r_vecs], writes=[r_ybuf])
        E.dma("pool", lambda g, j=j: g.dma_start(out=ap_slab(xT_out, j), in_=sb_slab(ybuf[:])),
              reads=[r_ybuf], writes=[r_xout], dest=r_xout)
        emit_ep_ln(C, ybuf, r_ybuf, j, tT_next, r_tTn)


def emit_moe_prologue(C, W):
    E, nc = C.E, C.nc
    M = 1408
    aps = {}
    for nm, shape in (("w_g", (8, D, M)), ("w_u", (8, D, M)), ("w_d", (8, M, D))):
        bf = nc.dram_tensor(nm + "_bfs", list(shape), BF16, kind="Internal").ap()
        r = E.dma_res(nm + "_bf")
        W[nm + "_bf"] = bf
        W["r_" + nm + "_bf"] = r
        for e in range(8):
            q = "pool"
            E.dma(q, lambda g, nm=nm, e=e, bf=bf: g.dma_start(out=bf[e], in_=W[nm][e]),
                  reads=[W["r_" + nm]], writes=[r], dest=r)
    return W



T_FULL = 4096
N_CORES = 8
_FUSED_CACHE = {}


def mk_fused(T):
    C = Ctx(T)
    nc = C.nc
    C.open()
    E = C.E
    xT_in = nc.dram_tensor("xT_in", [D, T], F32, kind="ExternalInput").ap()
    xT_out = nc.dram_tensor("xT_out", [D, T], F32, kind="ExternalOutput").ap()
    scrA = nc.dram_tensor("scrA", [D, T], F32, kind="Internal").ap()
    scrB = nc.dram_tensor("scrB", [D, T], F32, kind="Internal").ap()
    tTs = [nc.dram_tensor(f"tT_scratch{i}", [D, T], BF16, kind="Internal").ap() for i in range(2)]
    r_xin = E.res("xT_in")
    r_A = E.dma_res("scrA")
    r_B = E.dma_res("scrB")
    r_out = E.dma_res("xT_out")
    r_tTs = [E.dma_res(f"tT{i}") for i in range(2)]

    def decl(name, shape):
        ap = nc.dram_tensor(name, list(shape), F32, kind="ExternalInput").ap()
        return ap, E.res(name)

    WA = {}
    for n, shape in (("w_qkv", (D, 3 * D)), ("w_proj", (D, D))):
        WA[n], WA["r_" + n] = decl(n, shape)
    for n in ("bpc", "gammac", "betac"):
        sb_t = C.sb("v_" + n, [128, NCH], F32)
        r = E.dma_res("v_" + n)
        inp = nc.dram_tensor("in_" + n, [128, NCH], F32, kind="ExternalInput").ap()
        E.dma("sync", lambda s, sb_t=sb_t, inp=inp: s.dma_start(out=sb_t[:], in_=inp), writes=[r])
        WA[n] = sb_t
        WA["r_" + n] = r
    WM = {}
    for n, shape in (("w_gate", (D, 8)), ("w_g", (8, D, 1408)), ("w_u", (8, D, 1408)),
                     ("w_d", (8, 1408, D))):
        WM[n], WM["r_" + n] = decl(n, shape)
    WB = {}
    for n in ("w_qT", "w_kT", "w_vT", "w_oT"):
        WB[n], WB["r_" + n] = decl(n, (D, D))
    W5 = {}
    for n, shape in (("W_in", (D, 4 * D)), ("W_out", (D, D))):
        W5[n], W5["r_" + n] = decl(n, shape)
    W6 = {}
    for n, shape in (("cn_w1", (D, D)), ("cn_w2", (D, 512)), ("cn_w3", (512, 256)),
                     ("gc_w1", (D, 512)), ("gc_w2", (512, 256))):
        W6[n], W6["r_" + n] = decl(n, shape)
    for n in ("cn_w4c", "gc_w3c"):
        W6[n], W6["r_" + n] = decl(n, (128, 2))
    W7 = {}
    for n, shape in (("in_wT", (D, 2048)), ("xp_wTa", (2048, 65)), ("out_wT", (16, D))):
        W7[n], W7["r_" + n] = decl(n, shape)

    emit_moe_prologue(C, WM)
    chain = [(xT_in, r_xin), (scrA, r_A), (scrB, r_B)]
    seq = [("attn_a0", None), ("moe", "m2_"), ("attn_b0", None), ("moe", "m4_"),
           ("attn_a1", None), ("l1ffn", "f5_"), ("mamba", None), ("l1ffn", "f7_")]
    cur = chain[0]
    for si, (kind, tag) in enumerate(seq):
        nxt = (xT_out, r_out) if si == len(seq) - 1 else chain[1 + (si % 2)]
        tT, r_tT = tTs[si % 2], r_tTs[si % 2]
        if si < len(seq) - 1:
            tT_next, r_tTn = tTs[(si + 1) % 2], r_tTs[(si + 1) % 2]
        else:
            tT_next, r_tTn = None, None
        kw = dict(tT_next=tT_next, r_tTn=r_tTn, skip_ln=(si > 0))
        C.push_scope(f"s{si}_")
        if kind == "attn_a0":
            emit_attn_a0(C, cur[0], cur[1], nxt[0], nxt[1], tT, r_tT, WA, **kw)
        elif kind == "moe":
            emit_moe(C, cur[0], cur[1], nxt[0], nxt[1], tT, r_tT, WM, tag, **kw)
        elif kind == "attn_b0":
            emit_attn_b0(C, cur[0], cur[1], nxt[0], nxt[1], tT, r_tT, WB, **kw)
        elif kind == "attn_a1":
            emit_attn_a1(C, cur[0], cur[1], nxt[0], nxt[1], tT, r_tT, W5, **kw)
        elif kind == "l1ffn":
            emit_l1ffn(C, cur[0], cur[1], nxt[0], nxt[1], tT, r_tT, W6, tag, **kw)
        elif kind == "mamba":
            emit_mamba(C, cur[0], cur[1], nxt[0], nxt[1], tT, r_tT, W7, **kw)
        C.barrier()
        C.pop_scope()
        cur = nxt
    E.wait_all("sync")
    C.finish()
    return C


def _fused_inputs(C, P):
    d = dict(C.host_consts)
    d["w_qkv"] = P["L0_attn_a_w_qkv"]
    d["w_proj"] = P["L0_attn_a_w_proj"]
    d["in_bpc"] = chunked_vec(P["L0_attn_a_b_proj"])
    d["in_gammac"] = chunked_vec(P["L0_attn_a_ln_gamma"])
    d["in_betac"] = chunked_vec(P["L0_attn_a_ln_beta"])
    d["w_gate"] = P["L0_ffn_w_gate"]
    d["w_g"] = P["L0_ffn_w_g"]
    d["w_u"] = P["L0_ffn_w_u"]
    d["w_d"] = P["L0_ffn_w_d"]
    d["w_qT"] = np.ascontiguousarray(P["L0_attn_b_w_q"].T)
    d["w_kT"] = np.ascontiguousarray(P["L0_attn_b_w_k"].T)
    d["w_vT"] = np.ascontiguousarray(P["L0_attn_b_w_v"].T)
    d["w_oT"] = np.ascontiguousarray(P["L0_attn_b_w_o"].T)
    d["W_in"] = P["L1_attn_a_W_in"]
    d["W_out"] = P["L1_attn_a_W_out"]
    d["a1_binc_in"] = chunked_vec(P["L1_attn_a_b_in"])
    d["a1_boutc_in"] = chunked_vec(P["L1_attn_a_b_out"])
    for n in ("cn_w1", "cn_w2", "cn_w3", "gc_w1", "gc_w2"):
        d[n] = P["L1_ffn_" + n]
    d["cn_w4c"] = chunked_vec(P["L1_ffn_cn_w4"][:, 0])
    d["gc_w3c"] = chunked_vec(P["L1_ffn_gc_w3"][:, 0])
    bvec = np.zeros((128, 24), np.float32)
    bvec[:, 0:8] = chunked_vec(P["L1_ffn_cn_b1"])
    bvec[:, 8:12] = chunked_vec(P["L1_ffn_cn_b2"])
    bvec[:, 12:14] = chunked_vec(P["L1_ffn_cn_b3"])
    bvec[:, 14:18] = chunked_vec(P["L1_ffn_gc_b1"])
    bvec[:, 18:20] = chunked_vec(P["L1_ffn_gc_b2"])
    bvec[:, 20] = P["L1_ffn_cn_b4"][0]
    bvec[:, 21] = P["L1_ffn_gc_b3"][0]
    bvec[:, 22] = P["L1_ffn_adaptive_range"][0]
    d["f5_bvec_in"] = bvec
    d["f7_bvec_in"] = bvec
    DI = 2048
    d["in_wT"] = np.ascontiguousarray(P["L1_attn_b_in_w"].T)
    xp_w = P["L1_attn_b_xp_w"]
    xpa = np.zeros((DI, 65), np.float32)
    xpa[:, 0:16] = xp_w[0:16].T
    xpa[:, 32:48] = xp_w[16:32].T
    xpa[:, 64] = 1.0
    d["xp_wTa"] = xpa
    d["out_wT"] = np.ascontiguousarray(P["L1_attn_b_out_w"].T)
    vecs = np.zeros((128, 114), np.float32)
    vecs[:, 0:16] = chunked_vec(P["L1_attn_b_in_b"])
    vecs[:, 16:32] = chunked_vec(P["L1_attn_b_conv_b"])
    vecs[:, 32:40] = chunked_vec(P["L1_attn_b_out_b"])
    cw = P["L1_attn_b_conv_w"][:, 0, :]
    vecs[:, 48:112] = cw.reshape(16, 128, 4).transpose(1, 0, 2).reshape(128, 64)
    vecs[0:16, 112] = P["L1_attn_b_xp_b"][0:16]
    vecs[32:48, 113] = P["L1_attn_b_xp_b"][16:32]
    d["mb_vecs_in"] = vecs
    return d


def kernel(**P):
    from concourse.bass_utils import run_bass_kernel_spmd
    if T_FULL not in _FUSED_CACHE:
        _FUSED_CACHE[T_FULL] = mk_fused(T_FULL)
    C = _FUSED_CACHE[T_FULL]
    x = np.asarray(P["L0_attn_a_target"])
    B = x.shape[0]
    shared = _fused_inputs(C, {k: np.asarray(v) for k, v in P.items()})
    in_maps = [dict(shared, xT_in=np.ascontiguousarray(x[b].T).astype(np.float32))
               for b in range(B)]
    res = run_bass_kernel_spmd(C.nc, in_maps, core_ids=list(range(N_CORES)))
    out = np.stack([np.asarray(res.results[b]["xT_out"]).T for b in range(B)], axis=0)
    return out.astype(np.float32)


# version 15
# speedup vs baseline: 1.0506x; 1.0121x over previous
import contextlib
import numpy as np
import concourse.bass as bass
import concourse.mybir as mybir

F32 = mybir.dt.float32
BF16 = mybir.dt.bfloat16
AF = mybir.ActivationFunctionType
ALU = mybir.AluOpType
AX = mybir.AxisListType

D = 1024
NCH = D // 128
NH = 16
HD = 64


class ConstTable:
    def __init__(self, nc, sb_tensor, cap):
        self.nc = nc
        self.sb = sb_tensor
        self.cap = cap
        self.vals = {}
    def f32(self, v):
        v = float(np.float32(v))
        if v not in self.vals:
            idx = len(self.vals)
            assert idx < self.cap, "const table full"
            self.vals[v] = idx
            self.nc.const_aps.register(mybir.dt.float32, v, self.sb[:, idx:idx + 1])
        return v
    def host_array(self):
        arr = np.zeros((128, self.cap), dtype=np.float32)
        for v, i in self.vals.items():
            arr[:, i] = v
        return arr


class Res:
    __slots__ = ("name", "writer", "readers", "sem", "sem_count")
    def __init__(self, name, sem=None):
        self.name = name
        self.writer = None
        self.readers = []
        self.sem = sem
        self.sem_count = 0


class Emitter:
    def __init__(self, nc, stack):
        self.nc = nc
        self.stack = stack
        self.engines = {}
        self.sems = {}
        self.counts = {}
        self.watermarks = {}
        self.n_waits = 0
        self.n_inst = 0
        for name, handle in (("pe", nc.tensor), ("act", nc.scalar),
                             ("dve", nc.vector), ("pool", nc.gpsimd),
                             ("sync", nc.sync)):
            self.engines[name] = handle
            self.watermarks[name] = {}
            if name != "sync":
                self._new_sem(name)

    def _new_sem(self, name):
        if name in self.sems:
            i = 2
            while f"{name}{i}" in self.sems:
                i += 1
            name = f"{name}{i}"
        sem = self.stack.enter_context(self.nc.semaphore(f"sem_{name}"))
        self.sems[name] = sem
        self.counts[name] = 0
        return name

    def res(self, name):
        return Res(name)

    def dma_res(self, name):
        return Res(name, sem=self._new_sem(name))


    def _wait(self, eng, sem_name, value):
        if value <= 0:
            return
        wm = self.watermarks[eng]
        if wm.get(sem_name, 0) >= value:
            return
        self.engines[eng].wait_ge(self.sems[sem_name], value)
        wm[sem_name] = value
        self.n_waits += 1

    def _deps(self, eng, reads, writes):
        def need(sname):
            return not (eng == "pe" and sname == "pe")
        for r in reads:
            if r.writer is not None and need(r.writer[0]):
                self._wait(eng, *r.writer)
        for w in writes:
            if w.writer is not None and need(w.writer[0]):
                self._wait(eng, *w.writer)
            for (sname, c) in w.readers:
                if need(sname):
                    self._wait(eng, sname, c)

    def emit(self, eng, fn, reads=(), writes=()):
        self._deps(eng, reads, writes)
        inst = fn(self.engines[eng])
        if isinstance(inst, list):
            inst = inst[-1]
        self.counts[eng] += 1
        cnt = self.counts[eng]
        inst.then_inc(self.sems[eng], 1)
        self.n_inst += 1
        for r in reads:
            r.readers.append((eng, cnt))
        for w in writes:
            w.writer = (eng, cnt)
            w.readers = []
        return inst

    def dma(self, queue, fn, reads=(), writes=(), dest=None):
        if dest is None:
            assert len(writes) == 1
            dest = writes[0]
        assert dest.sem is not None, f"{dest.name}: DMA dest needs dma_res()"
        self._deps(queue, reads, writes)
        inst = fn(self.engines[queue])
        self.counts[dest.sem] += 16
        cnt = self.counts[dest.sem]
        dest.sem_count = cnt
        inst.then_inc(self.sems[dest.sem], 16)
        self.n_inst += 1
        for r in reads:
            r.readers.append((dest.sem, cnt))
        for w in writes:
            w.writer = (dest.sem, cnt)
            w.readers = []
        return inst

    def wait_all(self, eng):
        for sname, c in self.counts.items():
            self._wait(eng, sname, c)


class Ctx:
    def __init__(self, T):
        self.T = T
        self.NT = T // 512
        self.nc = bass.Bass("TRN2", target_bir_lowering=False)
        self.stack = contextlib.ExitStack()
        self.host_consts = {}

    def input(self, name, shape):
        ap = self.nc.dram_tensor(name, list(shape), F32, kind="ExternalInput").ap()
        return ap, self.E.res("in_" + name)

    def open(self):
        nc = self.nc
        st = self.stack
        self.E = Emitter(nc, st)
        self.prefix = ""
        self.scope = None
        def _sb(name, shape, dt):
            tgt = self.scope if self.scope is not None else st
            return tgt.enter_context(nc.sbuf_tensor(self.prefix + name, shape, dt))
        self.sb = _sb
        self.PA = st.enter_context(nc.psum_tensor("PA", [128, 1024], F32))
        self.PB = st.enter_context(nc.psum_tensor("PB", [128, 1024], F32))
        self.PC = st.enter_context(nc.psum_tensor("PC", [128, 1024], F32))
        self.PL = st.enter_context(nc.psum_tensor("PL", [128, 512], F32))
        self.PT = st.enter_context(nc.psum_tensor("PT", [128, 128], BF16))
        E = self.E
        self.rPA = [E.res("PAh0"), E.res("PAh1")]
        self.rPB = [E.res("PBh0"), E.res("PBh1")]
        self.rPC = [E.res("PCh0"), E.res("PCh1")]
        self.rPL = E.res("PL")
        self.rPT = E.res("PT")
        self.const_sb = self.sb("const_sb", [128, 48], F32)
        self.CT = ConstTable(nc, self.const_sb, 48)
        self.r_const = E.dma_res("consts")
        self.consts_in = nc.dram_tensor("consts", [128, 48], F32, kind="ExternalInput").ap()
        self.ident_bf = self.sb("ident_bf", [128, 128], BF16)
        self.ident_f32 = self.sb("ident_f32", [128, 128], F32)
        self.r_ident = E.dma_res("ident")
        self.r_ident_b = E.dma_res("ident_b")
        self.ident_in = nc.dram_tensor("ident_in", [128, 128], F32, kind="ExternalInput").ap()
        self.ones_f32 = self.sb("ones_f32", [128, 2], F32)
        self.r_ones = E.dma_res("ones")
        self.ones_in = nc.dram_tensor("ones_in", [128, 2], F32, kind="ExternalInput").ap()
        self.ones_row_f32 = self.sb("ones_row_f32", [1, 128], F32)
        self.ones_row_bf = self.sb("ones_row_bf", [1, 128], BF16)
        self.r_ones_row = E.dma_res("ones_row")
        self.r_ones_row_b = E.dma_res("ones_row_b")
        self.ones_row_in = nc.dram_tensor("ones_row_in", [1, 128], F32, kind="ExternalInput").ap()
        E.dma("sync", lambda s: s.dma_start(out=self.const_sb[:], in_=self.consts_in), writes=[self.r_const])
        E.dma("sync", lambda s: s.dma_start(out=self.ident_f32[:], in_=self.ident_in), writes=[self.r_ident])
        E.dma("pool", lambda g: g.dma_start(out=self.ident_bf[:], in_=self.ident_in), writes=[self.r_ident_b])
        E.dma("sync", lambda s: s.dma_start(out=self.ones_f32[:], in_=self.ones_in), writes=[self.r_ones])
        E.dma("sync", lambda s: s.dma_start(out=self.ones_row_f32[:], in_=self.ones_row_in), writes=[self.r_ones_row])
        E.dma("pool", lambda g: g.dma_start(out=self.ones_row_bf[:], in_=self.ones_row_in), writes=[self.r_ones_row_b])
        self.host_consts["ident_in"] = np.eye(128, dtype=np.float32)
        self.host_consts["ones_in"] = np.ones((128, 2), dtype=np.float32)
        self.host_consts["ones_row_in"] = np.ones((1, 128), dtype=np.float32)
        self.slabA = self.sb("slabA", [128, NCH * 512], F32)
        self.slabB = self.sb("slabB", [128, NCH * 512], F32)
        self.r_slabA = E.dma_res("slabA")
        self.r_slabB = E.res("slabB")
        self.tslab = [self.sb(f"tslab{i}", [128, NCH * 512], BF16) for i in range(2)]
        self.r_tslab = [E.dma_res(f"tslab{i}") for i in range(2)]
        self.tepb = self.sb("tepb", [128, NCH * 512], BF16)
        self.r_tepb = self.E.res("tepb")
        self.stat = self.sb("stat", [1, 2048], F32)
        self.r_stat = E.res("stat")
        self.masks = None
        self.hsel = None
        self.ensure_masks()
        self.ensure_hsel()

    def ensure_masks(self):
        if self.masks is not None:
            return
        _sc, _pf = self.scope, self.prefix
        self.scope, self.prefix = None, ""
        try:
            self._ensure_masks_inner()
        finally:
            self.scope, self.prefix = _sc, _pf

    def _ensure_masks_inner(self):
        E, nc = self.E, self.nc
        self.masks = self.sb("masks", [128, 4 * 512], BF16)
        self.r_masks = E.dma_res("masks")
        self.masks_in = nc.dram_tensor("masks_in", [128, 2048], F32, kind="ExternalInput").ap()
        E.dma("pool", lambda g: g.dma_start(out=self.masks[:], in_=self.masks_in), writes=[self.r_masks])
        mk = np.zeros((128, 4, 512), np.float32)
        p = np.arange(128)[:, None]
        f = np.arange(512)[None, :]
        for ci, coff in enumerate((0, 128, 256, 384)):
            mk[:, ci, :] = (f >= p + coff).astype(np.float32)
        self.host_consts["masks_in"] = np.ascontiguousarray(mk.reshape(128, 2048))

    def push_scope(self, prefix):
        import contextlib as _ctx
        assert self.scope is None
        self.scope = _ctx.ExitStack()
        self.prefix = prefix

    def pop_scope(self):
        self.scope.close()
        self.scope = None
        self.prefix = ""

    def barrier(self):
        for eng in ("pe", "act", "dve", "pool", "sync"):
            self.E.wait_all(eng)

    def ensure_hsel(self):
        if getattr(self, "hsel", None) is not None:
            return
        _sc, _pf = self.scope, self.prefix
        self.scope, self.prefix = None, ""
        try:
            self._ensure_hsel_inner()
        finally:
            self.scope, self.prefix = _sc, _pf

    def _ensure_hsel_inner(self):
        E, nc = self.E, self.nc
        self.hsel = self.sb("hsel", [128, 2], F32)
        self.hsel2 = self.sb("hsel2", [2, 128], F32)
        self.hsel2b = self.sb("hsel2b", [66, 128], F32)
        self.r_hsel = E.dma_res("hsel")
        self.hsel_in = nc.dram_tensor("hsel_in", [128, 2], F32, kind="ExternalInput").ap()
        self.hsel2_in = nc.dram_tensor("hsel2_in", [2, 128], F32, kind="ExternalInput").ap()
        self.hsel2b_in = nc.dram_tensor("hsel2b_in", [66, 128], F32, kind="ExternalInput").ap()
        E.dma("sync", lambda s: s.dma_start(out=self.hsel[:], in_=self.hsel_in), writes=[self.r_hsel])
        E.dma("sync", lambda s: s.dma_start(out=self.hsel2[:], in_=self.hsel2_in), writes=[self.r_hsel])
        E.dma("sync", lambda s: s.dma_start(out=self.hsel2b[:], in_=self.hsel2b_in), writes=[self.r_hsel])
        h = np.zeros((128, 2), np.float32)
        h[0:64, 0] = 1.0
        h[64:128, 1] = 1.0
        self.host_consts["hsel_in"] = h
        self.host_consts["hsel2_in"] = np.ascontiguousarray(h.T)
        h2b = np.zeros((66, 128), np.float32)
        h2b[64, 0:64] = 1.0
        h2b[65, 64:128] = 1.0
        self.host_consts["hsel2b_in"] = h2b

    def finish(self):
        self.host_consts["consts"] = self.CT.host_array()
        return self.host_consts

    def close(self):
        self.stack.close()


def chunked_vec(v):
    return np.ascontiguousarray(v.reshape(-1, 128).T)




def ap_slab(hbm, j, w=512, p=128):
    return hbm[:, j * w:(j + 1) * w].rearrange("(c p) t -> p c t", p=p)


def sb_slab(sb_ap, w=512):
    return sb_ap.rearrange("p (c t) -> p c t", t=w)


def emit_lnstats(C, buf, r_buf, eps, chunks=None):
    E, CT = C.E, C.CT
    PL, PB = C.PL, C.PB
    sq, r_sq = C.slabB, C.r_slabB
    if chunks is None:
        chunks = [buf[:, c * 512:(c + 1) * 512] for c in range(NCH)]
    for c in range(NCH):
        E.emit("act", lambda s, c=c: s.activation(out=sq[:, c * 512:(c + 1) * 512], in_=chunks[c],
                                                  func=AF.Square),
               reads=[r_buf], writes=[r_sq])
    E.emit("pe", lambda t: [t.matmul(PL[0:1, :], C.ones_f32[:, 0:1], chunks[c],
                                     start=(c == 0), stop=(c == NCH - 1)) for c in range(NCH)],
           reads=[r_buf, C.r_ones], writes=[C.rPL])
    E.emit("pe", lambda t: [t.matmul(PL[32:33, :], C.ones_f32[:, 0:1], sq[:, c * 512:(c + 1) * 512],
                                     start=(c == 0), stop=(c == NCH - 1)) for c in range(NCH)],
           reads=[r_sq, C.r_ones], writes=[C.rPL])
    st_, r_st = C.stat, C.r_stat
    E.emit("dve", lambda v: v.tensor_scalar(out=st_[:, 0:512], in0=PL[0:1, :],
                                            scalar1=1.0 / D, scalar2=None, op0=ALU.mult),
           reads=[C.rPL], writes=[r_st])
    E.emit("dve", lambda v: v.tensor_tensor(out=st_[:, 512:1024], in0=st_[:, 0:512],
                                            in1=st_[:, 0:512], op=ALU.mult),
           reads=[r_st], writes=[r_st])
    E.emit("dve", lambda v: v.scalar_tensor_tensor(out=st_[:, 1024:1536], in0=PL[32:33, :],
                                                   scalar=1.0 / D, in1=st_[:, 512:1024],
                                                   op0=ALU.mult, op1=ALU.subtract),
           reads=[C.rPL, r_st], writes=[r_st])
    E.emit("act", lambda s: s.activation(out=st_[:, 512:1024], in_=st_[:, 1024:1536], func=AF.Ln,
                                         bias=CT.f32(eps)),
           reads=[r_st, C.r_const], writes=[r_st])
    E.emit("act", lambda s: s.activation(out=st_[:, 1536:2048], in_=st_[:, 512:1024], func=AF.Exp,
                                         scale=CT.f32(-0.5)),
           reads=[r_st, C.r_const], writes=[r_st])
    E.emit("pe", lambda t: t.matmul(PB[:, 0:512], C.ones_row_f32[0:1, :], st_[:, 0:512],
                                    start=True, stop=True),
           reads=[r_st, C.r_ones_row], writes=[C.rPB[0]])
    E.emit("pe", lambda t: t.matmul(PB[:, 512:1024], C.ones_row_f32[0:1, :], st_[:, 1536:2048],
                                    start=True, stop=True),
           reads=[r_st, C.r_ones_row], writes=[C.rPB[1]])


def emit_ln_to_tT(C, xT_in, r_xin, tT_hbm, r_tT, eps=1e-5):
    E, T, NT = C.E, C.T, C.NT
    xs, r_xs = C.slabA, C.r_slabA
    tb, r_tb = C.tslab[0], C.r_tslab[0]
    for j in range(NT):
        E.dma("sync", lambda s, j=j: s.dma_start(out=sb_slab(xs[:]), in_=ap_slab(xT_in, j)),
              reads=[r_xin], writes=[r_xs])
        emit_lnstats(C, xs, r_xs, eps)
        for c in range(NCH):
            sl = slice(c * 512, (c + 1) * 512)
            E.emit("dve", lambda v, sl=sl: v.tensor_tensor(out=C.slabB[:, sl], in0=xs[:, sl],
                                                           in1=C.PB[:, 0:512], op=ALU.subtract),
                   reads=[r_xs, C.rPB[0]], writes=[C.r_slabB])
            E.emit("dve", lambda v, sl=sl: v.tensor_tensor(out=tb[:, sl], in0=C.slabB[:, sl],
                                                           in1=C.PB[:, 512:1024], op=ALU.mult),
                   reads=[C.r_slabB, C.rPB[1]], writes=[r_tb])
        E.dma("pool", lambda g, j=j: g.dma_start(out=ap_slab(tT_hbm, j), in_=sb_slab(tb[:])),
              reads=[r_tb], writes=[r_tT], dest=r_tT)


def emit_ep_ln(C, ybuf, r_ybuf, jw, tT_next, r_tTn, eps=1e-5, chunks=None, tb_idx=0):
    E, T = C.E, C.T
    if tT_next is None:
        return
    emit_lnstats(C, ybuf, r_ybuf, eps, chunks=chunks)
    if chunks is None:
        chunks = [ybuf[:, c * 512:(c + 1) * 512] for c in range(NCH)]
    tb, r_tb = C.tepb, C.r_tepb
    for c in range(NCH):
        sl = slice(c * 512, (c + 1) * 512)
        E.emit("dve", lambda v, sl=sl, c=c: v.tensor_tensor(out=C.slabB[:, sl], in0=chunks[c],
                                                            in1=C.PB[:, 0:512], op=ALU.subtract),
               reads=[r_ybuf, C.rPB[0]], writes=[C.r_slabB])
        E.emit("dve", lambda v, sl=sl: v.tensor_tensor(out=tb[:, sl], in0=C.slabB[:, sl],
                                                       in1=C.PB[:, 512:1024], op=ALU.mult),
               reads=[C.r_slabB, C.rPB[1]], writes=[r_tb])
    E.dma("pool", lambda g, jw=jw: g.dma_start(out=ap_slab(tT_next, jw), in_=sb_slab(tb[:])),
          reads=[r_tb], writes=[r_tTn], dest=r_tTn)

def precast(C, W, name, queue="pool"):
    if name + "_bf" in W:
        return
    E, nc = C.E, C.nc
    ap = W[name]
    bf = nc.dram_tensor(name + "_bfs", list(ap.shape), BF16, kind="Internal").ap()
    r = E.dma_res(name + "_bfr")
    E.dma(queue, lambda g: g.dma_start(out=bf, in_=ap), reads=[W["r_" + name]], writes=[r])
    W[name + "_bf"] = bf
    W["r_" + name + "_bf"] = r


def load_weight_lhsT(C, w_hbm, r_w, dst_sb, r_dst, cols, queue="pool"):
    E = C.E
    m = cols.stop - cols.start
    nrow = w_hbm.shape[0] // 128
    src = w_hbm[:, cols].rearrange("(c p) m -> p c m", p=128)
    dst3 = dst_sb[:, 0:nrow * m].rearrange("p (c m) -> p c m", m=m)
    E.dma(queue, lambda g: g.dma_start(out=dst3, in_=src),
          reads=[r_w], writes=[r_dst])


def load_w(C, W, name, dst_sb, r_dst, cols, queue="act"):
    load_weight_lhsT(C, W[name + "_bf"], W["r_" + name + "_bf"], dst_sb, r_dst, cols, queue)


def emit_attn_a0(C, xT_in, r_xin, xT_out, r_xout, tT_hbm, r_tT, W, tT_next=None, r_tTn=None, skip_ln=False):
    E, nc, CT, T, NT = C.E, C.nc, C.CT, C.T, C.NT
    NQ = T // 512
    NTK = T // 128
    SHIFT = 8.0
    PA, PB, PC, PL, PT = C.PA, C.PB, C.PC, C.PL, C.PT
    C.ensure_masks()

    precast(C, W, "w_qkv")
    precast(C, W, "w_proj")
    if not skip_ln:
        emit_ln_to_tT(C, xT_in, r_xin, tT_hbm, r_tT)

    QT = C.sb("aa_QT", [128, T], BF16); r_QT = E.res("aa_QT")
    KT = C.sb("aa_KT", [128, T], BF16); r_KT = E.res("aa_KT")
    VTs = C.sb("aa_VTs", [128, 512], BF16); r_VTs = E.res("aa_VTs")
    Vp = C.sb("aa_Vp", [128, NTK * 130], BF16); r_Vp = E.res("aa_Vp")
    Pt = [C.sb(f"aa_P{i}", [128, 1024], BF16) for i in range(2)]
    r_Pt = [E.res(f"aa_P{i}") for i in range(2)]
    attnO = C.sb("aa_attnO", [128, NCH * T], BF16); r_attnO = E.res("aa_attnO")
    recrow = C.sb("aa_recrow", [1, 1024], F32); r_rec = E.res("aa_recrow")
    rb = C.sb("aa_rb", [128, 1024], F32); r_rb = E.res("aa_rb")
    wstage = C.sb("wstage", [128, NCH * D], BF16); r_wstage = E.dma_res("wstage")

    for i in range(NTK):
        E.emit("dve", lambda v, i=i: v.memset(Vp[:, i * 130 + 64:i * 130 + 65], 1.0), writes=[r_Vp])
        E.emit("dve", lambda v, i=i: v.memset(Vp[:, i * 130 + 129:i * 130 + 130], 1.0), writes=[r_Vp])

    w_qkv, r_wqkv = W["w_qkv"], W["r_w_qkv"]
    wq = wstage[:, 0:NCH * 128]
    wk = wstage[:, NCH * 128:2 * NCH * 128]
    wv = wstage[:, 2 * NCH * 128:3 * NCH * 128]

    pend = [None]
    for hp in range(NH // 2):
        if pend[0] is not None:
            pend[0]()
            pend[0] = None
        c0 = hp * 128
        load_w(C, W, "w_qkv", wq, r_wstage, slice(c0, c0 + 128), "act")
        load_w(C, W, "w_qkv", wk, r_wstage, slice(D + c0, D + c0 + 128), "act")
        load_w(C, W, "w_qkv", wv, r_wstage, slice(2 * D + c0, 2 * D + c0 + 128), "act")
        for j in range(NT):
            tb, r_tb = C.tslab[j % 2], C.r_tslab[j % 2]
            E.dma("sync", lambda s, j=j: s.dma_start(out=sb_slab(tb[:]), in_=ap_slab(tT_hbm, j)),
                  reads=[r_tT], writes=[r_tb])
            for wi, (wsb, dst, r_dst) in enumerate(((wq, QT, r_QT), (wk, KT, r_KT))):
                E.emit("pe", lambda t, wsb=wsb, tb=tb, wi=wi: [t.matmul(
                           PB[:, wi * 512:(wi + 1) * 512], wsb[:, c * 128:(c + 1) * 128],
                           tb[:, c * 512:(c + 1) * 512],
                           start=(c == 0), stop=(c == NCH - 1)) for c in range(NCH)],
                       reads=[r_wstage, r_tb], writes=[C.rPB[wi]])
                E.emit("dve", lambda v, dst=dst, j=j, wi=wi: v.tensor_copy(
                           out=dst[:, j * 512:(j + 1) * 512], in_=PB[:, wi * 512:(wi + 1) * 512]),
                       reads=[C.rPB[wi]], writes=[r_dst])
            E.emit("pe", lambda t, tb=tb: [t.matmul(
                       PC[:, 0:512], wv[:, c * 128:(c + 1) * 128], tb[:, c * 512:(c + 1) * 512],
                       start=(c == 0), stop=(c == NCH - 1)) for c in range(NCH)],
                   reads=[r_wstage, r_tb], writes=[C.rPC[0]])
            E.emit("dve", lambda v: v.tensor_copy(out=VTs[:], in_=PC[:, 0:512]),
                   reads=[C.rPC[0]], writes=[r_VTs])
            for k in range(4):
                i = j * 4 + k
                E.emit("pe", lambda t, k=k: t.transpose(PT[:], VTs[:, k * 128:(k + 1) * 128], C.ident_bf[:]),
                       reads=[r_VTs, C.r_ident_b], writes=[C.rPT])
                E.emit("dve", lambda v, i=i: v.tensor_copy(out=Vp[:, i * 130:i * 130 + 64],
                                                           in_=PT[:, 0:64]),
                       reads=[C.rPT], writes=[r_Vp])
                E.emit("dve", lambda v, i=i: v.tensor_copy(out=Vp[:, i * 130 + 65:i * 130 + 129],
                                                           in_=PT[:, 64:128]),
                       reads=[C.rPT], writes=[r_Vp])
        for qb in range(NQ):
            ntk = (qb + 1) * 4
            qc = slice(qb * 512, (qb + 1) * 512)
            def emit_S(i, qc=qc):
                ib = i % 2
                Ps = PB if ib == 0 else PC
                rS = C.rPB if ib == 0 else C.rPC
                E.emit("pe", lambda t, i=i, Ps=Ps, qc=qc: [t.matmul(
                           Ps[:, 0:512], KT[0:64, i * 128:(i + 1) * 128], QT[0:64, qc],
                           start=True, stop=True, tile_position=(0, 0)),
                       t.matmul(
                           Ps[:, 512:1024], KT[64:128, i * 128:(i + 1) * 128], QT[64:128, qc],
                           start=True, stop=True, tile_position=(64, 0))],
                       reads=[r_KT, r_QT], writes=[rS[0], rS[1]])
            def emit_elem(i, qb=qb):
                ib = i % 2
                Ps = PB if ib == 0 else PC
                rS = C.rPB if ib == 0 else C.rPC
                E.emit("act", lambda s, Ps=Ps, ib=ib: s.activation(
                           out=Pt[ib][:], in_=Ps[:, 0:1024], func=AF.Exp,
                           scale=CT.f32(0.125), bias=CT.f32(-SHIFT)),
                       reads=[rS[0], rS[1], C.r_const], writes=[r_Pt[ib]])
                coff = i * 128 - qb * 512
                if coff >= 0:
                    ci = coff // 128
                    for half in range(2):
                        E.emit("dve", lambda v, ib=ib, ci=ci, half=half: v.tensor_tensor(
                                   out=Pt[ib][:, half * 512:(half + 1) * 512],
                                   in0=Pt[ib][:, half * 512:(half + 1) * 512],
                                   in1=C.masks[:, ci * 512:(ci + 1) * 512], op=ALU.mult),
                               reads=[C.r_masks], writes=[r_Pt[ib]])
            def emit_PV(i, ntk=ntk):
                ib = i % 2
                E.emit("pe", lambda t, i=i, ib=ib, ntk=ntk: t.matmul(
                           PA[0:65, 0:512], Vp[:, i * 130:i * 130 + 65], Pt[ib][:, 0:512],
                           start=(i == 0), stop=(i == ntk - 1)),
                       reads=[r_Vp, r_Pt[ib]], writes=[C.rPA[0]])
                E.emit("pe", lambda t, i=i, ib=ib, ntk=ntk: t.matmul(
                           PA[0:65, 512:1024], Vp[:, i * 130 + 65:i * 130 + 130], Pt[ib][:, 512:1024],
                           start=(i == 0), stop=(i == ntk - 1)),
                       reads=[r_Vp, r_Pt[ib]], writes=[C.rPA[1]])
            for i in range(ntk):
                emit_S(i)
                if i == 0 and pend[0] is not None:
                    pend[0]()
                    pend[0] = None
                if i > 0:
                    emit_elem(i - 1)
                    emit_PV(i - 1)
            def mk_tail(emit_elem=emit_elem, emit_PV=emit_PV, ntk=ntk, qb=qb, hp=hp):
                def tail():
                    emit_elem(ntk - 1)
                    emit_PV(ntk - 1)
                    E.emit("act", lambda s: s.activation(out=recrow[:], in_=PA[64:65, 0:1024], func=AF.Ln),
                           reads=[C.rPA[0], C.rPA[1]], writes=[r_rec])
                    E.emit("act", lambda s: s.activation(out=recrow[:], in_=recrow[:], func=AF.Exp,
                                                         scale=CT.f32(-1.0)),
                           reads=[r_rec, C.r_const], writes=[r_rec])
                    for hh in range(2):
                        E.emit("pe", lambda t, hh=hh: t.matmul(PL[:, 0:512], C.ones_row_f32[0:1, :],
                                                               recrow[0:1, hh * 512:(hh + 1) * 512],
                                                               start=True, stop=True),
                               reads=[r_rec, C.r_ones_row], writes=[C.rPL])
                        E.emit("dve", lambda v, hh=hh: v.tensor_copy(out=rb[:, hh * 512:(hh + 1) * 512],
                                                                     in_=PL[:, 0:512]),
                               reads=[C.rPL], writes=[r_rb])
                    E.emit("dve", lambda v, hp=hp, qb=qb: v.tensor_tensor(
                               out=attnO[0:64, hp * T + qb * 512: hp * T + (qb + 1) * 512],
                               in0=PA[0:64, 0:512], in1=rb[0:64, 0:512], op=ALU.mult),
                           reads=[C.rPA[0], r_rb], writes=[r_attnO])
                    E.emit("dve", lambda v, hp=hp, qb=qb: v.tensor_tensor(
                               out=attnO[64:128, hp * T + qb * 512: hp * T + (qb + 1) * 512],
                               in0=PA[0:64, 512:1024], in1=rb[0:64, 512:1024], op=ALU.mult),
                           reads=[C.rPA[1], r_rb], writes=[r_attnO])
                return tail
            pend[0] = mk_tail()

    if pend[0] is not None:
        pend[0]()
        pend[0] = None
    load_w(C, W, "w_proj", wstage, r_wstage, slice(0, D), "act")
    ybuf, r_ybuf = C.slabA, C.r_slabA
    sq, r_sq = C.slabB, C.r_slabB
    for j in range(NT):
        tb, r_tb = C.tslab[j % 2], C.r_tslab[j % 2]
        E.dma("sync", lambda s, j=j: s.dma_start(out=sb_slab(tb[:]), in_=ap_slab(tT_hbm, j)),
              reads=[r_tT], writes=[r_tb])
        for dc in range(NCH):
            h = dc % 2
            E.emit("pe", lambda t, dc=dc, j=j, h=h: [t.matmul(
                       PC[:, h * 512:(h + 1) * 512],
                       wstage[:, c * D + dc * 128: c * D + (dc + 1) * 128],
                       attnO[:, c * T + j * 512: c * T + (j + 1) * 512],
                       start=(c == 0), stop=(c == NCH - 1)) for c in range(NCH)],
                   reads=[r_wstage, r_attnO], writes=[C.rPC[h]])
            E.emit("dve", lambda v, dc=dc, h=h, tb=tb: v.scalar_tensor_tensor(
                       out=ybuf[:, dc * 512:(dc + 1) * 512], in0=PC[:, h * 512:(h + 1) * 512],
                       scalar=W["bpc"][:, dc:dc + 1], in1=tb[:, dc * 512:(dc + 1) * 512],
                       op0=ALU.add, op1=ALU.add),
                   reads=[C.rPC[h], W["r_bpc"], r_tb], writes=[r_ybuf])
        emit_lnstats(C, ybuf, r_ybuf, 1e-5)
        for dc in range(NCH):
            sl = slice(dc * 512, (dc + 1) * 512)
            E.emit("dve", lambda v, sl=sl: v.tensor_tensor(out=sq[:, sl], in0=ybuf[:, sl],
                                                           in1=PB[:, 0:512], op=ALU.subtract),
                   reads=[r_ybuf, C.rPB[0]], writes=[r_sq])
            E.emit("dve", lambda v, sl=sl: v.tensor_tensor(out=sq[:, sl], in0=sq[:, sl],
                                                           in1=PB[:, 512:1024], op=ALU.mult),
                   reads=[r_sq, C.rPB[1]], writes=[r_sq])
            E.emit("dve", lambda v, sl=sl, dc=dc: v.tensor_scalar(
                       out=sq[:, sl], in0=sq[:, sl], scalar1=W["gammac"][:, dc:dc + 1],
                       scalar2=W["betac"][:, dc:dc + 1], op0=ALU.mult, op1=ALU.add),
                   reads=[r_sq, W["r_gammac"], W["r_betac"]], writes=[r_sq])
        E.dma("sync", lambda s, j=j: s.dma_start(out=sb_slab(ybuf[:]), in_=ap_slab(xT_in, j)),
              reads=[r_xin], writes=[r_ybuf])
        for dc in range(NCH):
            sl = slice(dc * 512, (dc + 1) * 512)
            E.emit("dve", lambda v, sl=sl: v.tensor_tensor(out=ybuf[:, sl], in0=sq[:, sl],
                                                           in1=ybuf[:, sl], op=ALU.add),
                   reads=[r_sq, r_ybuf], writes=[r_ybuf])
        E.dma("pool", lambda g, j=j: g.dma_start(out=ap_slab(xT_out, j), in_=sb_slab(ybuf[:])),
              reads=[r_ybuf], writes=[r_xout], dest=r_xout)
        emit_ep_ln(C, ybuf, r_ybuf, j, tT_next, r_tTn)


def emit_moe(C, xT_in, r_xin, xT_out, r_xout, tT_hbm, r_tT, W, tag, tT_next=None, r_tTn=None, skip_ln=False):
    E, nc, CT, T, NT = C.E, C.nc, C.CT, C.T, C.NT
    M, NM = 1408, 11
    NE = 8
    PA, PB, PC, PL = C.PA, C.PB, C.PC, C.PL
    if not skip_ln:
        emit_ln_to_tT(C, xT_in, r_xin, tT_hbm, r_tT)

    wgate = C.sb(tag + "wgate", [128, NCH * 8], BF16); r_wgate = E.dma_res(tag + "wgate")
    load_weight_lhsT(C, W["w_gate"], W["r_w_gate"], wgate, r_wgate, slice(0, 8), "pool")
    NTT = T // 128
    gbuf = C.sb(tag + "gbuf", [128, NTT * 8], F32); r_gbuf = E.res(tag + "gbuf")
    gT = C.sb(tag + "gT", [8, T], BF16); r_gT = E.res(tag + "gT")
    selmat = C.sb(tag + "sel", [8, 8 * 128], BF16); r_sel = E.dma_res(tag + "sel")
    sel_in = C.nc.dram_tensor(tag + "sel_in", [8, 8 * 128], F32, kind="ExternalInput").ap()
    E.dma("pool", lambda g: g.dma_start(out=selmat[:], in_=sel_in), writes=[r_sel])
    selh = np.zeros((8, 8, 128), np.float32)
    for e in range(8):
        selh[e, e, :] = 1.0
    C.host_consts[tag + "sel_in"] = np.ascontiguousarray(selh.transpose(1, 0, 2).reshape(8, 8 * 128))
    grow = C.sb(tag + "grow", [128, 8 * 4], F32); r_grow = E.res(tag + "grow")
    g_bc = C.sb(tag + "g_bc", [128, 512], F32); r_gbc = E.res(tag + "g_bc")
    for j in range(NT):
        tb, r_tb = C.tslab[j % 2], C.r_tslab[j % 2]
        E.dma("sync", lambda s, j=j: s.dma_start(out=sb_slab(tb[:]), in_=ap_slab(tT_hbm, j)),
              reads=[r_tT], writes=[r_tb])
        for k in range(4):
            tt = j * 4 + k
            E.emit("pe", lambda t, tb=tb, k=k: [t.matmul(
                       PL[:, 0:8], tb[:, c * 512 + k * 128: c * 512 + (k + 1) * 128],
                       wgate[:, c * 8:(c + 1) * 8],
                       start=(c == 0), stop=(c == NCH - 1)) for c in range(NCH)],
                   reads=[r_tb, r_wgate], writes=[C.rPL])
            sc = gbuf[:, tt * 8:(tt + 1) * 8]
            E.emit("dve", lambda v, sc=sc: v.tensor_copy(out=sc, in_=PL[:, 0:8]),
                   reads=[C.rPL], writes=[r_gbuf])
            m1 = grow[:, 0:1]; m2 = grow[:, 1:2]; p1 = grow[:, 2:3]; p2 = grow[:, 3:4]
            mask1 = grow[:, 8:16]; sp = grow[:, 16:24]
            E.emit("dve", lambda v, sc=sc, m1=m1: v.tensor_reduce(out=m1, in_=sc, axis=AX.X, op=ALU.max),
                   reads=[r_gbuf], writes=[r_grow])
            E.emit("dve", lambda v, sc=sc: v.tensor_scalar(out=mask1, in0=sc, scalar1=m1,
                                                           scalar2=None, op0=ALU.is_equal),
                   reads=[r_gbuf, r_grow], writes=[r_grow])
            E.emit("dve", lambda v, sc=sc: v.scalar_tensor_tensor(out=sp, in0=mask1, scalar=-1e30,
                                                                  in1=sc, op0=ALU.mult, op1=ALU.add),
                   reads=[r_grow, r_gbuf], writes=[r_grow])
            E.emit("dve", lambda v: v.tensor_reduce(out=m2, in_=sp, axis=AX.X, op=ALU.max),
                   reads=[r_grow], writes=[r_grow])
            E.emit("dve", lambda v: v.tensor_tensor(out=grow[:, 4:5], in0=m1, in1=m2, op=ALU.subtract),
                   reads=[r_grow], writes=[r_grow])
            E.emit("act", lambda s: s.activation(out=p1, in_=grow[:, 4:5], func=AF.Sigmoid),
                   reads=[r_grow], writes=[r_grow])
            E.emit("act", lambda s: s.activation(out=p2, in_=grow[:, 4:5], func=AF.Sigmoid,
                                                 scale=CT.f32(-1.0)),
                   reads=[r_grow, C.r_const], writes=[r_grow])
            E.emit("dve", lambda v: v.tensor_scalar(out=sp, in0=sp, scalar1=m2, scalar2=None,
                                                    op0=ALU.is_equal),
                   reads=[r_grow], writes=[r_grow])
            E.emit("dve", lambda v: v.tensor_scalar(out=mask1, in0=mask1, scalar1=p1, scalar2=None,
                                                    op0=ALU.mult),
                   reads=[r_grow], writes=[r_grow])
            E.emit("dve", lambda v, sc=sc: v.scalar_tensor_tensor(out=sc, in0=sp, scalar=p2,
                                                                  in1=mask1, op0=ALU.mult, op1=ALU.add),
                   reads=[r_grow], writes=[r_gbuf])
            E.emit("pe", lambda t, sc=sc: t.transpose(PL[0:8, 0:128], sc, C.ident_f32[:]),
                   reads=[r_gbuf, C.r_ident], writes=[C.rPL])
            E.emit("dve", lambda v, tt=tt: v.tensor_copy(out=gT[:, tt * 128:(tt + 1) * 128],
                                                         in_=PL[0:8, 0:128]),
                   reads=[C.rPL], writes=[r_gT])

    use_bf = "w_g_bf" in W
    wg = C.sb(tag + "wg", [128, NCH * M], BF16); r_wg = E.dma_res(tag + "wg")
    wu = C.sb(tag + "wu", [128, NCH * M], BF16); r_wu = E.dma_res(tag + "wu")
    wd = C.sb(tag + "wd", [128, NM * D], BF16); r_wd = E.dma_res(tag + "wd")
    hbuf = C.sb(tag + "hbuf", [128, NM * 512], BF16); r_hbuf = E.res(tag + "hbuf")
    sg = C.sb(tag + "sg", [128, 512], BF16); r_sg = E.res(tag + "sg")
    tmp = C.sb(tag + "tmp", [128, 512], BF16); r_tmp = E.res(tag + "tmp")
    HTOK = max(512, T // 4)
    NHALF = T // HTOK
    NJ = HTOK // 512
    yacc = C.sb(tag + "yacc", [128, NCH * HTOK], F32); r_yacc = E.dma_res(tag + "yacc")

    for H in range(NHALF):
        src = xT_in[:, H * HTOK:(H + 1) * HTOK].rearrange("(c p) t -> p c t", p=128)
        E.dma("sync", lambda s: s.dma_start(out=yacc[:, 0:NCH * HTOK].rearrange("p (c t) -> p c t", t=HTOK),
                                            in_=src),
              reads=[r_xin], writes=[r_yacc])
        for e in range(NE):
            if use_bf:
                E.dma("act", lambda g, e=e: g.dma_start(
                          out=wg[:].rearrange("p (c m) -> p c m", m=M),
                          in_=W["w_g_bf"][e].rearrange("(c p) m -> p c m", p=128)),
                      reads=[W["r_w_g_bf"]], writes=[r_wg])
                E.dma("sync", lambda g, e=e: g.dma_start(
                          out=wu[:].rearrange("p (c m) -> p c m", m=M),
                          in_=W["w_u_bf"][e].rearrange("(c p) m -> p c m", p=128)),
                      reads=[W["r_w_u_bf"]], writes=[r_wu])
                E.dma("act", lambda g, e=e: g.dma_start(
                          out=wd[:].rearrange("p (c m) -> p c m", m=D),
                          in_=W["w_d_bf"][e].rearrange("(c p) m -> p c m", p=128)),
                      reads=[W["r_w_d_bf"]], writes=[r_wd])
            else:
                E.dma("pool", lambda g, e=e: g.dma_start(
                          out=wg[:].rearrange("p (c m) -> p c m", m=M),
                          in_=W["w_g"][e].rearrange("(c p) m -> p c m", p=128)),
                      reads=[W["r_w_g"]], writes=[r_wg])
                E.dma("pool", lambda g, e=e: g.dma_start(
                          out=wu[:].rearrange("p (c m) -> p c m", m=M),
                          in_=W["w_u"][e].rearrange("(c p) m -> p c m", p=128)),
                      reads=[W["r_w_u"]], writes=[r_wu])
                E.dma("pool", lambda g, e=e: g.dma_start(
                          out=wd[:].rearrange("p (c m) -> p c m", m=D),
                          in_=W["w_d"][e].rearrange("(c p) m -> p c m", p=128)),
                      reads=[W["r_w_d"]], writes=[r_wd])
            for j in range(NJ):
                jj = H * NJ + j
                tb, r_tb = C.tslab[jj % 2], C.r_tslab[jj % 2]
                E.dma("sync", lambda s, jj=jj: s.dma_start(out=sb_slab(tb[:]), in_=ap_slab(tT_hbm, jj)),
                      reads=[r_tT], writes=[r_tb])
                E.emit("pe", lambda t, e=e, jj=jj: t.matmul(
                           PL[:, 0:512], selmat[:, e * 128:(e + 1) * 128],
                           gT[0:8, jj * 512:(jj + 1) * 512],
                           start=True, stop=True),
                       reads=[r_gT, r_sel], writes=[C.rPL])
                E.emit("dve", lambda v: v.tensor_copy(out=g_bc[:], in_=PL[:, 0:512]),
                       reads=[C.rPL], writes=[r_gbc])
                for mi in range(NM):
                    Ps = PA if mi % 2 == 0 else PC
                    rPs = C.rPA if mi % 2 == 0 else C.rPC
                    E.emit("pe", lambda t, mi=mi, tb=tb, Ps=Ps: [t.matmul(
                               Ps[:, 0:512], wg[:, c * M + mi * 128: c * M + (mi + 1) * 128],
                               tb[:, c * 512:(c + 1) * 512],
                               start=(c == 0), stop=(c == NCH - 1)) for c in range(NCH)],
                           reads=[r_wg, r_tb], writes=[rPs[0]])
                    E.emit("pe", lambda t, mi=mi, tb=tb, Ps=Ps: [t.matmul(
                               Ps[:, 512:1024], wu[:, c * M + mi * 128: c * M + (mi + 1) * 128],
                               tb[:, c * 512:(c + 1) * 512],
                               start=(c == 0), stop=(c == NCH - 1)) for c in range(NCH)],
                           reads=[r_wu, r_tb], writes=[rPs[1]])
                    E.emit("act", lambda s, Ps=Ps: s.activation(out=sg[:], in_=Ps[:, 0:512], func=AF.Silu),
                           reads=[rPs[0]], writes=[r_sg])
                    E.emit("dve", lambda v, Ps=Ps: v.tensor_tensor(out=tmp[:], in0=Ps[:, 512:1024],
                                                            in1=g_bc[:], op=ALU.mult),
                           reads=[rPs[1], r_gbc], writes=[r_tmp])
                    E.emit("dve", lambda v, mi=mi: v.tensor_tensor(
                               out=hbuf[:, mi * 512:(mi + 1) * 512], in0=sg[:], in1=tmp[:], op=ALU.mult),
                           reads=[r_sg, r_tmp], writes=[r_hbuf])
                for dc in range(NCH):
                    h2 = dc % 2
                    E.emit("pe", lambda t, dc=dc, h2=h2: [t.matmul(
                               PB[:, h2 * 512:(h2 + 1) * 512],
                               wd[:, mi * D + dc * 128: mi * D + (dc + 1) * 128],
                               hbuf[:, mi * 512:(mi + 1) * 512],
                               start=(mi == 0), stop=(mi == NM - 1)) for mi in range(NM)],
                           reads=[r_wd, r_hbuf], writes=[C.rPB[h2]])
                    ysl = slice(dc * HTOK + j * 512, dc * HTOK + (j + 1) * 512)
                    E.emit("dve", lambda v, ysl=ysl, h2=h2: v.tensor_tensor(
                               out=yacc[:, ysl], in0=yacc[:, ysl], in1=PB[:, h2 * 512:(h2 + 1) * 512],
                               op=ALU.add),
                           reads=[C.rPB[h2]], writes=[r_yacc])
        dst = xT_out[:, H * HTOK:(H + 1) * HTOK].rearrange("(c p) t -> p c t", p=128)
        E.dma("pool", lambda g: g.dma_start(out=dst,
                                            in_=yacc[:, 0:NCH * HTOK].rearrange("p (c t) -> p c t", t=HTOK)),
              reads=[r_yacc], writes=[r_xout], dest=r_xout)
        for js in range(NJ):
            chunks = [yacc[:, c * HTOK + js * 512: c * HTOK + (js + 1) * 512] for c in range(NCH)]
            emit_ep_ln(C, yacc, r_yacc, H * NJ + js, tT_next, r_tTn, chunks=chunks)


def emit_attn_b0(C, xT_in, r_xin, xT_out, r_xout, tT_hbm, r_tT, W, tT_next=None, r_tTn=None, skip_ln=False):
    E, nc, CT, T, NT = C.E, C.nc, C.CT, C.T, C.NT
    NQ = T // 512
    NTK = T // 128
    SHIFT = 50.0
    EPS = 1.1920929e-07
    PA, PB, PC, PL, PT = C.PA, C.PB, C.PC, C.PL, C.PT
    C.ensure_masks()
    C.ensure_hsel()
    for nm in ("w_qT", "w_kT", "w_vT", "w_oT"):
        precast(C, W, nm)
    if not skip_ln:
        emit_ln_to_tT(C, xT_in, r_xin, tT_hbm, r_tT)

    QT2 = [C.sb(f"ab_QT{i}", [128, T], BF16) for i in range(2)]
    r_QT2 = [E.res(f"ab_QT{i}") for i in range(2)]
    KT2 = [C.sb(f"ab_KT{i}", [128, T], BF16) for i in range(2)]
    r_KT2 = [E.res(f"ab_KT{i}") for i in range(2)]
    Vp2 = [C.sb(f"ab_Vp{i}", [128, NTK * 130], BF16) for i in range(2)]
    r_Vp2 = [E.res(f"ab_Vp{i}") for i in range(2)]
    VTs = C.sb("ab_VTs", [128, 512], BF16); r_VTs = E.res("ab_VTs")
    qraw = C.sb("ab_qraw", [128, 512], BF16); r_qraw = E.res("ab_qraw")
    Pt = [C.sb(f"ab_P{i}", [128, 1024], BF16) for i in range(2)]
    r_Pt = [E.res(f"ab_P{i}") for i in range(2)]
    th = C.sb("ab_th", [128, 1024], F32); r_th = E.res("ab_th")
    th2 = C.sb("ab_th2", [128, 1024], F32); r_th2 = E.res("ab_th2")
    aO_hbm = nc.dram_tensor("ab_aO_hbm", [D, T], BF16, kind="Internal").ap()
    r_aOh = E.dma_res("ab_aOh")
    aOs = C.sb("ab_aOs", [128, 512], BF16); r_aOs = E.res("ab_aOs")
    recrow = C.sb("ab_recrow", [1, 1024], F32); r_rec = E.res("ab_recrow")
    rb = C.sb("ab_rb", [128, 1024], F32); r_rb = E.res("ab_rb")
    rmsq = C.sb("ab_rmsq", [128, 512], F32); r_rmsq = E.res("ab_rmsq")
    rmsr = C.sb("ab_rmsr", [2, 512], F32); r_rmsr = E.res("ab_rmsr")
    qksum2 = [C.sb(f"ab_qksum{i}", [66, T], F32) for i in range(2)]
    r_qksum2 = [E.res(f"ab_qksum{i}") for i in range(2)]
    for _p in range(2):
        E.emit("dve", lambda v, _p=_p: v.memset(qksum2[_p][:], 1.0), writes=[r_qksum2[_p]])
    rmsbc = C.sb("ab_rmsbc", [128, 512], F32); r_rmsbc = E.res("ab_rmsbc")
    wstage = C.sb("wstage", [128, NCH * D], BF16)
    r_ws = [E.dma_res("ab_ws0"), E.dma_res("ab_ws1")]

    for par in range(2):
        for i in range(NTK):
            E.emit("dve", lambda v, i=i, par=par: v.memset(Vp2[par][:, i * 130 + 64:i * 130 + 65], 1.0),
                   writes=[r_Vp2[par]])
            E.emit("dve", lambda v, i=i, par=par: v.memset(Vp2[par][:, i * 130 + 129:i * 130 + 130], 1.0),
                   writes=[r_Vp2[par]])

    def prep_gen(hp):
        par = hp % 2
        c0 = hp * 128
        ws = wstage[:, par * NCH * 512:(par + 1) * NCH * 512]
        r_w = r_ws[par]
        QT, KT, Vp = QT2[par], KT2[par], Vp2[par]
        r_QT, r_KT, r_Vp = r_QT2[par], r_KT2[par], r_Vp2[par]
        wq = ws[:, 0:NCH * 128]
        wk = ws[:, NCH * 128:2 * NCH * 128]
        wv = ws[:, 2 * NCH * 128:3 * NCH * 128]
        for nm, dsts in (("w_qT", wq), ("w_kT", wk), ("w_vT", wv)):
            bf = W[nm + "_bf"]
            sap = bf[:, c0:c0 + 128].rearrange("(c p) m -> p c m", p=128)
            E.dma("act", lambda g, dsts=dsts, sap=sap: g.dma_start(
                      out=dsts.rearrange("p (c m) -> p c m", m=128), in_=sap),
                  reads=[W["r_" + nm + "_bf"]], writes=[r_w], dest=r_w)
        for j in range(NT):
            tb, r_tb = C.tslab[j % 2], C.r_tslab[j % 2]
            E.dma("sync", lambda s, j=j: s.dma_start(out=sb_slab(tb[:]), in_=ap_slab(tT_hbm, j)),
                  reads=[r_tT], writes=[r_tb])
            for wi, (wsb, dst, r_dst, sro) in enumerate((
                    (wq, QT, r_QT, 0), (wk, KT, r_KT, 64))):
                smb, r_smb = qksum2[par], r_qksum2[par]
                E.emit("pe", lambda t, wsb=wsb, tb=tb: [t.matmul(
                           PL[:, 0:512], wsb[:, c * 128:(c + 1) * 128],
                           tb[:, c * 512:(c + 1) * 512],
                           start=(c == 0), stop=(c == NCH - 1)) for c in range(NCH)],
                       reads=[r_w, r_tb], writes=[C.rPL])
                E.emit("dve", lambda v, dst=dst, j=j: v.tensor_copy(
                           out=dst[:, j * 512:(j + 1) * 512], in_=PL[:, 0:512]),
                       reads=[C.rPL], writes=[r_dst])
                E.emit("act", lambda s, dst=dst, j=j: s.activation(
                           out=rmsq[:], in_=dst[:, j * 512:(j + 1) * 512], func=AF.Square),
                       reads=[r_dst], writes=[r_rmsq])
                E.emit("pe", lambda t: t.matmul(PL[0:2, 0:512], C.hsel[:, 0:2], rmsq[:],
                                                start=True, stop=True),
                       reads=[r_rmsq, C.r_hsel], writes=[C.rPL])
                E.emit("dve", lambda v, smb=smb, j=j, sro=sro: v.tensor_copy(
                           out=smb[sro:sro + 2, j * 512:(j + 1) * 512], in_=PL[0:2, 0:512]),
                       reads=[C.rPL], writes=[r_smb])
                yield
            E.emit("pe", lambda t, tb=tb: [t.matmul(
                       PL[:, 0:512], wv[:, c * 128:(c + 1) * 128], tb[:, c * 512:(c + 1) * 512],
                       start=(c == 0), stop=(c == NCH - 1)) for c in range(NCH)],
                   reads=[r_w, r_tb], writes=[C.rPL])
            E.emit("dve", lambda v: v.tensor_copy(out=VTs[:], in_=PL[:, 0:512]),
                   reads=[C.rPL], writes=[r_VTs])
            for k in range(4):
                i = j * 4 + k
                E.emit("pe", lambda t, k=k: t.transpose(PT[:], VTs[:, k * 128:(k + 1) * 128], C.ident_bf[:]),
                       reads=[r_VTs, C.r_ident_b], writes=[C.rPT])
                E.emit("dve", lambda v, i=i, Vp=Vp: v.tensor_copy(out=Vp[:, i * 130:i * 130 + 64],
                                                           in_=PT[:, 0:64]),
                       reads=[C.rPT], writes=[r_Vp])
                E.emit("dve", lambda v, i=i, Vp=Vp: v.tensor_copy(out=Vp[:, i * 130 + 65:i * 130 + 129],
                                                           in_=PT[:, 64:128]),
                       reads=[C.rPT], writes=[r_Vp])
            yield
        smb, r_smb = qksum2[par], r_qksum2[par]
        E.emit("act", lambda s, smb=smb: s.activation(out=smb[:], in_=smb[:], func=AF.Ln,
                                                      scale=CT.f32(1.0 / 64.0), bias=CT.f32(EPS)),
               reads=[r_smb], writes=[r_smb])
        E.emit("act", lambda s, smb=smb: s.activation(out=smb[:], in_=smb[:], func=AF.Exp,
                                                      scale=CT.f32(-0.5)),
               reads=[r_smb], writes=[r_smb])
        yield
        for j in range(NT):
            for (dst, r_dst, sro) in ((QT, r_QT, 0), (KT, r_KT, 64)):
                smb, r_smb = qksum2[par], r_qksum2[par]
                E.emit("pe", lambda t, smb=smb, j=j, sro=sro: t.matmul(
                           PL[:, 0:512], C.hsel2[0:2, :] if sro == 0 else C.hsel2b[64:66, :],
                           smb[sro:sro + 2, j * 512:(j + 1) * 512],
                           start=True, stop=True),
                       reads=[r_smb, C.r_hsel], writes=[C.rPL])
                E.emit("dve", lambda v: v.tensor_copy(out=rmsbc[:], in_=PL[:, 0:512]),
                       reads=[C.rPL], writes=[r_rmsbc])
                E.emit("dve", lambda v, dst=dst, j=j: v.tensor_tensor(
                           out=dst[:, j * 512:(j + 1) * 512], in0=dst[:, j * 512:(j + 1) * 512],
                           in1=rmsbc[:], op=ALU.mult),
                       reads=[r_rmsbc], writes=[r_dst])
            yield

    def attention(hp, feeder):
        par = hp % 2
        QT, KT, Vp = QT2[par], KT2[par], Vp2[par]
        r_QT, r_KT, r_Vp = r_QT2[par], r_KT2[par], r_Vp2[par]
        for qb in range(NQ):
            ntk = (qb + 1) * 4
            qc = slice(qb * 512, (qb + 1) * 512)
            def emit_S(i, qc=qc):
                ib = i % 2
                Ps = PB if ib == 0 else PC
                rS = C.rPB if ib == 0 else C.rPC
                E.emit("pe", lambda t, i=i, Ps=Ps, qc=qc: [t.matmul(
                           Ps[:, 0:512], KT[0:64, i * 128:(i + 1) * 128], QT[0:64, qc],
                           start=True, stop=True, tile_position=(0, 0)),
                       t.matmul(
                           Ps[:, 512:1024], KT[64:128, i * 128:(i + 1) * 128], QT[64:128, qc],
                           start=True, stop=True, tile_position=(64, 0))],
                       reads=[r_KT, r_QT], writes=[rS[0], rS[1]])
            def emit_elem(i, qb=qb):
                ib = i % 2
                Ps = PB if ib == 0 else PC
                rS = C.rPB if ib == 0 else C.rPC
                thb = th if ib == 0 else th2
                r_thb = r_th if ib == 0 else r_th2
                E.emit("act", lambda s, Ps=Ps, thb=thb: s.activation(
                           out=thb[:], in_=Ps[:, 0:1024], func=AF.Tanh, scale=CT.f32(1.0 / 400.0)),
                       reads=[rS[0], rS[1], C.r_const], writes=[r_thb])
                E.emit("act", lambda s, ib=ib, thb=thb: s.activation(
                           out=Pt[ib][:], in_=thb[:], func=AF.Exp,
                           scale=CT.f32(50.0), bias=CT.f32(-SHIFT)),
                       reads=[r_thb, C.r_const], writes=[r_Pt[ib]])
                coff = i * 128 - qb * 512
                if coff >= 0:
                    ci = coff // 128
                    for half in range(2):
                        E.emit("dve", lambda v, ib=ib, ci=ci, half=half: v.tensor_tensor(
                                   out=Pt[ib][:, half * 512:(half + 1) * 512],
                                   in0=Pt[ib][:, half * 512:(half + 1) * 512],
                                   in1=C.masks[:, ci * 512:(ci + 1) * 512], op=ALU.mult),
                               reads=[C.r_masks], writes=[r_Pt[ib]])
            def emit_PV(i, ntk=ntk):
                ib = i % 2
                E.emit("pe", lambda t, i=i, ib=ib, ntk=ntk: t.matmul(
                           PA[0:65, 0:512], Vp[:, i * 130:i * 130 + 65], Pt[ib][:, 0:512],
                           start=(i == 0), stop=(i == ntk - 1)),
                       reads=[r_Vp, r_Pt[ib]], writes=[C.rPA[0]])
                E.emit("pe", lambda t, i=i, ib=ib, ntk=ntk: t.matmul(
                           PA[0:65, 512:1024], Vp[:, i * 130 + 65:i * 130 + 130], Pt[ib][:, 512:1024],
                           start=(i == 0), stop=(i == ntk - 1)),
                       reads=[r_Vp, r_Pt[ib]], writes=[C.rPA[1]])
            for i in range(ntk):
                emit_S(i)
                if i > 0:
                    emit_elem(i - 1)
                    emit_PV(i - 1)
                if i % 2 == 0:
                    feeder()
            emit_elem(ntk - 1)
            emit_PV(ntk - 1)
            E.emit("dve", lambda v: v.reciprocal(out=recrow[:], in_=PA[64:65, 0:1024]),
                   reads=[C.rPA[0], C.rPA[1]], writes=[r_rec])
            E.emit("pe", lambda t: t.matmul(PB[:, 0:512], C.ones_row_f32[0:1, :], recrow[0:1, 0:512],
                                            start=True, stop=True),
                   reads=[r_rec, C.r_ones_row], writes=[C.rPB[0]])
            E.emit("pe", lambda t: t.matmul(PB[:, 512:1024], C.ones_row_f32[0:1, :], recrow[0:1, 512:1024],
                                            start=True, stop=True),
                   reads=[r_rec, C.r_ones_row], writes=[C.rPB[1]])
            E.emit("dve", lambda v: v.tensor_copy(out=rb[:], in_=PB[:, 0:1024]),
                   reads=[C.rPB[0], C.rPB[1]], writes=[r_rb])
            E.emit("dve", lambda v: v.tensor_tensor(
                       out=aOs[0:64, :], in0=PA[0:64, 0:512], in1=rb[0:64, 0:512], op=ALU.mult),
                   reads=[C.rPA[0], r_rb], writes=[r_aOs])
            E.emit("dve", lambda v: v.tensor_tensor(
                       out=aOs[64:128, :], in0=PA[0:64, 512:1024], in1=rb[0:64, 512:1024], op=ALU.mult),
                   reads=[C.rPA[1], r_rb], writes=[r_aOs])
            E.dma("pool", lambda g, hp=hp, qb=qb: g.dma_start(
                      out=aO_hbm[hp * 128:(hp + 1) * 128, qb * 512:(qb + 1) * 512], in_=aOs[:]),
                  reads=[r_aOs], writes=[r_aOh], dest=r_aOh)

    g = prep_gen(0)
    for _ in g:
        pass
    for hp in range(NH // 2):
        gnext = prep_gen(hp + 1) if hp + 1 < NH // 2 else None
        state = {"cnt": 0}
        def feeder():
            state["cnt"] += 1
            if gnext is not None and state["cnt"] % 3 == 0:
                next(gnext, None)
        attention(hp, feeder)
        if gnext is not None:
            for _ in gnext:
                pass

    load_w(C, W, "w_oT", wstage, r_ws[0], slice(0, D), "act")
    r_ws[0].readers.extend(r_ws[1].readers)
    ybuf, r_ybuf = C.slabA, C.r_slabA
    for j in range(NT):
        E.dma("sync", lambda s, j=j: s.dma_start(out=sb_slab(ybuf[:]), in_=ap_slab(xT_in, j)),
              reads=[r_xin], writes=[r_ybuf])
        aos2, r_aos2 = C.tslab[j % 2], C.r_tslab[j % 2]
        E.dma("sync", lambda s, j=j, aos2=aos2: s.dma_start(out=sb_slab(aos2[:]), in_=ap_slab(aO_hbm, j)),
              reads=[r_aOh], writes=[r_aos2])
        for dc in range(NCH):
            h = dc % 2
            E.emit("pe", lambda t, dc=dc, h=h, aos2=aos2: [t.matmul(
                       PC[:, h * 512:(h + 1) * 512],
                       wstage[:, c * D + dc * 128: c * D + (dc + 1) * 128],
                       aos2[:, c * 512:(c + 1) * 512],
                       start=(c == 0), stop=(c == NCH - 1)) for c in range(NCH)],
                   reads=[r_ws[0], r_aos2], writes=[C.rPC[h]])
            E.emit("dve", lambda v, dc=dc, h=h: v.tensor_tensor(
                       out=ybuf[:, dc * 512:(dc + 1) * 512], in0=ybuf[:, dc * 512:(dc + 1) * 512],
                       in1=PC[:, h * 512:(h + 1) * 512], op=ALU.add),
                   reads=[C.rPC[h]], writes=[r_ybuf])
        E.dma("pool", lambda g, j=j: g.dma_start(out=ap_slab(xT_out, j), in_=sb_slab(ybuf[:])),
              reads=[r_ybuf], writes=[r_xout], dest=r_xout)
        emit_ep_ln(C, ybuf, r_ybuf, j, tT_next, r_tTn)


def emit_attn_a1(C, xT_in, r_xin, xT_out, r_xout, tT_hbm, r_tT, W, tT_next=None, r_tTn=None, skip_ln=False):
    E, nc, CT, T, NT = C.E, C.nc, C.CT, C.T, C.NT
    NQ = T // 512
    NTK = T // 128
    EPS = 1e-4
    PA, PB, PC, PL, PT = C.PA, C.PB, C.PC, C.PL, C.PT
    C.ensure_hsel()
    precast(C, W, "W_in")
    precast(C, W, "W_out")
    if not skip_ln:
        emit_ln_to_tT(C, xT_in, r_xin, tT_hbm, r_tT)

    KTp2 = [C.sb(f"a1_KTp{i}", [128, T], BF16) for i in range(2)]
    r_KTp2 = [E.res(f"a1_KTp{i}") for i in range(2)]
    QTp2 = [C.sb(f"a1_QTp{i}", [128, T], BF16) for i in range(2)]
    r_QTp2 = [E.res(f"a1_QTp{i}") for i in range(2)]
    G_hbm = nc.dram_tensor("a1_G_hbm", [D, T], BF16, kind="Internal").ap()
    r_Gh = E.dma_res("a1_Gh")
    gsl2 = [C.sb(f"a1_gsl{i}", [128, 512], BF16) for i in range(2)]
    r_gsl2 = [E.dma_res(f"a1_gsl{i}") for i in range(2)]
    Vp2 = [C.sb(f"a1_Vp{i}", [128, NTK * 130], BF16) for i in range(2)]
    r_Vp2 = [E.res(f"a1_Vp{i}") for i in range(2)]
    q2A2 = [C.sb(f"a1_q2A{i}", [1, T], BF16) for i in range(2)]
    r_q2A2 = [E.res(f"a1_q2A{i}") for i in range(2)]
    q2B2 = [C.sb(f"a1_q2B{i}", [1, T], BF16) for i in range(2)]
    r_q2B2 = [E.res(f"a1_q2B{i}") for i in range(2)]
    k2m2 = [C.sb(f"a1_k2m{i}", [128, 2 * (T // 128)], F32) for i in range(2)]
    r_k2m2 = [E.res(f"a1_k2m{i}") for i in range(2)]
    q2bc = C.sb("a1_q2bc", [128, 1024], BF16); r_q2bc = E.res("a1_q2bc")
    ktok = C.sb("a1_ktok", [128, 128], BF16); r_ktok = E.res("a1_ktok")
    k2sq = C.sb("a1_k2sq", [128, 128], BF16); r_k2sq = E.res("a1_k2sq")
    qkslab = C.sb("a1_qks", [128, 512], BF16); r_qks = E.res("a1_qks")
    sqs = C.sb("a1_sqs", [128, 512], F32); r_sqs = E.res("a1_sqs")
    HO_hbm = nc.dram_tensor("a1_HO_hbm", [D, T], BF16, kind="Internal").ap()
    r_HOh = E.dma_res("a1_HOh")
    HO = C.sb("a1_HO", [128, 512], BF16); r_HO = E.res("a1_HO")

    Pt = [C.sb(f"a1_P{i}", [128, 2048], BF16) for i in range(2)]
    r_Pt = [E.res(f"a1_P{i}") for i in range(2)]
    d2c = [C.sb(f"a1_d2c{i}", [128, 2048], BF16) for i in range(2)]
    r_d2c = [E.res(f"a1_d2c{i}") for i in range(2)]
    Lb = C.sb("a1_Lb", [128, 2048], F32); r_Lb = E.res("a1_Lb")
    Db = C.sb("a1_Db", [128, 512], BF16); r_Db = E.res("a1_Db")
    recrow = C.sb("a1_recrow", [1, 512], F32); r_rec = E.res("a1_recrow")
    rb = C.sb("a1_rb", [128, 1024], F32); r_rb = E.res("a1_rb")
    wstage = C.sb("wstage", [128, NCH * D], BF16)
    r_ws = [E.dma_res("a1_ws0"), E.dma_res("a1_ws1")]
    binc = C.sb("a1_binc", [128, 32], F32); r_binc = E.dma_res("a1_binc")
    boutc = C.sb("a1_boutc", [128, NCH], F32); r_boutc = E.dma_res("a1_boutc")
    binc_in = nc.dram_tensor("a1_binc_in", [128, 32], F32, kind="ExternalInput").ap()
    boutc_in = nc.dram_tensor("a1_boutc_in", [128, NCH], F32, kind="ExternalInput").ap()
    E.dma("sync", lambda s: s.dma_start(out=binc[:], in_=binc_in), writes=[r_binc])
    E.dma("sync", lambda s: s.dma_start(out=boutc[:], in_=boutc_in), writes=[r_boutc])

    for par in range(2):
        for i in range(NTK):
            E.emit("dve", lambda v, i=i, par=par: v.memset(Vp2[par][:, i * 130 + 64:i * 130 + 65], 1.0),
                   writes=[r_Vp2[par]])
            E.emit("dve", lambda v, i=i, par=par: v.memset(Vp2[par][:, i * 130 + 129:i * 130 + 130], 1.0),
                   writes=[r_Vp2[par]])

    W_in, r_Win = W["W_in"], W["r_W_in"]

    def prep_gen(hp):
        par = hp % 2
        ws = wstage[:, par * NCH * 512:(par + 1) * NCH * 512]
        r_w = r_ws[par]
        KTp, QTp, Vp = KTp2[par], QTp2[par], Vp2[par]
        r_KTp, r_QTp, r_Vp = r_KTp2[par], r_QTp2[par], r_Vp2[par]
        q2A, q2B, k2m = q2A2[par], q2B2[par], k2m2[par]
        r_q2A, r_q2B, r_k2m = r_q2A2[par], r_q2B2[par], r_k2m2[par]
        src_ap = W["W_in_bf"][:, 512 * hp:512 * hp + 512].rearrange("(c p) m -> p c m", p=128)
        E.dma("act", lambda g: g.dma_start(out=ws.rearrange("p (c m) -> p c m", m=512), in_=src_ap),
              reads=[W["r_W_in_bf"]], writes=[r_w])
        for j in range(NT):
            tb, r_tb = C.tslab[j % 2], C.r_tslab[j % 2]
            E.dma("sync", lambda s, j=j: s.dma_start(out=sb_slab(tb[:]), in_=ap_slab(tT_hbm, j)),
                  reads=[r_tT], writes=[r_tb])
            for sub in range(4):
                cg = 4 * hp + sub
                hb = 0 if sub < 2 else 64
                E.emit("pe", lambda t, tb=tb, sub=sub, ws=ws: [t.matmul(
                           PL[:, 0:512], ws[:, c * 512 + sub * 128: c * 512 + (sub + 1) * 128],
                           tb[:, c * 512:(c + 1) * 512],
                           start=(c == 0), stop=(c == NCH - 1)) for c in range(NCH)],
                       reads=[r_w, r_tb], writes=[C.rPL])
                E.emit("dve", lambda v, cg=cg: v.tensor_scalar(
                           out=qkslab[:], in0=PL[:, 0:512], scalar1=binc[:, cg:cg + 1],
                           scalar2=None, op0=ALU.add),
                       reads=[C.rPL, r_binc], writes=[r_qks])
                if sub % 2 == 0:
                    E.emit("dve", lambda v, hb=hb, j=j, QTp=QTp: v.tensor_copy(
                               out=QTp[hb:hb + 64, j * 512:(j + 1) * 512], in_=qkslab[0:64, :]),
                           reads=[r_qks], writes=[r_QTp])
                    E.emit("dve", lambda v, hb=hb, j=j, KTp=KTp: v.tensor_copy(
                               out=KTp[hb:hb + 64, j * 512:(j + 1) * 512], in_=qkslab[64:128, :]),
                           reads=[r_qks], writes=[r_KTp])
                    E.emit("act", lambda s: s.activation(out=sqs[0:64, :], in_=qkslab[0:64, :], func=AF.Square),
                           reads=[r_qks], writes=[r_sqs])
                    E.emit("pe", lambda t: t.matmul(PL[0:1, 0:512], C.hsel[0:64, 0:1], sqs[0:64, :],
                                                    start=True, stop=True),
                           reads=[r_sqs, C.r_hsel], writes=[C.rPL])
                    q2buf, r_q2x = (q2A, r_q2A) if hb == 0 else (q2B, r_q2B)
                    E.emit("dve", lambda v, j=j, q2buf=q2buf: v.tensor_scalar(
                               out=q2buf[0:1, j * 512:(j + 1) * 512], in0=PL[0:1, 0:512],
                               scalar1=-0.5, scalar2=None, op0=ALU.mult),
                           reads=[C.rPL], writes=[r_q2x])
                    for k in range(4):
                        i = j * 4 + k
                        kcol = (0 if hb == 0 else (T // 128)) + i
                        E.emit("pe", lambda t, k=k: t.transpose(PT[:, 0:64],
                                                                qkslab[64:128, k * 128:(k + 1) * 128],
                                                                C.ident_bf[64:128, 64:128]),
                               reads=[r_qks, C.r_ident_b], writes=[C.rPT])
                        E.emit("dve", lambda v: v.tensor_copy(out=ktok[:, 0:64], in_=PT[:, 0:64]),
                               reads=[C.rPT], writes=[r_ktok])
                        E.emit("dve", lambda v: v.scalar_tensor_tensor(out=k2sq[:, 0:64], in0=ktok[:, 0:64],
                                                                scalar=0.5, in1=ktok[:, 0:64],
                                                                op0=ALU.mult, op1=ALU.mult),
                               reads=[r_ktok], writes=[r_k2sq])
                        E.emit("dve", lambda v, kcol=kcol, k2m=k2m: v.tensor_reduce(
                                   out=k2m[:, kcol:kcol + 1],
                                   in_=k2sq[:, 0:64], axis=AX.X, op=ALU.add, negate=True),
                               reads=[r_k2sq], writes=[r_k2m])
                else:
                    E.dma("pool", lambda g, hb=hb, j=j, hp=hp: g.dma_start(
                              out=G_hbm[hp * 128 + hb:hp * 128 + hb + 64, j * 512:(j + 1) * 512],
                              in_=qkslab[64:128, :]),
                          reads=[r_qks], writes=[r_Gh], dest=r_Gh)
                    for k in range(4):
                        i = j * 4 + k
                        off = 0 if hb == 0 else 65
                        E.emit("pe", lambda t, k=k: t.transpose(PT[:, 0:64],
                                                                qkslab[0:64, k * 128:(k + 1) * 128],
                                                                C.ident_bf[0:64, 0:64]),
                               reads=[r_qks, C.r_ident_b], writes=[C.rPT])
                        E.emit("dve", lambda v, i=i, off=off, Vp=Vp: v.tensor_copy(
                                   out=Vp[:, i * 130 + off:i * 130 + off + 64], in_=PT[:, 0:64]),
                               reads=[C.rPT], writes=[r_Vp])
                yield

    def attention(hp, feeder):
        par = hp % 2
        KTp, QTp, Vp = KTp2[par], QTp2[par], Vp2[par]
        r_KTp, r_QTp, r_Vp = r_KTp2[par], r_QTp2[par], r_Vp2[par]
        q2A, q2B, k2m = q2A2[par], q2B2[par], k2m2[par]
        r_q2A, r_q2B, r_k2m = r_q2A2[par], r_q2B2[par], r_k2m2[par]
        NP = NTK // 2

        def q2head(qb):
            qc = slice(qb * 512, (qb + 1) * 512)
            gb, r_gb = gsl2[qb % 2], r_gsl2[qb % 2]
            E.dma("sync", lambda s, hp=hp, qb=qb, gb=gb: s.dma_start(
                      out=gb[:], in_=G_hbm[hp * 128:(hp + 1) * 128, qb * 512:(qb + 1) * 512]),
                  reads=[r_Gh], writes=[r_gb])
            E.emit("pe", lambda t, qc=qc: t.matmul(PL[:, 0:512], C.ones_row_bf[0:1, :], q2A[0:1, qc],
                                                   start=True, stop=True),
                   reads=[r_q2A, C.r_ones_row_b], writes=[C.rPL])
            E.emit("dve", lambda v: v.tensor_copy(out=q2bc[:, 0:512], in_=PL[:, 0:512]),
                   reads=[C.rPL], writes=[r_q2bc])
            E.emit("pe", lambda t, qc=qc: t.matmul(PL[:, 0:512], C.ones_row_bf[0:1, :], q2B[0:1, qc],
                                                   start=True, stop=True),
                   reads=[r_q2B, C.r_ones_row_b], writes=[C.rPL])
            E.emit("dve", lambda v: v.tensor_copy(out=q2bc[:, 512:1024], in_=PL[:, 0:512]),
                   reads=[C.rPL], writes=[r_q2bc])

        def emit_S(qb, i):
            qc = slice(qb * 512, (qb + 1) * 512)
            ib = i % 2
            Ps = PB if ib == 0 else PC
            rS = C.rPB if ib == 0 else C.rPC
            E.emit("pe", lambda t, i=i, Ps=Ps, qc=qc: [
                       t.matmul(Ps[:, 0:512], KTp[0:64, i * 128:(i + 1) * 128], QTp[0:64, qc],
                                start=True, stop=True, tile_position=(0, 0)),
                       t.matmul(Ps[:, 512:1024], KTp[64:128, i * 128:(i + 1) * 128], QTp[64:128, qc],
                                start=True, stop=True, tile_position=(64, 0))],
                   reads=[r_KTp, r_QTp], writes=[rS[0], rS[1]])

        def emit_clamp(n, i):
            ib = i % 2
            ip = n % 2
            quarter = (i % 2) * 1024
            Ps = PB if ib == 0 else PC
            rS = C.rPB if ib == 0 else C.rPC
            for half in range(2):
                kcol = (0 if half == 0 else (T // 128)) + i
                E.emit("dve", lambda v, Ps=Ps, half=half, kcol=kcol:
                           v.scalar_tensor_tensor(
                           out=Db[:, 0:512],
                           in0=Ps[:, half * 512:(half + 1) * 512], scalar=k2m[:, kcol:kcol + 1],
                           in1=q2bc[:, half * 512:(half + 1) * 512], op0=ALU.add, op1=ALU.add),
                       reads=[rS[half], r_k2m, r_q2bc], writes=[r_Db])
                E.emit("dve", lambda v, half=half, ip=ip, quarter=quarter: v.tensor_scalar(
                           out=d2c[ip][:, quarter + half * 512: quarter + (half + 1) * 512],
                           in0=Db[:, 0:512],
                           scalar1=-2.0, scalar2=0.0, op0=ALU.mult, op1=ALU.max),
                       reads=[r_Db], writes=[r_d2c[ip]])

        def emit_chain(n):
            ip = n % 2
            E.emit("act", lambda s, ip=ip: s.activation(out=Lb[:], in_=d2c[ip][:], func=AF.Ln),
                   reads=[r_d2c[ip]], writes=[r_Lb])
            E.emit("act", lambda s, ip=ip: s.activation(out=d2c[ip][:], in_=Lb[:], func=AF.Exp,
                                                        scale=CT.f32(0.5)),
                   reads=[r_Lb, C.r_const], writes=[r_d2c[ip]])
            E.emit("act", lambda s, ip=ip: s.activation(out=Lb[:], in_=d2c[ip][:], func=AF.Ln,
                                                        bias=CT.f32(EPS)),
                   reads=[r_d2c[ip], C.r_const], writes=[r_Lb])
            E.emit("act", lambda s, ip=ip: s.activation(out=Pt[ip][:], in_=Lb[:], func=AF.Exp,
                                                        scale=CT.f32(-2.0)),
                   reads=[r_Lb, C.r_const], writes=[r_Pt[ip]])

        def emit_PVpair(n):
            ip = n % 2
            for k in range(2):
                i = (n % NP) * 2 + k
                quarter = k * 1024
                E.emit("pe", lambda t, i=i, ip=ip, quarter=quarter: t.matmul(
                           PA[0:65, 0:512], Vp[:, i * 130:i * 130 + 65],
                           Pt[ip][:, quarter:quarter + 512],
                           start=(i == 0), stop=(i == NTK - 1)),
                       reads=[r_Vp, r_Pt[ip]], writes=[C.rPA[0]])
                E.emit("pe", lambda t, i=i, ip=ip, quarter=quarter: t.matmul(
                           PA[0:65, 512:1024], Vp[:, i * 130 + 65:i * 130 + 130],
                           Pt[ip][:, quarter + 512:quarter + 1024],
                           start=(i == 0), stop=(i == NTK - 1)),
                       reads=[r_Vp, r_Pt[ip]], writes=[C.rPA[1]])

        def norm_tail(qb):
            for hh in range(2):
                E.emit("act", lambda s, hh=hh: s.activation(out=recrow[:], in_=PA[64:65, hh * 512:(hh + 1) * 512],
                                                            func=AF.Ln, bias=CT.f32(EPS)),
                       reads=[C.rPA[hh], C.r_const], writes=[r_rec])
                E.emit("act", lambda s: s.activation(out=recrow[:], in_=recrow[:], func=AF.Exp,
                                                     scale=CT.f32(-1.0)),
                       reads=[r_rec, C.r_const], writes=[r_rec])
                E.emit("pe", lambda t, hh=hh: t.matmul(PB[:, hh * 512:(hh + 1) * 512], C.ones_row_f32[0:1, :],
                                                       recrow[0:1, 0:512], start=True, stop=True),
                       reads=[r_rec, C.r_ones_row], writes=[C.rPB[hh]])
            E.emit("dve", lambda v: v.tensor_copy(out=rb[:], in_=PB[:, 0:1024]),
                   reads=[C.rPB[0], C.rPB[1]], writes=[r_rb])
            E.emit("dve", lambda v: v.tensor_tensor(
                       out=Db[0:64, 0:512],
                       in0=PA[0:64, 0:512], in1=rb[0:64, 0:512], op=ALU.mult),
                   reads=[C.rPA[0], r_rb], writes=[r_Db])
            E.emit("dve", lambda v: v.tensor_tensor(
                       out=Db[64:128, 0:512],
                       in0=PA[0:64, 512:1024], in1=rb[0:64, 512:1024], op=ALU.mult),
                   reads=[C.rPA[1], r_rb], writes=[r_Db])
            gb, r_gb = gsl2[qb % 2], r_gsl2[qb % 2]
            E.emit("dve", lambda v, gb=gb: v.tensor_tensor(
                       out=HO[0:64, :],
                       in0=Db[0:64, 0:512], in1=gb[0:64, :], op=ALU.mult),
                   reads=[r_Db, r_gb], writes=[r_HO])
            E.emit("dve", lambda v, gb=gb: v.tensor_tensor(
                       out=HO[64:128, :],
                       in0=Db[64:128, 0:512], in1=gb[64:128, :], op=ALU.mult),
                   reads=[r_Db, r_gb], writes=[r_HO])
            E.dma("pool", lambda g, hp=hp, qb=qb: g.dma_start(
                      out=HO_hbm[hp * 128:(hp + 1) * 128, qb * 512:(qb + 1) * 512], in_=HO[:]),
                  reads=[r_HO], writes=[r_HOh], dest=r_HOh)

        NTOT = NQ * NP
        for n in range(NTOT + 2):
            if n < NTOT:
                qb = n // NP
                if n % NP == 0:
                    q2head(qb)
                emit_S(qb, (n % NP) * 2)
                emit_clamp(n, (n % NP) * 2)
                emit_S(qb, (n % NP) * 2 + 1)
                emit_clamp(n, (n % NP) * 2 + 1)
                feeder()
            if n >= 1 and n - 1 < NTOT:
                emit_chain(n - 1)
            if n >= 2 and n - 2 < NTOT:
                emit_PVpair(n - 2)
                if (n - 2) % NP == NP - 1:
                    norm_tail((n - 2) // NP)

    g = prep_gen(0)
    for _ in g:
        pass
    for hp in range(NH // 2):
        gnext = prep_gen(hp + 1) if hp + 1 < NH // 2 else None
        state = {"cnt": 0}
        def feeder():
            state["cnt"] += 1
            if gnext is not None and state["cnt"] % 4 == 0:
                next(gnext, None)
        attention(hp, feeder)
        if gnext is not None:
            for _ in gnext:
                pass

    load_w(C, W, "W_out", wstage, r_ws[0], slice(0, D), "act")
    r_ws[0].readers.extend(r_ws[1].readers)
    ybuf, r_ybuf = C.slabA, C.r_slabA
    for j in range(NT):
        E.dma("sync", lambda s, j=j: s.dma_start(out=sb_slab(ybuf[:]), in_=ap_slab(xT_in, j)),
              reads=[r_xin], writes=[r_ybuf])
        hos, r_hos = C.tslab[j % 2], C.r_tslab[j % 2]
        E.dma("sync", lambda s, j=j, hos=hos: s.dma_start(out=sb_slab(hos[:]), in_=ap_slab(HO_hbm, j)),
              reads=[r_HOh], writes=[r_hos])
        for dc in range(NCH):
            h = dc % 2
            E.emit("pe", lambda t, dc=dc, j=j, h=h: [t.matmul(
                       PC[:, h * 512:(h + 1) * 512],
                       wstage[:, c * D + dc * 128: c * D + (dc + 1) * 128],
                       hos[:, c * 512:(c + 1) * 512],
                       start=(c == 0), stop=(c == NCH - 1)) for c in range(NCH)],
                   reads=[r_ws[0], r_hos], writes=[C.rPC[h]])
            E.emit("dve", lambda v, dc=dc, h=h: v.scalar_tensor_tensor(
                       out=ybuf[:, dc * 512:(dc + 1) * 512], in0=PC[:, h * 512:(h + 1) * 512],
                       scalar=boutc[:, dc:dc + 1], in1=ybuf[:, dc * 512:(dc + 1) * 512],
                       op0=ALU.add, op1=ALU.add),
                   reads=[C.rPC[h], r_boutc], writes=[r_ybuf])
        E.dma("pool", lambda g, j=j: g.dma_start(out=ap_slab(xT_out, j), in_=sb_slab(ybuf[:])),
              reads=[r_ybuf], writes=[r_xout], dest=r_xout)
        emit_ep_ln(C, ybuf, r_ybuf, j, tT_next, r_tTn)


def emit_l1ffn(C, xT_in, r_xin, xT_out, r_xout, tT_hbm, r_tT, W, tag, tT_next=None, r_tTn=None, skip_ln=False):
    E, nc, CT, T, NT = C.E, C.nc, C.CT, C.T, C.NT
    PA, PB, PC, PL = C.PA, C.PB, C.PC, C.PL
    if not skip_ln:
        emit_ln_to_tT(C, xT_in, r_xin, tT_hbm, r_tT)

    out = C.sb(tag + "out", [128, NCH * T], BF16); r_out = E.dma_res(tag + "out")
    gcb = C.sb(tag + "gcb", [128, 16], BF16); r_outb = E.res(tag + "gcb")
    outb = gcb
    h1b = C.sb(tag + "h1", [128, NCH * 512], BF16); r_h1 = E.res(tag + "h1")
    h2b = C.sb(tag + "h2", [128, 4 * 512], BF16); r_h2 = E.res(tag + "h2")
    h3b = C.sb(tag + "h3", [128, 2 * 512], BF16); r_h3 = E.res(tag + "h3")
    tfrow = C.sb(tag + "tf", [1, T], F32); r_tf = E.res(tag + "tf")
    crow = C.sb(tag + "crow", [1, 512], F32); r_crow = E.res(tag + "crow")
    gcacc = C.sb(tag + "gcacc", [128, NCH + 2], F32); r_gcacc = E.res(tag + "gcacc")
    gfbuf = C.sb(tag + "gf", [128, 8], F32); r_gf = E.res(tag + "gf")
    wstage = C.sb("wstage", [128, NCH * D], BF16); r_wstage = E.dma_res("wstage")
    mtmp = C.sb(tag + "mtmp", [128, 512], F32); r_mtmp = E.res(tag + "mtmp")
    ttmp = C.sb(tag + "ttmp", [128, 512], F32); r_ttmp = E.res(tag + "ttmp")
    bvec = C.sb(tag + "bvec", [128, 24], F32); r_bvec = E.dma_res(tag + "bvec")
    bvec_in = nc.dram_tensor(tag + "bvec_in", [128, 24], F32, kind="ExternalInput").ap()
    E.dma("sync", lambda s: s.dma_start(out=bvec[:], in_=bvec_in), writes=[r_bvec])
    B1, B2, B3, G1, G2, SC = 0, 8, 12, 14, 18, 20

    E.emit("dve", lambda v: v.memset(gcacc[:, 0:NCH], 0.0), writes=[r_gcacc])
    for j in range(NT):
        tb, r_tb = C.tslab[j % 2], C.r_tslab[j % 2]
        E.dma("sync", lambda s, j=j: s.dma_start(out=sb_slab(tb[:]), in_=ap_slab(tT_hbm, j)),
              reads=[r_tT], writes=[r_tb])
        for c in range(NCH):
            E.emit("dve", lambda v, c=c, j=j, tb=tb: v.tensor_copy(
                       out=out[:, c * T + j * 512: c * T + (j + 1) * 512],
                       in_=tb[:, c * 512:(c + 1) * 512]),
                   reads=[r_tb], writes=[r_out])
            E.emit("dve", lambda v, c=c, tb=tb: v.tensor_reduce(
                       out=gcacc[:, NCH:NCH + 1], in_=tb[:, c * 512:(c + 1) * 512],
                       axis=AX.X, op=ALU.add),
                   reads=[r_tb], writes=[r_gcacc])
            E.emit("dve", lambda v, c=c: v.tensor_tensor(
                       out=gcacc[:, c:c + 1], in0=gcacc[:, c:c + 1], in1=gcacc[:, NCH:NCH + 1],
                       op=ALU.add) if True else None,
                   reads=[r_gcacc], writes=[r_gcacc])

    E.emit("dve", lambda v: v.tensor_scalar(out=gcacc[:, 0:NCH], in0=gcacc[:, 0:NCH],
                                            scalar1=1.0 / T, scalar2=None, op0=ALU.mult),
           reads=[r_gcacc], writes=[r_gcacc])
    E.emit("dve", lambda v: v.tensor_copy(out=gcb[:, 0:NCH], in_=gcacc[:, 0:NCH]),
           reads=[r_gcacc], writes=[r_outb])
    load_weight_lhsT(C, W["gc_w1"], W["r_gc_w1"], wstage[:, 0:NCH * 512], r_wstage, slice(0, 512), "pool")
    for dc in range(4):
        E.emit("pe", lambda t, dc=dc: [t.matmul(
                   PL[:, dc:dc + 1], wstage[:, c * 512 + dc * 128: c * 512 + (dc + 1) * 128],
                   gcb[:, c:c + 1], start=(c == 0), stop=(c == NCH - 1)) for c in range(NCH)],
               reads=[r_wstage, r_outb], writes=[C.rPL])
    for dc in range(4):
        E.emit("act", lambda s, dc=dc: s.activation(out=gfbuf[:, dc:dc + 1], in_=PL[:, dc:dc + 1],
                                                    func=AF.Gelu, bias=bvec[:, G1 + dc:G1 + dc + 1]),
               reads=[C.rPL, r_bvec], writes=[r_gf])
    load_weight_lhsT(C, W["gc_w2"], W["r_gc_w2"], wstage[:, 0:4 * 256], r_wstage, slice(0, 256), "pool")
    E.emit("dve", lambda v: v.tensor_copy(out=gcb[:, 8:12], in_=gfbuf[:, 0:4]),
           reads=[r_gf], writes=[r_outb])
    for dc in range(2):
        E.emit("pe", lambda t, dc=dc: [t.matmul(
                   PL[:, 8 + dc:9 + dc], wstage[:, c * 256 + dc * 128: c * 256 + (dc + 1) * 128],
                   gcb[:, 8 + c:9 + c], start=(c == 0), stop=(c == 3)) for c in range(4)],
               reads=[r_wstage, r_outb], writes=[C.rPL])
    for dc in range(2):
        E.emit("act", lambda s, dc=dc: s.activation(out=gfbuf[:, 4 + dc:5 + dc], in_=PL[:, 8 + dc:9 + dc],
                                                    func=AF.Gelu, bias=bvec[:, G2 + dc:G2 + dc + 1]),
               reads=[C.rPL, r_bvec], writes=[r_gf])
    E.dma("pool", lambda g: g.dma_start(out=wstage[:, 0:2], in_=W["gc_w3c"]), reads=[W["r_gc_w3c"]], writes=[r_wstage])
    E.emit("dve", lambda v: v.tensor_copy(out=gcb[:, 12:14], in_=gfbuf[:, 4:6]),
           reads=[r_gf], writes=[r_outb])
    E.emit("pe", lambda t: [t.matmul(PL[0:1, 16:17], wstage[:, c:c + 1], gcb[:, 12 + c:13 + c],
                                     start=(c == 0), stop=(c == 1)) for c in range(2)],
           reads=[r_wstage, r_outb], writes=[C.rPL])
    E.emit("act", lambda s: s.activation(out=gfbuf[0:1, 6:7], in_=PL[0:1, 16:17], func=AF.Sigmoid,
                                         bias=bvec[0:1, SC + 1:SC + 2]),
           reads=[C.rPL, r_bvec], writes=[r_gf])
    E.emit("dve", lambda v: v.tensor_scalar(out=gfbuf[0:1, 7:8], in0=gfbuf[0:1, 6:7],
                                            scalar1=0.7, scalar2=None, op0=ALU.mult),
           reads=[r_gf], writes=[r_gf])

    C.w2stage = C.sb(tag + "w2", [128, NCH * 512], BF16); C.r_w2stage = E.dma_res(tag + "w2")
    C.w3stage = C.sb(tag + "w3", [128, 4 * 256], BF16); C.r_w3stage = E.dma_res(tag + "w3")
    C.w4stage = C.sb(tag + "w4", [128, 2 * 1], BF16); C.r_w4stage = E.dma_res(tag + "w4")
    load_weight_lhsT(C, W["cn_w1"], W["r_cn_w1"], wstage, r_wstage, slice(0, D), "pool")
    load_weight_lhsT(C, W["cn_w2"], W["r_cn_w2"], C.w2stage, C.r_w2stage, slice(0, 512), "pool")
    load_weight_lhsT(C, W["cn_w3"], W["r_cn_w3"], C.w3stage, C.r_w3stage, slice(0, 256), "pool")
    E.dma("pool", lambda g: g.dma_start(out=C.w4stage[:], in_=W["cn_w4c"]), reads=[W["r_cn_w4c"]], writes=[C.r_w4stage])
    for cyc in range(3):
        for j in range(NT):
            for dc in range(NCH):
                h = dc % 2
                E.emit("pe", lambda t, dc=dc, h=h, j=j: [t.matmul(
                           PB[:, h * 512:(h + 1) * 512],
                           wstage[:, c * D + dc * 128: c * D + (dc + 1) * 128],
                           out[:, c * T + j * 512: c * T + (j + 1) * 512],
                           start=(c == 0), stop=(c == NCH - 1)) for c in range(NCH)],
                       reads=[r_wstage, r_out], writes=[C.rPB[h]])
                E.emit("act", lambda s, dc=dc, h=h: s.activation(
                           out=h1b[:, dc * 512:(dc + 1) * 512], in_=PB[:, h * 512:(h + 1) * 512],
                           func=AF.Gelu, bias=bvec[:, B1 + dc:B1 + dc + 1]),
                       reads=[C.rPB[h], r_bvec], writes=[r_h1])
            for dc in range(4):
                h = dc % 2
                E.emit("pe", lambda t, dc=dc, h=h: [t.matmul(
                           PC[:, h * 512:(h + 1) * 512],
                           C.w2stage[:, c * 512 + dc * 128: c * 512 + (dc + 1) * 128],
                           h1b[:, c * 512:(c + 1) * 512],
                           start=(c == 0), stop=(c == NCH - 1)) for c in range(NCH)],
                       reads=[C.r_w2stage, r_h1], writes=[C.rPC[h]])
                E.emit("act", lambda s, dc=dc, h=h: s.activation(
                           out=h2b[:, dc * 512:(dc + 1) * 512], in_=PC[:, h * 512:(h + 1) * 512],
                           func=AF.Gelu, bias=bvec[:, B2 + dc:B2 + dc + 1]),
                       reads=[C.rPC[h], r_bvec], writes=[r_h2])
            for dc in range(2):
                E.emit("pe", lambda t, dc=dc: [t.matmul(
                           PL[:, 0:512],
                           C.w3stage[:, c * 256 + dc * 128: c * 256 + (dc + 1) * 128],
                           h2b[:, c * 512:(c + 1) * 512],
                           start=(c == 0), stop=(c == 3)) for c in range(4)],
                       reads=[C.r_w3stage, r_h2], writes=[C.rPL])
                E.emit("act", lambda s, dc=dc: s.activation(
                           out=h3b[:, dc * 512:(dc + 1) * 512], in_=PL[:, 0:512],
                           func=AF.Gelu, bias=bvec[:, B3 + dc:B3 + dc + 1]),
                       reads=[C.rPL, r_bvec], writes=[r_h3])
            E.emit("pe", lambda t: [t.matmul(PL[0:1, 0:512], C.w4stage[:, c:c + 1],
                                             h3b[:, c * 512:(c + 1) * 512],
                                             start=(c == 0), stop=(c == 1)) for c in range(2)],
                   reads=[C.r_w4stage, r_h3], writes=[C.rPL])
            E.emit("act", lambda s, j=j: s.activation(out=tfrow[0:1, j * 512:(j + 1) * 512],
                                                      in_=PL[0:1, 0:512], func=AF.Sigmoid,
                                                      bias=bvec[0:1, SC:SC + 1]),
                   reads=[C.rPL, r_bvec], writes=[r_tf])
            E.emit("dve", lambda v, j=j: v.tensor_scalar(
                       out=crow[:], in0=tfrow[0:1, j * 512:(j + 1) * 512],
                       scalar1=0.3, scalar2=gfbuf[0:1, 7:8], op0=ALU.mult, op1=ALU.add),
                   reads=[r_tf, r_gf], writes=[r_crow])
            E.emit("dve", lambda v: v.tensor_scalar(
                       out=crow[:], in0=crow[:], scalar1=0.5, scalar2=bvec[0:1, SC + 2:SC + 3],
                       op0=ALU.subtract, op1=ALU.mult),
                   reads=[r_crow, r_bvec], writes=[r_crow])
            E.emit("dve", lambda v: v.tensor_scalar(
                       out=crow[:], in0=crow[:], scalar1=0.05, scalar2=2.0,
                       op0=ALU.add, op1=ALU.mult),
                   reads=[r_crow], writes=[r_crow])
            E.emit("pe", lambda t: t.matmul(PL[:, 0:512], C.ones_row_f32[0:1, :], crow[0:1, :],
                                            start=True, stop=True),
                   reads=[r_crow, C.r_ones_row], writes=[C.rPL])
            for a in range(4):
                b = a + 4
                s_a = +1 if a < 2 else -1
                s_b = -s_a
                sla = slice(a * T + j * 512, a * T + (j + 1) * 512)
                slb = slice(b * T + j * 512, b * T + (j + 1) * 512)
                E.emit("dve", lambda v, sla=sla: v.tensor_copy(out=ttmp[:], in_=out[:, sla]),
                       reads=[r_out], writes=[r_ttmp])
                E.emit("dve", lambda v, slb=slb: v.tensor_tensor(out=mtmp[:], in0=out[:, slb],
                                                                 in1=PL[:, 0:512], op=ALU.mult),
                       reads=[r_out, C.rPL], writes=[r_mtmp])
                E.emit("dve", lambda v, sla=sla, s_a=s_a: v.tensor_tensor(
                           out=out[:, sla], in0=out[:, sla], in1=mtmp[:],
                           op=(ALU.add if s_a > 0 else ALU.subtract)),
                       reads=[r_mtmp], writes=[r_out])
                E.emit("dve", lambda v: v.tensor_tensor(out=mtmp[:], in0=ttmp[:],
                                                        in1=PL[:, 0:512], op=ALU.mult),
                       reads=[r_ttmp, C.rPL], writes=[r_mtmp])
                E.emit("dve", lambda v, slb=slb, s_b=s_b: v.tensor_tensor(
                           out=out[:, slb], in0=out[:, slb], in1=mtmp[:],
                           op=(ALU.add if s_b > 0 else ALU.subtract)),
                       reads=[r_mtmp], writes=[r_out])

    ybuf, r_ybuf = C.slabA, C.r_slabA
    for j in range(NT):
        E.dma("sync", lambda s, j=j: s.dma_start(out=sb_slab(ybuf[:]), in_=ap_slab(xT_in, j)),
              reads=[r_xin], writes=[r_ybuf])
        for c in range(NCH):
            E.emit("dve", lambda v, c=c, j=j: v.tensor_tensor(
                       out=ybuf[:, c * 512:(c + 1) * 512], in0=ybuf[:, c * 512:(c + 1) * 512],
                       in1=out[:, c * T + j * 512: c * T + (j + 1) * 512], op=ALU.add),
                   reads=[r_out], writes=[r_ybuf])
        E.dma("pool", lambda g, j=j: g.dma_start(out=ap_slab(xT_out, j), in_=sb_slab(ybuf[:])),
              reads=[r_ybuf], writes=[r_xout], dest=r_xout)
        emit_ep_ln(C, ybuf, r_ybuf, j, tT_next, r_tTn)


def emit_mamba(C, xT_in, r_xin, xT_out, r_xout, tT_hbm, r_tT, W, tT_next=None, r_tTn=None, skip_ln=False):
    E, nc, CT, T, NT = C.E, C.nc, C.CT, C.T, C.NT
    DI, NDI = 2048, 16
    PA, PB, PC, PL = C.PA, C.PB, C.PC, C.PL
    if not skip_ln:
        emit_ln_to_tT(C, xT_in, r_xin, tT_hbm, r_tT)

    inwT = C.sb("mb_inwT", [128, NCH * DI], BF16); r_inwT = E.dma_res("mb_inwT")
    precast(C, W, "in_wT")
    load_w(C, W, "in_wT", inwT, r_inwT, slice(0, DI), "act")
    xpw = C.sb("mb_xpw", [128, 16 * 65], BF16); r_xpw = E.dma_res("mb_xpw")
    E.dma("pool", lambda g: g.dma_start(out=xpw[:].rearrange("p (c m) -> p c m", m=65),
                                        in_=W["xp_wTa"].rearrange("(c p) m -> p c m", p=128)),
          reads=[W["r_xp_wTa"]], writes=[r_xpw])
    outwT = C.sb("mb_outwT", [16, NCH * 128], BF16); r_outwT = E.dma_res("mb_outwT")
    E.dma("pool", lambda g: g.dma_start(out=outwT[:].rearrange("p (c m) -> p c m", m=128),
                                        in_=W["out_wT"].rearrange("p (c m) -> p c m", m=128)),
          reads=[W["r_out_wT"]], writes=[r_outwT])
    vecs = C.sb("mb_vecs", [128, 16 * 3 + 64 + 2], F32); r_vecs = E.dma_res("mb_vecs")
    vecs_in = nc.dram_tensor("mb_vecs_in", [128, 114], F32, kind="ExternalInput").ap()
    E.dma("sync", lambda s: s.dma_start(out=vecs[:], in_=vecs_in), writes=[r_vecs])
    IB, CB, OB, CW, XB = 0, 16, 32, 48, 112
    hraw2 = [C.sb(f"mb_hraw{i}", [128, T], BF16) for i in range(2)]
    r_hraw2 = [E.res(f"mb_hraw{i}") for i in range(2)]
    acc = C.sb("mb_acc", [128, T], BF16); r_acc = E.res("mb_acc")
    xpacc = C.sb("mb_xpacc", [65, T], F32); r_xpacc = E.res("mb_xpacc")
    r16a = C.sb("mb_r16a", [16, T], F32); r_r16a = E.res("mb_r16a")
    r16b = C.sb("mb_r16b", [16, T], F32); r_r16b = E.res("mb_r16b")
    ysb = C.sb("mb_ysb", [16, T], BF16); r_ysb = E.res("mb_ysb")
    hsrow = C.sb("mb_hsrow", [1, T], BF16); r_hsrow = E.res("mb_hsrow")

    E.emit("dve", lambda v: v.memset(xpacc[:], 0.0), writes=[r_xpacc])
    for di in range(NDI):
        hraw, r_hraw = hraw2[di % 2], r_hraw2[di % 2]
        for j in range(NT):
            tb, r_tb = C.tslab[j % 2], C.r_tslab[j % 2]
            if di == 0 or True:
                pass
            E.dma("sync", lambda s, j=j: s.dma_start(out=sb_slab(tb[:]), in_=ap_slab(tT_hbm, j)),
                  reads=[r_tT], writes=[r_tb])
            h2 = j % 2
            Ps = PB if h2 == 0 else PC
            rPs = C.rPB if h2 == 0 else C.rPC
            E.emit("pe", lambda t, di=di, tb=tb, Ps=Ps: [t.matmul(
                       Ps[:, 0:512], inwT[:, c * DI + di * 128: c * DI + (di + 1) * 128],
                       tb[:, c * 512:(c + 1) * 512],
                       start=(c == 0), stop=(c == NCH - 1)) for c in range(NCH)],
                   reads=[r_inwT, r_tb], writes=[rPs[0]])
            E.emit("dve", lambda v, j=j, di=di, Ps=Ps: v.tensor_scalar(
                       out=hraw[:, j * 512:(j + 1) * 512], in0=Ps[:, 0:512],
                       scalar1=vecs[:, IB + di:IB + di + 1], scalar2=None, op0=ALU.add),
                   reads=[rPs[0], r_vecs], writes=[r_hraw])
        E.emit("dve", lambda v, di=di: v.tensor_scalar(
                   out=acc[:], in0=hraw[:], scalar1=vecs[:, CW + di * 4 + 3:CW + di * 4 + 4],
                   scalar2=None, op0=ALU.mult),
               reads=[r_hraw, r_vecs], writes=[r_acc])
        for s_ in range(1, 4):
            E.emit("dve", lambda v, di=di, s_=s_: v.scalar_tensor_tensor(
                       out=acc[:, s_:T], in0=hraw[:, 0:T - s_],
                       scalar=vecs[:, CW + di * 4 + 3 - s_:CW + di * 4 + 4 - s_],
                       in1=acc[:, s_:T], op0=ALU.mult, op1=ALU.add),
                   reads=[r_hraw, r_vecs, r_acc], writes=[r_acc])
        E.emit("act", lambda s, di=di: s.activation(out=hraw[:], in_=acc[:], func=AF.Silu,
                                                    bias=vecs[:, CB + di:CB + di + 1]),
               reads=[r_acc, r_vecs], writes=[r_hraw])
        for j in range(NT):
            E.emit("pe", lambda t, j=j, di=di: t.matmul(
                       PL[0:65, 0:512], xpw[:, di * 65:(di + 1) * 65], hraw[:, j * 512:(j + 1) * 512],
                       start=True, stop=True),
                   reads=[r_xpw, r_hraw], writes=[C.rPL])
            E.emit("dve", lambda v, j=j: v.tensor_tensor(
                       out=xpacc[:, j * 512:(j + 1) * 512], in0=xpacc[:, j * 512:(j + 1) * 512],
                       in1=PL[0:65, 0:512], op=ALU.add),
                   reads=[C.rPL], writes=[r_xpacc])
    E.emit("act", lambda s: s.activation(out=r16a[:], in_=xpacc[0:16, :], func=AF.Exp,
                                         bias=vecs[0:16, XB:XB + 1]),
           reads=[r_xpacc, r_vecs], writes=[r_r16a])
    E.emit("act", lambda s: s.activation(out=r16a[:], in_=r16a[:], func=AF.Exp,
                                         scale=CT.f32(-1.0)),
           reads=[r_r16a, C.r_const], writes=[r_r16a])
    E.emit("act", lambda s: s.activation(out=r16b[:], in_=xpacc[32:48, :], func=AF.Silu,
                                         bias=vecs[32:48, XB + 1:XB + 2]),
           reads=[r_xpacc, r_vecs], writes=[r_r16b])
    E.emit("dve", lambda v: v.tensor_copy(out=hsrow[:], in_=xpacc[64:65, :]),
           reads=[r_xpacc], writes=[r_hsrow])
    for j in range(NT):
        E.emit("pe", lambda t, j=j: t.matmul(
                   PL[0:16, 0:512], C.ones_row_bf[0:1, 0:16], hsrow[0:1, j * 512:(j + 1) * 512],
                   start=True, stop=True),
               reads=[r_hsrow, C.r_ones_row_b], writes=[C.rPL])
        E.emit("dve", lambda v, j=j: v.tensor_tensor(
                   out=r16b[:, j * 512:(j + 1) * 512], in0=r16b[:, j * 512:(j + 1) * 512],
                   in1=PL[0:16, 0:512], op=ALU.mult),
               reads=[C.rPL], writes=[r_r16b])
    E.emit("dve", lambda v: v.tensor_tensor_scan(out=ysb[:], data0=r16a[:], data1=r16b[:],
                                                 initial=0.0, op0=ALU.mult, op1=ALU.add),
           reads=[r_r16a, r_r16b], writes=[r_ysb])
    ybuf, r_ybuf = C.slabA, C.r_slabA
    for j in range(NT):
        E.dma("sync", lambda s, j=j: s.dma_start(out=sb_slab(ybuf[:]), in_=ap_slab(xT_in, j)),
              reads=[r_xin], writes=[r_ybuf])
        for dc in range(NCH):
            h2 = dc % 2
            Ps = PB if h2 == 0 else PC
            rPs = C.rPB if h2 == 0 else C.rPC
            E.emit("pe", lambda t, dc=dc, j=j, Ps=Ps: t.matmul(
                       Ps[:, 0:512], outwT[:, dc * 128:(dc + 1) * 128],
                       ysb[:, j * 512:(j + 1) * 512], start=True, stop=True),
                   reads=[r_outwT, r_ysb], writes=[rPs[0]])
            E.emit("dve", lambda v, dc=dc, Ps=Ps: v.scalar_tensor_tensor(
                       out=ybuf[:, dc * 512:(dc + 1) * 512], in0=Ps[:, 0:512],
                       scalar=vecs[:, OB + dc:OB + dc + 1], in1=ybuf[:, dc * 512:(dc + 1) * 512],
                       op0=ALU.add, op1=ALU.add),
                   reads=[rPs[0], r_vecs], writes=[r_ybuf])
        E.dma("pool", lambda g, j=j: g.dma_start(out=ap_slab(xT_out, j), in_=sb_slab(ybuf[:])),
              reads=[r_ybuf], writes=[r_xout], dest=r_xout)
        emit_ep_ln(C, ybuf, r_ybuf, j, tT_next, r_tTn)


def emit_moe_prologue(C, W):
    E, nc = C.E, C.nc
    M = 1408
    aps = {}
    for nm, shape in (("w_g", (8, D, M)), ("w_u", (8, D, M)), ("w_d", (8, M, D))):
        bf = nc.dram_tensor(nm + "_bfs", list(shape), BF16, kind="Internal").ap()
        r = E.dma_res(nm + "_bf")
        W[nm + "_bf"] = bf
        W["r_" + nm + "_bf"] = r
        for e in range(8):
            q = "pool"
            E.dma(q, lambda g, nm=nm, e=e, bf=bf: g.dma_start(out=bf[e], in_=W[nm][e]),
                  reads=[W["r_" + nm]], writes=[r], dest=r)
    return W



T_FULL = 4096
N_CORES = 8
_FUSED_CACHE = {}


def mk_fused(T):
    C = Ctx(T)
    nc = C.nc
    C.open()
    E = C.E
    xT_in = nc.dram_tensor("xT_in", [D, T], F32, kind="ExternalInput").ap()
    xT_out = nc.dram_tensor("xT_out", [D, T], F32, kind="ExternalOutput").ap()
    scrA = nc.dram_tensor("scrA", [D, T], F32, kind="Internal").ap()
    scrB = nc.dram_tensor("scrB", [D, T], F32, kind="Internal").ap()
    tTs = [nc.dram_tensor(f"tT_scratch{i}", [D, T], BF16, kind="Internal").ap() for i in range(2)]
    r_xin = E.res("xT_in")
    r_A = E.dma_res("scrA")
    r_B = E.dma_res("scrB")
    r_out = E.dma_res("xT_out")
    r_tTs = [E.dma_res(f"tT{i}") for i in range(2)]

    def decl(name, shape):
        ap = nc.dram_tensor(name, list(shape), F32, kind="ExternalInput").ap()
        return ap, E.res(name)

    WA = {}
    for n, shape in (("w_qkv", (D, 3 * D)), ("w_proj", (D, D))):
        WA[n], WA["r_" + n] = decl(n, shape)
    for n in ("bpc", "gammac", "betac"):
        sb_t = C.sb("v_" + n, [128, NCH], F32)
        r = E.dma_res("v_" + n)
        inp = nc.dram_tensor("in_" + n, [128, NCH], F32, kind="ExternalInput").ap()
        E.dma("sync", lambda s, sb_t=sb_t, inp=inp: s.dma_start(out=sb_t[:], in_=inp), writes=[r])
        WA[n] = sb_t
        WA["r_" + n] = r
    WM = {}
    for n, shape in (("w_gate", (D, 8)), ("w_g", (8, D, 1408)), ("w_u", (8, D, 1408)),
                     ("w_d", (8, 1408, D))):
        WM[n], WM["r_" + n] = decl(n, shape)
    WB = {}
    for n in ("w_qT", "w_kT", "w_vT", "w_oT"):
        WB[n], WB["r_" + n] = decl(n, (D, D))
    W5 = {}
    for n, shape in (("W_in", (D, 4 * D)), ("W_out", (D, D))):
        W5[n], W5["r_" + n] = decl(n, shape)
    W6 = {}
    for n, shape in (("cn_w1", (D, D)), ("cn_w2", (D, 512)), ("cn_w3", (512, 256)),
                     ("gc_w1", (D, 512)), ("gc_w2", (512, 256))):
        W6[n], W6["r_" + n] = decl(n, shape)
    for n in ("cn_w4c", "gc_w3c"):
        W6[n], W6["r_" + n] = decl(n, (128, 2))
    W7 = {}
    for n, shape in (("in_wT", (D, 2048)), ("xp_wTa", (2048, 65)), ("out_wT", (16, D))):
        W7[n], W7["r_" + n] = decl(n, shape)

    emit_moe_prologue(C, WM)
    chain = [(xT_in, r_xin), (scrA, r_A), (scrB, r_B)]
    seq = [("attn_a0", None), ("moe", "m2_"), ("attn_b0", None), ("moe", "m4_"),
           ("attn_a1", None), ("l1ffn", "f5_"), ("mamba", None), ("l1ffn", "f7_")]
    cur = chain[0]
    for si, (kind, tag) in enumerate(seq):
        nxt = (xT_out, r_out) if si == len(seq) - 1 else chain[1 + (si % 2)]
        tT, r_tT = tTs[si % 2], r_tTs[si % 2]
        if si < len(seq) - 1:
            tT_next, r_tTn = tTs[(si + 1) % 2], r_tTs[(si + 1) % 2]
        else:
            tT_next, r_tTn = None, None
        kw = dict(tT_next=tT_next, r_tTn=r_tTn, skip_ln=(si > 0))
        C.push_scope(f"s{si}_")
        if kind == "attn_a0":
            emit_attn_a0(C, cur[0], cur[1], nxt[0], nxt[1], tT, r_tT, WA, **kw)
        elif kind == "moe":
            emit_moe(C, cur[0], cur[1], nxt[0], nxt[1], tT, r_tT, WM, tag, **kw)
        elif kind == "attn_b0":
            emit_attn_b0(C, cur[0], cur[1], nxt[0], nxt[1], tT, r_tT, WB, **kw)
        elif kind == "attn_a1":
            emit_attn_a1(C, cur[0], cur[1], nxt[0], nxt[1], tT, r_tT, W5, **kw)
        elif kind == "l1ffn":
            emit_l1ffn(C, cur[0], cur[1], nxt[0], nxt[1], tT, r_tT, W6, tag, **kw)
        elif kind == "mamba":
            emit_mamba(C, cur[0], cur[1], nxt[0], nxt[1], tT, r_tT, W7, **kw)
        C.barrier()
        C.pop_scope()
        cur = nxt
    E.wait_all("sync")
    C.finish()
    return C


def _fused_inputs(C, P):
    d = dict(C.host_consts)
    d["w_qkv"] = P["L0_attn_a_w_qkv"]
    d["w_proj"] = P["L0_attn_a_w_proj"]
    d["in_bpc"] = chunked_vec(P["L0_attn_a_b_proj"])
    d["in_gammac"] = chunked_vec(P["L0_attn_a_ln_gamma"])
    d["in_betac"] = chunked_vec(P["L0_attn_a_ln_beta"])
    d["w_gate"] = P["L0_ffn_w_gate"]
    d["w_g"] = P["L0_ffn_w_g"]
    d["w_u"] = P["L0_ffn_w_u"]
    d["w_d"] = P["L0_ffn_w_d"]
    d["w_qT"] = np.ascontiguousarray(P["L0_attn_b_w_q"].T)
    d["w_kT"] = np.ascontiguousarray(P["L0_attn_b_w_k"].T)
    d["w_vT"] = np.ascontiguousarray(P["L0_attn_b_w_v"].T)
    d["w_oT"] = np.ascontiguousarray(P["L0_attn_b_w_o"].T)
    d["W_in"] = P["L1_attn_a_W_in"]
    d["W_out"] = P["L1_attn_a_W_out"]
    d["a1_binc_in"] = chunked_vec(P["L1_attn_a_b_in"])
    d["a1_boutc_in"] = chunked_vec(P["L1_attn_a_b_out"])
    for n in ("cn_w1", "cn_w2", "cn_w3", "gc_w1", "gc_w2"):
        d[n] = P["L1_ffn_" + n]
    d["cn_w4c"] = chunked_vec(P["L1_ffn_cn_w4"][:, 0])
    d["gc_w3c"] = chunked_vec(P["L1_ffn_gc_w3"][:, 0])
    bvec = np.zeros((128, 24), np.float32)
    bvec[:, 0:8] = chunked_vec(P["L1_ffn_cn_b1"])
    bvec[:, 8:12] = chunked_vec(P["L1_ffn_cn_b2"])
    bvec[:, 12:14] = chunked_vec(P["L1_ffn_cn_b3"])
    bvec[:, 14:18] = chunked_vec(P["L1_ffn_gc_b1"])
    bvec[:, 18:20] = chunked_vec(P["L1_ffn_gc_b2"])
    bvec[:, 20] = P["L1_ffn_cn_b4"][0]
    bvec[:, 21] = P["L1_ffn_gc_b3"][0]
    bvec[:, 22] = P["L1_ffn_adaptive_range"][0]
    d["f5_bvec_in"] = bvec
    d["f7_bvec_in"] = bvec
    DI = 2048
    d["in_wT"] = np.ascontiguousarray(P["L1_attn_b_in_w"].T)
    xp_w = P["L1_attn_b_xp_w"]
    xpa = np.zeros((DI, 65), np.float32)
    xpa[:, 0:16] = xp_w[0:16].T
    xpa[:, 32:48] = xp_w[16:32].T
    xpa[:, 64] = 1.0
    d["xp_wTa"] = xpa
    d["out_wT"] = np.ascontiguousarray(P["L1_attn_b_out_w"].T)
    vecs = np.zeros((128, 114), np.float32)
    vecs[:, 0:16] = chunked_vec(P["L1_attn_b_in_b"])
    vecs[:, 16:32] = chunked_vec(P["L1_attn_b_conv_b"])
    vecs[:, 32:40] = chunked_vec(P["L1_attn_b_out_b"])
    cw = P["L1_attn_b_conv_w"][:, 0, :]
    vecs[:, 48:112] = cw.reshape(16, 128, 4).transpose(1, 0, 2).reshape(128, 64)
    vecs[0:16, 112] = P["L1_attn_b_xp_b"][0:16]
    vecs[32:48, 113] = P["L1_attn_b_xp_b"][16:32]
    d["mb_vecs_in"] = vecs
    return d


def kernel(**P):
    from concourse.bass_utils import run_bass_kernel_spmd
    if T_FULL not in _FUSED_CACHE:
        _FUSED_CACHE[T_FULL] = mk_fused(T_FULL)
    C = _FUSED_CACHE[T_FULL]
    x = np.asarray(P["L0_attn_a_target"])
    B = x.shape[0]
    shared = _fused_inputs(C, {k: np.asarray(v) for k, v in P.items()})
    in_maps = [dict(shared, xT_in=np.ascontiguousarray(x[b].T).astype(np.float32))
               for b in range(B)]
    res = run_bass_kernel_spmd(C.nc, in_maps, core_ids=list(range(N_CORES)))
    out = np.stack([np.asarray(res.results[b]["xT_out"]).T for b in range(B)], axis=0)
    return out.astype(np.float32)


# version 16 (final)
# speedup vs baseline: 1.0882x; 1.0358x over previous
import contextlib
import numpy as np
import concourse.bass as bass
import concourse.mybir as mybir

F32 = mybir.dt.float32
BF16 = mybir.dt.bfloat16
AF = mybir.ActivationFunctionType
ALU = mybir.AluOpType
AX = mybir.AxisListType

D = 1024
NCH = D // 128
NH = 16
HD = 64


class ConstTable:
    def __init__(self, nc, sb_tensor, cap):
        self.nc = nc
        self.sb = sb_tensor
        self.cap = cap
        self.vals = {}
    def f32(self, v):
        v = float(np.float32(v))
        if v not in self.vals:
            idx = len(self.vals)
            assert idx < self.cap, "const table full"
            self.vals[v] = idx
            self.nc.const_aps.register(mybir.dt.float32, v, self.sb[:, idx:idx + 1])
        return v
    def host_array(self):
        arr = np.zeros((128, self.cap), dtype=np.float32)
        for v, i in self.vals.items():
            arr[:, i] = v
        return arr


class Res:
    __slots__ = ("name", "writer", "readers", "sem", "sem_count")
    def __init__(self, name, sem=None):
        self.name = name
        self.writer = None
        self.readers = []
        self.sem = sem
        self.sem_count = 0


class Emitter:
    def __init__(self, nc, stack):
        self.nc = nc
        self.stack = stack
        self.engines = {}
        self.sems = {}
        self.counts = {}
        self.watermarks = {}
        self.n_waits = 0
        self.n_inst = 0
        for name, handle in (("pe", nc.tensor), ("act", nc.scalar),
                             ("dve", nc.vector), ("pool", nc.gpsimd),
                             ("sync", nc.sync)):
            self.engines[name] = handle
            self.watermarks[name] = {}
            if name != "sync":
                self._new_sem(name)

    def _new_sem(self, name):
        if name in self.sems:
            i = 2
            while f"{name}{i}" in self.sems:
                i += 1
            name = f"{name}{i}"
        sem = self.stack.enter_context(self.nc.semaphore(f"sem_{name}"))
        self.sems[name] = sem
        self.counts[name] = 0
        return name

    def res(self, name):
        return Res(name)

    def dma_res(self, name):
        return Res(name, sem=self._new_sem(name))


    def _wait(self, eng, sem_name, value):
        if value <= 0:
            return
        wm = self.watermarks[eng]
        if wm.get(sem_name, 0) >= value:
            return
        self.engines[eng].wait_ge(self.sems[sem_name], value)
        wm[sem_name] = value
        self.n_waits += 1

    def _deps(self, eng, reads, writes):
        def need(sname):
            return not (eng == "pe" and sname == "pe")
        for r in reads:
            if r.writer is not None and need(r.writer[0]):
                self._wait(eng, *r.writer)
        for w in writes:
            if w.writer is not None and need(w.writer[0]):
                self._wait(eng, *w.writer)
            for (sname, c) in w.readers:
                if need(sname):
                    self._wait(eng, sname, c)

    def emit(self, eng, fn, reads=(), writes=()):
        self._deps(eng, reads, writes)
        inst = fn(self.engines[eng])
        if isinstance(inst, list):
            inst = inst[-1]
        self.counts[eng] += 1
        cnt = self.counts[eng]
        inst.then_inc(self.sems[eng], 1)
        self.n_inst += 1
        for r in reads:
            r.readers.append((eng, cnt))
        for w in writes:
            w.writer = (eng, cnt)
            w.readers = []
        return inst

    def dma(self, queue, fn, reads=(), writes=(), dest=None):
        if dest is None:
            assert len(writes) == 1
            dest = writes[0]
        assert dest.sem is not None, f"{dest.name}: DMA dest needs dma_res()"
        self._deps(queue, reads, writes)
        inst = fn(self.engines[queue])
        self.counts[dest.sem] += 16
        cnt = self.counts[dest.sem]
        dest.sem_count = cnt
        inst.then_inc(self.sems[dest.sem], 16)
        self.n_inst += 1
        for r in reads:
            r.readers.append((dest.sem, cnt))
        for w in writes:
            w.writer = (dest.sem, cnt)
            w.readers = []
        return inst

    def wait_all(self, eng):
        for sname, c in self.counts.items():
            self._wait(eng, sname, c)


class Ctx:
    def __init__(self, T):
        self.T = T
        self.NT = T // 512
        self.nc = bass.Bass("TRN2", target_bir_lowering=False)
        self.stack = contextlib.ExitStack()
        self.host_consts = {}

    def input(self, name, shape):
        ap = self.nc.dram_tensor(name, list(shape), F32, kind="ExternalInput").ap()
        return ap, self.E.res("in_" + name)

    def open(self):
        nc = self.nc
        st = self.stack
        self.E = Emitter(nc, st)
        self.prefix = ""
        self.scope = None
        def _sb(name, shape, dt):
            tgt = self.scope if self.scope is not None else st
            return tgt.enter_context(nc.sbuf_tensor(self.prefix + name, shape, dt))
        self.sb = _sb
        self.PA = st.enter_context(nc.psum_tensor("PA", [128, 1024], F32))
        self.PB = st.enter_context(nc.psum_tensor("PB", [128, 1024], F32))
        self.PC = st.enter_context(nc.psum_tensor("PC", [128, 1024], F32))
        self.PL = st.enter_context(nc.psum_tensor("PL", [128, 512], F32))
        self.PT = st.enter_context(nc.psum_tensor("PT", [128, 128], BF16))
        E = self.E
        self.rPA = [E.res("PAh0"), E.res("PAh1")]
        self.rPB = [E.res("PBh0"), E.res("PBh1")]
        self.rPC = [E.res("PCh0"), E.res("PCh1")]
        self.rPL = E.res("PL")
        self.rPT = E.res("PT")
        self.const_sb = self.sb("const_sb", [128, 48], F32)
        self.CT = ConstTable(nc, self.const_sb, 48)
        self.r_const = E.dma_res("consts")
        self.consts_in = nc.dram_tensor("consts", [128, 48], F32, kind="ExternalInput").ap()
        self.ident_bf = self.sb("ident_bf", [128, 128], BF16)
        self.ident_f32 = self.sb("ident_f32", [128, 128], F32)
        self.r_ident = E.dma_res("ident")
        self.r_ident_b = E.dma_res("ident_b")
        self.ident_in = nc.dram_tensor("ident_in", [128, 128], F32, kind="ExternalInput").ap()
        self.ones_f32 = self.sb("ones_f32", [128, 2], F32)
        self.r_ones = E.dma_res("ones")
        self.ones_in = nc.dram_tensor("ones_in", [128, 2], F32, kind="ExternalInput").ap()
        self.ones_row_f32 = self.sb("ones_row_f32", [1, 128], F32)
        self.ones_row_bf = self.sb("ones_row_bf", [1, 128], BF16)
        self.r_ones_row = E.dma_res("ones_row")
        self.r_ones_row_b = E.dma_res("ones_row_b")
        self.ones_row_in = nc.dram_tensor("ones_row_in", [1, 128], F32, kind="ExternalInput").ap()
        E.dma("sync", lambda s: s.dma_start(out=self.const_sb[:], in_=self.consts_in), writes=[self.r_const])
        E.dma("sync", lambda s: s.dma_start(out=self.ident_f32[:], in_=self.ident_in), writes=[self.r_ident])
        E.dma("pool", lambda g: g.dma_start(out=self.ident_bf[:], in_=self.ident_in), writes=[self.r_ident_b])
        E.dma("sync", lambda s: s.dma_start(out=self.ones_f32[:], in_=self.ones_in), writes=[self.r_ones])
        E.dma("sync", lambda s: s.dma_start(out=self.ones_row_f32[:], in_=self.ones_row_in), writes=[self.r_ones_row])
        E.dma("pool", lambda g: g.dma_start(out=self.ones_row_bf[:], in_=self.ones_row_in), writes=[self.r_ones_row_b])
        self.host_consts["ident_in"] = np.eye(128, dtype=np.float32)
        self.host_consts["ones_in"] = np.ones((128, 2), dtype=np.float32)
        self.host_consts["ones_row_in"] = np.ones((1, 128), dtype=np.float32)
        self.slabA = self.sb("slabA", [128, NCH * 512], F32)
        self.slabB = self.sb("slabB", [128, NCH * 512], F32)
        self.r_slabA = E.dma_res("slabA")
        self.r_slabB = E.res("slabB")
        self.tslab = [self.sb(f"tslab{i}", [128, NCH * 512], BF16) for i in range(2)]
        self.r_tslab = [E.dma_res(f"tslab{i}") for i in range(2)]
        self.tepb = self.sb("tepb", [128, NCH * 512], BF16)
        self.r_tepb = self.E.res("tepb")
        self.stat = self.sb("stat", [1, 2048], F32)
        self.r_stat = E.res("stat")
        self.masks = None
        self.hsel = None
        self.ensure_masks()
        self.ensure_hsel()

    def ensure_masks(self):
        if self.masks is not None:
            return
        _sc, _pf = self.scope, self.prefix
        self.scope, self.prefix = None, ""
        try:
            self._ensure_masks_inner()
        finally:
            self.scope, self.prefix = _sc, _pf

    def _ensure_masks_inner(self):
        E, nc = self.E, self.nc
        self.masks = self.sb("masks", [128, 4 * 512], BF16)
        self.r_masks = E.dma_res("masks")
        self.masks_in = nc.dram_tensor("masks_in", [128, 2048], F32, kind="ExternalInput").ap()
        E.dma("pool", lambda g: g.dma_start(out=self.masks[:], in_=self.masks_in), writes=[self.r_masks])
        mk = np.zeros((128, 4, 512), np.float32)
        p = np.arange(128)[:, None]
        f = np.arange(512)[None, :]
        for ci, coff in enumerate((0, 128, 256, 384)):
            mk[:, ci, :] = (f >= p + coff).astype(np.float32)
        self.host_consts["masks_in"] = np.ascontiguousarray(mk.reshape(128, 2048))

    def push_scope(self, prefix):
        import contextlib as _ctx
        assert self.scope is None
        self.scope = _ctx.ExitStack()
        self.prefix = prefix

    def pop_scope(self):
        self.scope.close()
        self.scope = None
        self.prefix = ""

    def barrier(self):
        for eng in ("pe", "act", "dve", "pool", "sync"):
            self.E.wait_all(eng)

    def ensure_hsel(self):
        if getattr(self, "hsel", None) is not None:
            return
        _sc, _pf = self.scope, self.prefix
        self.scope, self.prefix = None, ""
        try:
            self._ensure_hsel_inner()
        finally:
            self.scope, self.prefix = _sc, _pf

    def _ensure_hsel_inner(self):
        E, nc = self.E, self.nc
        self.hsel = self.sb("hsel", [128, 2], F32)
        self.hsel2 = self.sb("hsel2", [2, 128], F32)
        self.hsel2b = self.sb("hsel2b", [66, 128], F32)
        self.r_hsel = E.dma_res("hsel")
        self.hsel_in = nc.dram_tensor("hsel_in", [128, 2], F32, kind="ExternalInput").ap()
        self.hsel2_in = nc.dram_tensor("hsel2_in", [2, 128], F32, kind="ExternalInput").ap()
        self.hsel2b_in = nc.dram_tensor("hsel2b_in", [66, 128], F32, kind="ExternalInput").ap()
        E.dma("sync", lambda s: s.dma_start(out=self.hsel[:], in_=self.hsel_in), writes=[self.r_hsel])
        E.dma("sync", lambda s: s.dma_start(out=self.hsel2[:], in_=self.hsel2_in), writes=[self.r_hsel])
        E.dma("sync", lambda s: s.dma_start(out=self.hsel2b[:], in_=self.hsel2b_in), writes=[self.r_hsel])
        h = np.zeros((128, 2), np.float32)
        h[0:64, 0] = 1.0
        h[64:128, 1] = 1.0
        self.host_consts["hsel_in"] = h
        self.host_consts["hsel2_in"] = np.ascontiguousarray(h.T)
        h2b = np.zeros((66, 128), np.float32)
        h2b[64, 0:64] = 1.0
        h2b[65, 64:128] = 1.0
        self.host_consts["hsel2b_in"] = h2b

    def finish(self):
        self.host_consts["consts"] = self.CT.host_array()
        return self.host_consts

    def close(self):
        self.stack.close()


def chunked_vec(v):
    return np.ascontiguousarray(v.reshape(-1, 128).T)




def ap_slab(hbm, j, w=512, p=128):
    return hbm[:, j * w:(j + 1) * w].rearrange("(c p) t -> p c t", p=p)


def sb_slab(sb_ap, w=512):
    return sb_ap.rearrange("p (c t) -> p c t", t=w)


def emit_lnstats(C, buf, r_buf, eps, chunks=None):
    E, CT = C.E, C.CT
    PL, PB = C.PL, C.PB
    sq, r_sq = C.slabB, C.r_slabB
    if chunks is None:
        chunks = [buf[:, c * 512:(c + 1) * 512] for c in range(NCH)]
    for c in range(NCH):
        E.emit("act", lambda s, c=c: s.activation(out=sq[:, c * 512:(c + 1) * 512], in_=chunks[c],
                                                  func=AF.Square),
               reads=[r_buf], writes=[r_sq])
    E.emit("pe", lambda t: [t.matmul(PL[0:1, :], C.ones_f32[:, 0:1], chunks[c],
                                     start=(c == 0), stop=(c == NCH - 1)) for c in range(NCH)],
           reads=[r_buf, C.r_ones], writes=[C.rPL])
    E.emit("pe", lambda t: [t.matmul(PL[32:33, :], C.ones_f32[:, 0:1], sq[:, c * 512:(c + 1) * 512],
                                     start=(c == 0), stop=(c == NCH - 1)) for c in range(NCH)],
           reads=[r_sq, C.r_ones], writes=[C.rPL])
    st_, r_st = C.stat, C.r_stat
    E.emit("dve", lambda v: v.tensor_scalar(out=st_[:, 0:512], in0=PL[0:1, :],
                                            scalar1=1.0 / D, scalar2=None, op0=ALU.mult),
           reads=[C.rPL], writes=[r_st])
    E.emit("dve", lambda v: v.tensor_tensor(out=st_[:, 512:1024], in0=st_[:, 0:512],
                                            in1=st_[:, 0:512], op=ALU.mult),
           reads=[r_st], writes=[r_st])
    E.emit("dve", lambda v: v.scalar_tensor_tensor(out=st_[:, 1024:1536], in0=PL[32:33, :],
                                                   scalar=1.0 / D, in1=st_[:, 512:1024],
                                                   op0=ALU.mult, op1=ALU.subtract),
           reads=[C.rPL, r_st], writes=[r_st])
    E.emit("act", lambda s: s.activation(out=st_[:, 512:1024], in_=st_[:, 1024:1536], func=AF.Ln,
                                         bias=CT.f32(eps)),
           reads=[r_st, C.r_const], writes=[r_st])
    E.emit("act", lambda s: s.activation(out=st_[:, 1536:2048], in_=st_[:, 512:1024], func=AF.Exp,
                                         scale=CT.f32(-0.5)),
           reads=[r_st, C.r_const], writes=[r_st])
    E.emit("pe", lambda t: t.matmul(PB[:, 0:512], C.ones_row_f32[0:1, :], st_[:, 0:512],
                                    start=True, stop=True),
           reads=[r_st, C.r_ones_row], writes=[C.rPB[0]])
    E.emit("pe", lambda t: t.matmul(PB[:, 512:1024], C.ones_row_f32[0:1, :], st_[:, 1536:2048],
                                    start=True, stop=True),
           reads=[r_st, C.r_ones_row], writes=[C.rPB[1]])


def emit_ln_to_tT(C, xT_in, r_xin, tT_hbm, r_tT, eps=1e-5):
    E, T, NT = C.E, C.T, C.NT
    xs, r_xs = C.slabA, C.r_slabA
    tb, r_tb = C.tslab[0], C.r_tslab[0]
    for j in range(NT):
        E.dma("sync", lambda s, j=j: s.dma_start(out=sb_slab(xs[:]), in_=ap_slab(xT_in, j)),
              reads=[r_xin], writes=[r_xs])
        emit_lnstats(C, xs, r_xs, eps)
        for c in range(NCH):
            sl = slice(c * 512, (c + 1) * 512)
            E.emit("dve", lambda v, sl=sl: v.tensor_tensor(out=C.slabB[:, sl], in0=xs[:, sl],
                                                           in1=C.PB[:, 0:512], op=ALU.subtract),
                   reads=[r_xs, C.rPB[0]], writes=[C.r_slabB])
            E.emit("dve", lambda v, sl=sl: v.tensor_tensor(out=tb[:, sl], in0=C.slabB[:, sl],
                                                           in1=C.PB[:, 512:1024], op=ALU.mult),
                   reads=[C.r_slabB, C.rPB[1]], writes=[r_tb])
        E.dma("pool", lambda g, j=j: g.dma_start(out=ap_slab(tT_hbm, j), in_=sb_slab(tb[:])),
              reads=[r_tb], writes=[r_tT], dest=r_tT)


def emit_ep_ln(C, ybuf, r_ybuf, jw, tT_next, r_tTn, eps=1e-5, chunks=None, tb_idx=0):
    E, T = C.E, C.T
    if tT_next is None:
        return
    emit_lnstats(C, ybuf, r_ybuf, eps, chunks=chunks)
    if chunks is None:
        chunks = [ybuf[:, c * 512:(c + 1) * 512] for c in range(NCH)]
    tb, r_tb = C.tepb, C.r_tepb
    for c in range(NCH):
        sl = slice(c * 512, (c + 1) * 512)
        E.emit("dve", lambda v, sl=sl, c=c: v.tensor_tensor(out=C.slabB[:, sl], in0=chunks[c],
                                                            in1=C.PB[:, 0:512], op=ALU.subtract),
               reads=[r_ybuf, C.rPB[0]], writes=[C.r_slabB])
        E.emit("dve", lambda v, sl=sl: v.tensor_tensor(out=tb[:, sl], in0=C.slabB[:, sl],
                                                       in1=C.PB[:, 512:1024], op=ALU.mult),
               reads=[C.r_slabB, C.rPB[1]], writes=[r_tb])
    E.dma("pool", lambda g, jw=jw: g.dma_start(out=ap_slab(tT_next, jw), in_=sb_slab(tb[:])),
          reads=[r_tb], writes=[r_tTn], dest=r_tTn)

def precast(C, W, name, queue="pool"):
    if name + "_bf" in W:
        return
    E, nc = C.E, C.nc
    ap = W[name]
    bf = nc.dram_tensor(name + "_bfs", list(ap.shape), BF16, kind="Internal").ap()
    r = E.dma_res(name + "_bfr")
    E.dma(queue, lambda g: g.dma_start(out=bf, in_=ap), reads=[W["r_" + name]], writes=[r])
    W[name + "_bf"] = bf
    W["r_" + name + "_bf"] = r


def load_weight_lhsT(C, w_hbm, r_w, dst_sb, r_dst, cols, queue="pool"):
    E = C.E
    m = cols.stop - cols.start
    nrow = w_hbm.shape[0] // 128
    src = w_hbm[:, cols].rearrange("(c p) m -> p c m", p=128)
    dst3 = dst_sb[:, 0:nrow * m].rearrange("p (c m) -> p c m", m=m)
    E.dma(queue, lambda g: g.dma_start(out=dst3, in_=src),
          reads=[r_w], writes=[r_dst])


def load_w(C, W, name, dst_sb, r_dst, cols, queue="act"):
    load_weight_lhsT(C, W[name + "_bf"], W["r_" + name + "_bf"], dst_sb, r_dst, cols, queue)


def emit_attn_a0(C, xT_in, r_xin, xT_out, r_xout, tT_hbm, r_tT, W, tT_next=None, r_tTn=None, skip_ln=False):
    E, nc, CT, T, NT = C.E, C.nc, C.CT, C.T, C.NT
    NQ = T // 512
    NTK = T // 128
    SHIFT = 8.0
    PA, PB, PC, PL, PT = C.PA, C.PB, C.PC, C.PL, C.PT
    C.ensure_masks()

    precast(C, W, "w_qkv")
    precast(C, W, "w_proj")
    if not skip_ln:
        emit_ln_to_tT(C, xT_in, r_xin, tT_hbm, r_tT)

    QT = C.sb("aa_QT", [128, T], BF16); r_QT = E.res("aa_QT")
    KT = C.sb("aa_KT", [128, T], BF16); r_KT = E.res("aa_KT")
    VTs = C.sb("aa_VTs", [128, 512], BF16); r_VTs = E.res("aa_VTs")
    Vp = C.sb("aa_Vp", [128, NTK * 130], BF16); r_Vp = E.res("aa_Vp")
    Pt = [C.sb(f"aa_P{i}", [128, 1024], BF16) for i in range(2)]
    r_Pt = [E.res(f"aa_P{i}") for i in range(2)]
    attnO = C.sb("aa_attnO", [128, NCH * T], BF16); r_attnO = E.res("aa_attnO")
    recrow = C.sb("aa_recrow", [1, 1024], F32); r_rec = E.res("aa_recrow")
    rb = C.sb("aa_rb", [128, 1024], F32); r_rb = E.res("aa_rb")
    wstage = C.sb("wstage", [128, NCH * D], BF16); r_wstage = E.dma_res("wstage")

    for i in range(NTK):
        E.emit("dve", lambda v, i=i: v.memset(Vp[:, i * 130 + 64:i * 130 + 65], 1.0), writes=[r_Vp])
        E.emit("dve", lambda v, i=i: v.memset(Vp[:, i * 130 + 129:i * 130 + 130], 1.0), writes=[r_Vp])

    w_qkv, r_wqkv = W["w_qkv"], W["r_w_qkv"]
    wq = wstage[:, 0:NCH * 128]
    wk = wstage[:, NCH * 128:2 * NCH * 128]
    wv = wstage[:, 2 * NCH * 128:3 * NCH * 128]

    for hp in range(NH // 2):
        c0 = hp * 128
        load_w(C, W, "w_qkv", wq, r_wstage, slice(c0, c0 + 128), "act")
        load_w(C, W, "w_qkv", wk, r_wstage, slice(D + c0, D + c0 + 128), "act")
        load_w(C, W, "w_qkv", wv, r_wstage, slice(2 * D + c0, 2 * D + c0 + 128), "act")
        for j in range(NT):
            tb, r_tb = C.tslab[j % 2], C.r_tslab[j % 2]
            E.dma("sync", lambda s, j=j: s.dma_start(out=sb_slab(tb[:]), in_=ap_slab(tT_hbm, j)),
                  reads=[r_tT], writes=[r_tb])
            for wi, (wsb, dst, r_dst) in enumerate(((wq, QT, r_QT), (wk, KT, r_KT))):
                E.emit("pe", lambda t, wsb=wsb, tb=tb, wi=wi: [t.matmul(
                           PB[:, wi * 512:(wi + 1) * 512], wsb[:, c * 128:(c + 1) * 128],
                           tb[:, c * 512:(c + 1) * 512],
                           start=(c == 0), stop=(c == NCH - 1)) for c in range(NCH)],
                       reads=[r_wstage, r_tb], writes=[C.rPB[wi]])
                E.emit("dve", lambda v, dst=dst, j=j, wi=wi: v.tensor_copy(
                           out=dst[:, j * 512:(j + 1) * 512], in_=PB[:, wi * 512:(wi + 1) * 512]),
                       reads=[C.rPB[wi]], writes=[r_dst])
            E.emit("pe", lambda t, tb=tb: [t.matmul(
                       PC[:, 0:512], wv[:, c * 128:(c + 1) * 128], tb[:, c * 512:(c + 1) * 512],
                       start=(c == 0), stop=(c == NCH - 1)) for c in range(NCH)],
                   reads=[r_wstage, r_tb], writes=[C.rPC[0]])
            E.emit("dve", lambda v: v.tensor_copy(out=VTs[:], in_=PC[:, 0:512]),
                   reads=[C.rPC[0]], writes=[r_VTs])
            for k in range(4):
                i = j * 4 + k
                E.emit("pe", lambda t, k=k: t.transpose(PT[:], VTs[:, k * 128:(k + 1) * 128], C.ident_bf[:]),
                       reads=[r_VTs, C.r_ident_b], writes=[C.rPT])
                E.emit("dve", lambda v, i=i: v.tensor_copy(out=Vp[:, i * 130:i * 130 + 64],
                                                           in_=PT[:, 0:64]),
                       reads=[C.rPT], writes=[r_Vp])
                E.emit("dve", lambda v, i=i: v.tensor_copy(out=Vp[:, i * 130 + 65:i * 130 + 129],
                                                           in_=PT[:, 64:128]),
                       reads=[C.rPT], writes=[r_Vp])
        for qb in range(NQ):
            ntk = (qb + 1) * 4
            qc = slice(qb * 512, (qb + 1) * 512)
            def emit_S(i, qc=qc):
                ib = i % 2
                Ps = PB if ib == 0 else PC
                rS = C.rPB if ib == 0 else C.rPC
                E.emit("pe", lambda t, i=i, Ps=Ps, qc=qc: [t.matmul(
                           Ps[:, 0:512], KT[0:64, i * 128:(i + 1) * 128], QT[0:64, qc],
                           start=True, stop=True, tile_position=(0, 0)),
                       t.matmul(
                           Ps[:, 512:1024], KT[64:128, i * 128:(i + 1) * 128], QT[64:128, qc],
                           start=True, stop=True, tile_position=(64, 0))],
                       reads=[r_KT, r_QT], writes=[rS[0], rS[1]])
            def emit_elem(i, qb=qb):
                ib = i % 2
                Ps = PB if ib == 0 else PC
                rS = C.rPB if ib == 0 else C.rPC
                E.emit("act", lambda s, Ps=Ps, ib=ib: s.activation(
                           out=Pt[ib][:], in_=Ps[:, 0:1024], func=AF.Exp,
                           scale=CT.f32(0.125), bias=CT.f32(-SHIFT)),
                       reads=[rS[0], rS[1], C.r_const], writes=[r_Pt[ib]])
                coff = i * 128 - qb * 512
                if coff >= 0:
                    ci = coff // 128
                    for half in range(2):
                        E.emit("dve", lambda v, ib=ib, ci=ci, half=half: v.tensor_tensor(
                                   out=Pt[ib][:, half * 512:(half + 1) * 512],
                                   in0=Pt[ib][:, half * 512:(half + 1) * 512],
                                   in1=C.masks[:, ci * 512:(ci + 1) * 512], op=ALU.mult),
                               reads=[C.r_masks], writes=[r_Pt[ib]])
            def emit_PV(i, ntk=ntk):
                ib = i % 2
                E.emit("pe", lambda t, i=i, ib=ib, ntk=ntk: t.matmul(
                           PA[0:65, 0:512], Vp[:, i * 130:i * 130 + 65], Pt[ib][:, 0:512],
                           start=(i == 0), stop=(i == ntk - 1)),
                       reads=[r_Vp, r_Pt[ib]], writes=[C.rPA[0]])
                E.emit("pe", lambda t, i=i, ib=ib, ntk=ntk: t.matmul(
                           PA[0:65, 512:1024], Vp[:, i * 130 + 65:i * 130 + 130], Pt[ib][:, 512:1024],
                           start=(i == 0), stop=(i == ntk - 1)),
                       reads=[r_Vp, r_Pt[ib]], writes=[C.rPA[1]])
            for i in range(ntk):
                emit_S(i)
                if i > 0:
                    emit_elem(i - 1)
                    emit_PV(i - 1)
            emit_elem(ntk - 1)
            emit_PV(ntk - 1)
            E.emit("act", lambda s: s.activation(out=recrow[:], in_=PA[64:65, 0:1024], func=AF.Ln),
                   reads=[C.rPA[0], C.rPA[1]], writes=[r_rec])
            E.emit("act", lambda s: s.activation(out=recrow[:], in_=recrow[:], func=AF.Exp,
                                                 scale=CT.f32(-1.0)),
                   reads=[r_rec, C.r_const], writes=[r_rec])
            E.emit("pe", lambda t: t.matmul(PB[:, 0:512], C.ones_row_f32[0:1, :], recrow[0:1, 0:512],
                                            start=True, stop=True),
                   reads=[r_rec, C.r_ones_row], writes=[C.rPB[0]])
            E.emit("pe", lambda t: t.matmul(PB[:, 512:1024], C.ones_row_f32[0:1, :], recrow[0:1, 512:1024],
                                            start=True, stop=True),
                   reads=[r_rec, C.r_ones_row], writes=[C.rPB[1]])
            E.emit("dve", lambda v: v.tensor_copy(out=rb[:], in_=PB[:, 0:1024]),
                   reads=[C.rPB[0], C.rPB[1]], writes=[r_rb])
            E.emit("dve", lambda v, hp=hp, qb=qb: v.tensor_tensor(
                       out=attnO[0:64, hp * T + qb * 512: hp * T + (qb + 1) * 512],
                       in0=PA[0:64, 0:512], in1=rb[0:64, 0:512], op=ALU.mult),
                   reads=[C.rPA[0], r_rb], writes=[r_attnO])
            E.emit("dve", lambda v, hp=hp, qb=qb: v.tensor_tensor(
                       out=attnO[64:128, hp * T + qb * 512: hp * T + (qb + 1) * 512],
                       in0=PA[0:64, 512:1024], in1=rb[0:64, 512:1024], op=ALU.mult),
                   reads=[C.rPA[1], r_rb], writes=[r_attnO])

    load_w(C, W, "w_proj", wstage, r_wstage, slice(0, D), "act")
    ybuf, r_ybuf = C.slabA, C.r_slabA
    sq, r_sq = C.slabB, C.r_slabB
    for j in range(NT):
        tb, r_tb = C.tslab[j % 2], C.r_tslab[j % 2]
        E.dma("sync", lambda s, j=j: s.dma_start(out=sb_slab(tb[:]), in_=ap_slab(tT_hbm, j)),
              reads=[r_tT], writes=[r_tb])
        for dc in range(NCH):
            h = dc % 2
            E.emit("pe", lambda t, dc=dc, j=j, h=h: [t.matmul(
                       PC[:, h * 512:(h + 1) * 512],
                       wstage[:, c * D + dc * 128: c * D + (dc + 1) * 128],
                       attnO[:, c * T + j * 512: c * T + (j + 1) * 512],
                       start=(c == 0), stop=(c == NCH - 1)) for c in range(NCH)],
                   reads=[r_wstage, r_attnO], writes=[C.rPC[h]])
            E.emit("dve", lambda v, dc=dc, h=h, tb=tb: v.scalar_tensor_tensor(
                       out=ybuf[:, dc * 512:(dc + 1) * 512], in0=PC[:, h * 512:(h + 1) * 512],
                       scalar=W["bpc"][:, dc:dc + 1], in1=tb[:, dc * 512:(dc + 1) * 512],
                       op0=ALU.add, op1=ALU.add),
                   reads=[C.rPC[h], W["r_bpc"], r_tb], writes=[r_ybuf])
        emit_lnstats(C, ybuf, r_ybuf, 1e-5)
        for dc in range(NCH):
            sl = slice(dc * 512, (dc + 1) * 512)
            E.emit("dve", lambda v, sl=sl: v.tensor_tensor(out=sq[:, sl], in0=ybuf[:, sl],
                                                           in1=PB[:, 0:512], op=ALU.subtract),
                   reads=[r_ybuf, C.rPB[0]], writes=[r_sq])
            E.emit("dve", lambda v, sl=sl: v.tensor_tensor(out=sq[:, sl], in0=sq[:, sl],
                                                           in1=PB[:, 512:1024], op=ALU.mult),
                   reads=[r_sq, C.rPB[1]], writes=[r_sq])
            E.emit("dve", lambda v, sl=sl, dc=dc: v.tensor_scalar(
                       out=sq[:, sl], in0=sq[:, sl], scalar1=W["gammac"][:, dc:dc + 1],
                       scalar2=W["betac"][:, dc:dc + 1], op0=ALU.mult, op1=ALU.add),
                   reads=[r_sq, W["r_gammac"], W["r_betac"]], writes=[r_sq])
        E.dma("sync", lambda s, j=j: s.dma_start(out=sb_slab(ybuf[:]), in_=ap_slab(xT_in, j)),
              reads=[r_xin], writes=[r_ybuf])
        for dc in range(NCH):
            sl = slice(dc * 512, (dc + 1) * 512)
            E.emit("dve", lambda v, sl=sl: v.tensor_tensor(out=ybuf[:, sl], in0=sq[:, sl],
                                                           in1=ybuf[:, sl], op=ALU.add),
                   reads=[r_sq, r_ybuf], writes=[r_ybuf])
        E.dma("pool", lambda g, j=j: g.dma_start(out=ap_slab(xT_out, j), in_=sb_slab(ybuf[:])),
              reads=[r_ybuf], writes=[r_xout], dest=r_xout)
        emit_ep_ln(C, ybuf, r_ybuf, j, tT_next, r_tTn)


def emit_moe(C, xT_in, r_xin, xT_out, r_xout, tT_hbm, r_tT, W, tag, tT_next=None, r_tTn=None, skip_ln=False):
    E, nc, CT, T, NT = C.E, C.nc, C.CT, C.T, C.NT
    M, NM = 1408, 11
    NE = 8
    PA, PB, PC, PL = C.PA, C.PB, C.PC, C.PL
    if not skip_ln:
        emit_ln_to_tT(C, xT_in, r_xin, tT_hbm, r_tT)

    wgate = C.sb(tag + "wgate", [128, NCH * 8], BF16); r_wgate = E.dma_res(tag + "wgate")
    load_weight_lhsT(C, W["w_gate"], W["r_w_gate"], wgate, r_wgate, slice(0, 8), "pool")
    NTT = T // 128
    gbuf = C.sb(tag + "gbuf", [128, NTT * 8], F32); r_gbuf = E.res(tag + "gbuf")
    gT = C.sb(tag + "gT", [8, T], BF16); r_gT = E.res(tag + "gT")
    selmat = C.sb(tag + "sel", [8, 8 * 128], BF16); r_sel = E.dma_res(tag + "sel")
    sel_in = C.nc.dram_tensor(tag + "sel_in", [8, 8 * 128], F32, kind="ExternalInput").ap()
    E.dma("pool", lambda g: g.dma_start(out=selmat[:], in_=sel_in), writes=[r_sel])
    selh = np.zeros((8, 8, 128), np.float32)
    for e in range(8):
        selh[e, e, :] = 1.0
    C.host_consts[tag + "sel_in"] = np.ascontiguousarray(selh.transpose(1, 0, 2).reshape(8, 8 * 128))
    grow = C.sb(tag + "grow", [128, 8 * 4], F32); r_grow = E.res(tag + "grow")
    g_bc = C.sb(tag + "g_bc", [128, 512], F32); r_gbc = E.res(tag + "g_bc")
    for j in range(NT):
        tb, r_tb = C.tslab[j % 2], C.r_tslab[j % 2]
        E.dma("sync", lambda s, j=j: s.dma_start(out=sb_slab(tb[:]), in_=ap_slab(tT_hbm, j)),
              reads=[r_tT], writes=[r_tb])
        for k in range(4):
            tt = j * 4 + k
            E.emit("pe", lambda t, tb=tb, k=k: [t.matmul(
                       PL[:, 0:8], tb[:, c * 512 + k * 128: c * 512 + (k + 1) * 128],
                       wgate[:, c * 8:(c + 1) * 8],
                       start=(c == 0), stop=(c == NCH - 1)) for c in range(NCH)],
                   reads=[r_tb, r_wgate], writes=[C.rPL])
            sc = gbuf[:, tt * 8:(tt + 1) * 8]
            E.emit("dve", lambda v, sc=sc: v.tensor_copy(out=sc, in_=PL[:, 0:8]),
                   reads=[C.rPL], writes=[r_gbuf])
            m1 = grow[:, 0:1]; m2 = grow[:, 1:2]; p1 = grow[:, 2:3]; p2 = grow[:, 3:4]
            mask1 = grow[:, 8:16]; sp = grow[:, 16:24]
            E.emit("dve", lambda v, sc=sc, m1=m1: v.tensor_reduce(out=m1, in_=sc, axis=AX.X, op=ALU.max),
                   reads=[r_gbuf], writes=[r_grow])
            E.emit("dve", lambda v, sc=sc: v.tensor_scalar(out=mask1, in0=sc, scalar1=m1,
                                                           scalar2=None, op0=ALU.is_equal),
                   reads=[r_gbuf, r_grow], writes=[r_grow])
            E.emit("dve", lambda v, sc=sc: v.scalar_tensor_tensor(out=sp, in0=mask1, scalar=-1e30,
                                                                  in1=sc, op0=ALU.mult, op1=ALU.add),
                   reads=[r_grow, r_gbuf], writes=[r_grow])
            E.emit("dve", lambda v: v.tensor_reduce(out=m2, in_=sp, axis=AX.X, op=ALU.max),
                   reads=[r_grow], writes=[r_grow])
            E.emit("dve", lambda v: v.tensor_tensor(out=grow[:, 4:5], in0=m1, in1=m2, op=ALU.subtract),
                   reads=[r_grow], writes=[r_grow])
            E.emit("act", lambda s: s.activation(out=p1, in_=grow[:, 4:5], func=AF.Sigmoid),
                   reads=[r_grow], writes=[r_grow])
            E.emit("act", lambda s: s.activation(out=p2, in_=grow[:, 4:5], func=AF.Sigmoid,
                                                 scale=CT.f32(-1.0)),
                   reads=[r_grow, C.r_const], writes=[r_grow])
            E.emit("dve", lambda v: v.tensor_scalar(out=sp, in0=sp, scalar1=m2, scalar2=None,
                                                    op0=ALU.is_equal),
                   reads=[r_grow], writes=[r_grow])
            E.emit("dve", lambda v: v.tensor_scalar(out=mask1, in0=mask1, scalar1=p1, scalar2=None,
                                                    op0=ALU.mult),
                   reads=[r_grow], writes=[r_grow])
            E.emit("dve", lambda v, sc=sc: v.scalar_tensor_tensor(out=sc, in0=sp, scalar=p2,
                                                                  in1=mask1, op0=ALU.mult, op1=ALU.add),
                   reads=[r_grow], writes=[r_gbuf])
            E.emit("pe", lambda t, sc=sc: t.transpose(PL[0:8, 0:128], sc, C.ident_f32[:]),
                   reads=[r_gbuf, C.r_ident], writes=[C.rPL])
            E.emit("dve", lambda v, tt=tt: v.tensor_copy(out=gT[:, tt * 128:(tt + 1) * 128],
                                                         in_=PL[0:8, 0:128]),
                   reads=[C.rPL], writes=[r_gT])

    use_bf = "w_g_bf" in W
    wg = C.sb(tag + "wg", [128, NCH * M], BF16); r_wg = E.dma_res(tag + "wg")
    wu = C.sb(tag + "wu", [128, NCH * M], BF16); r_wu = E.dma_res(tag + "wu")
    wd = C.sb(tag + "wd", [128, NM * D], BF16); r_wd = E.dma_res(tag + "wd")
    hbuf = C.sb(tag + "hbuf", [128, NM * 512], BF16); r_hbuf = E.res(tag + "hbuf")
    sg = C.sb(tag + "sg", [128, 512], BF16); r_sg = E.res(tag + "sg")
    tmp = C.sb(tag + "tmp", [128, 512], BF16); r_tmp = E.res(tag + "tmp")
    HTOK = max(512, T // 4)
    NHALF = T // HTOK
    NJ = HTOK // 512
    yacc = C.sb(tag + "yacc", [128, NCH * HTOK], F32); r_yacc = E.dma_res(tag + "yacc")

    for H in range(NHALF):
        src = xT_in[:, H * HTOK:(H + 1) * HTOK].rearrange("(c p) t -> p c t", p=128)
        E.dma("sync", lambda s: s.dma_start(out=yacc[:, 0:NCH * HTOK].rearrange("p (c t) -> p c t", t=HTOK),
                                            in_=src),
              reads=[r_xin], writes=[r_yacc])
        for e in range(NE):
            if use_bf:
                E.dma("act", lambda g, e=e: g.dma_start(
                          out=wg[:].rearrange("p (c m) -> p c m", m=M),
                          in_=W["w_g_bf"][e].rearrange("(c p) m -> p c m", p=128)),
                      reads=[W["r_w_g_bf"]], writes=[r_wg])
                E.dma("sync", lambda g, e=e: g.dma_start(
                          out=wu[:].rearrange("p (c m) -> p c m", m=M),
                          in_=W["w_u_bf"][e].rearrange("(c p) m -> p c m", p=128)),
                      reads=[W["r_w_u_bf"]], writes=[r_wu])
                E.dma("act", lambda g, e=e: g.dma_start(
                          out=wd[:].rearrange("p (c m) -> p c m", m=D),
                          in_=W["w_d_bf"][e].rearrange("(c p) m -> p c m", p=128)),
                      reads=[W["r_w_d_bf"]], writes=[r_wd])
            else:
                E.dma("pool", lambda g, e=e: g.dma_start(
                          out=wg[:].rearrange("p (c m) -> p c m", m=M),
                          in_=W["w_g"][e].rearrange("(c p) m -> p c m", p=128)),
                      reads=[W["r_w_g"]], writes=[r_wg])
                E.dma("pool", lambda g, e=e: g.dma_start(
                          out=wu[:].rearrange("p (c m) -> p c m", m=M),
                          in_=W["w_u"][e].rearrange("(c p) m -> p c m", p=128)),
                      reads=[W["r_w_u"]], writes=[r_wu])
                E.dma("pool", lambda g, e=e: g.dma_start(
                          out=wd[:].rearrange("p (c m) -> p c m", m=D),
                          in_=W["w_d"][e].rearrange("(c p) m -> p c m", p=128)),
                      reads=[W["r_w_d"]], writes=[r_wd])
            for j in range(NJ):
                jj = H * NJ + j
                tb, r_tb = C.tslab[jj % 2], C.r_tslab[jj % 2]
                E.dma("sync", lambda s, jj=jj: s.dma_start(out=sb_slab(tb[:]), in_=ap_slab(tT_hbm, jj)),
                      reads=[r_tT], writes=[r_tb])
                E.emit("pe", lambda t, e=e, jj=jj: t.matmul(
                           PL[:, 0:512], selmat[:, e * 128:(e + 1) * 128],
                           gT[0:8, jj * 512:(jj + 1) * 512],
                           start=True, stop=True),
                       reads=[r_gT, r_sel], writes=[C.rPL])
                E.emit("dve", lambda v: v.tensor_copy(out=g_bc[:], in_=PL[:, 0:512]),
                       reads=[C.rPL], writes=[r_gbc])
                for mi in range(NM):
                    Ps = PA if mi % 2 == 0 else PC
                    rPs = C.rPA if mi % 2 == 0 else C.rPC
                    E.emit("pe", lambda t, mi=mi, tb=tb, Ps=Ps: [t.matmul(
                               Ps[:, 0:512], wg[:, c * M + mi * 128: c * M + (mi + 1) * 128],
                               tb[:, c * 512:(c + 1) * 512],
                               start=(c == 0), stop=(c == NCH - 1)) for c in range(NCH)],
                           reads=[r_wg, r_tb], writes=[rPs[0]])
                    E.emit("pe", lambda t, mi=mi, tb=tb, Ps=Ps: [t.matmul(
                               Ps[:, 512:1024], wu[:, c * M + mi * 128: c * M + (mi + 1) * 128],
                               tb[:, c * 512:(c + 1) * 512],
                               start=(c == 0), stop=(c == NCH - 1)) for c in range(NCH)],
                           reads=[r_wu, r_tb], writes=[rPs[1]])
                    E.emit("act", lambda s, Ps=Ps: s.activation(out=sg[:], in_=Ps[:, 0:512], func=AF.Silu),
                           reads=[rPs[0]], writes=[r_sg])
                    E.emit("dve", lambda v, Ps=Ps: v.tensor_tensor(out=tmp[:], in0=Ps[:, 512:1024],
                                                            in1=g_bc[:], op=ALU.mult),
                           reads=[rPs[1], r_gbc], writes=[r_tmp])
                    E.emit("dve", lambda v, mi=mi: v.tensor_tensor(
                               out=hbuf[:, mi * 512:(mi + 1) * 512], in0=sg[:], in1=tmp[:], op=ALU.mult),
                           reads=[r_sg, r_tmp], writes=[r_hbuf])
                for dc in range(NCH):
                    h2 = dc % 2
                    E.emit("pe", lambda t, dc=dc, h2=h2: [t.matmul(
                               PB[:, h2 * 512:(h2 + 1) * 512],
                               wd[:, mi * D + dc * 128: mi * D + (dc + 1) * 128],
                               hbuf[:, mi * 512:(mi + 1) * 512],
                               start=(mi == 0), stop=(mi == NM - 1)) for mi in range(NM)],
                           reads=[r_wd, r_hbuf], writes=[C.rPB[h2]])
                    ysl = slice(dc * HTOK + j * 512, dc * HTOK + (j + 1) * 512)
                    E.emit("dve", lambda v, ysl=ysl, h2=h2: v.tensor_tensor(
                               out=yacc[:, ysl], in0=yacc[:, ysl], in1=PB[:, h2 * 512:(h2 + 1) * 512],
                               op=ALU.add),
                           reads=[C.rPB[h2]], writes=[r_yacc])
        dst = xT_out[:, H * HTOK:(H + 1) * HTOK].rearrange("(c p) t -> p c t", p=128)
        E.dma("pool", lambda g: g.dma_start(out=dst,
                                            in_=yacc[:, 0:NCH * HTOK].rearrange("p (c t) -> p c t", t=HTOK)),
              reads=[r_yacc], writes=[r_xout], dest=r_xout)
        for js in range(NJ):
            chunks = [yacc[:, c * HTOK + js * 512: c * HTOK + (js + 1) * 512] for c in range(NCH)]
            emit_ep_ln(C, yacc, r_yacc, H * NJ + js, tT_next, r_tTn, chunks=chunks)


def emit_attn_b0(C, xT_in, r_xin, xT_out, r_xout, tT_hbm, r_tT, W, tT_next=None, r_tTn=None, skip_ln=False):
    E, nc, CT, T, NT = C.E, C.nc, C.CT, C.T, C.NT
    NQ = T // 512
    NTK = T // 128
    SHIFT = 50.0
    EPS = 1.1920929e-07
    PA, PB, PC, PL, PT = C.PA, C.PB, C.PC, C.PL, C.PT
    C.ensure_masks()
    C.ensure_hsel()
    for nm in ("w_qT", "w_kT", "w_vT", "w_oT"):
        precast(C, W, nm)
    if not skip_ln:
        emit_ln_to_tT(C, xT_in, r_xin, tT_hbm, r_tT)

    QT2 = [C.sb(f"ab_QT{i}", [128, T], BF16) for i in range(2)]
    r_QT2 = [E.res(f"ab_QT{i}") for i in range(2)]
    KT2 = [C.sb(f"ab_KT{i}", [128, T], BF16) for i in range(2)]
    r_KT2 = [E.res(f"ab_KT{i}") for i in range(2)]
    Vp2 = [C.sb(f"ab_Vp{i}", [128, NTK * 130], BF16) for i in range(2)]
    r_Vp2 = [E.res(f"ab_Vp{i}") for i in range(2)]
    VTs = C.sb("ab_VTs", [128, 512], BF16); r_VTs = E.res("ab_VTs")
    qraw = C.sb("ab_qraw", [128, 512], BF16); r_qraw = E.res("ab_qraw")
    Pt = [C.sb(f"ab_P{i}", [128, 1024], BF16) for i in range(2)]
    r_Pt = [E.res(f"ab_P{i}") for i in range(2)]
    th = C.sb("ab_th", [128, 1024], F32); r_th = E.res("ab_th")
    th2 = C.sb("ab_th2", [128, 1024], F32); r_th2 = E.res("ab_th2")
    aO_hbm = nc.dram_tensor("ab_aO_hbm", [D, T], BF16, kind="Internal").ap()
    r_aOh = E.dma_res("ab_aOh")
    aOs = C.sb("ab_aOs", [128, 512], BF16); r_aOs = E.res("ab_aOs")
    recrow = C.sb("ab_recrow", [1, 1024], F32); r_rec = E.res("ab_recrow")
    rb = C.sb("ab_rb", [128, 1024], F32); r_rb = E.res("ab_rb")
    rmsq = C.sb("ab_rmsq", [128, 512], F32); r_rmsq = E.res("ab_rmsq")
    rmsr = C.sb("ab_rmsr", [2, 512], F32); r_rmsr = E.res("ab_rmsr")
    qksum2 = [C.sb(f"ab_qksum{i}", [66, T], F32) for i in range(2)]
    r_qksum2 = [E.res(f"ab_qksum{i}") for i in range(2)]
    for _p in range(2):
        E.emit("dve", lambda v, _p=_p: v.memset(qksum2[_p][:], 1.0), writes=[r_qksum2[_p]])
    rmsbc = C.sb("ab_rmsbc", [128, 512], F32); r_rmsbc = E.res("ab_rmsbc")
    wstage = C.sb("wstage", [128, NCH * D], BF16)
    r_ws = [E.dma_res("ab_ws0"), E.dma_res("ab_ws1")]

    for par in range(2):
        for i in range(NTK):
            E.emit("dve", lambda v, i=i, par=par: v.memset(Vp2[par][:, i * 130 + 64:i * 130 + 65], 1.0),
                   writes=[r_Vp2[par]])
            E.emit("dve", lambda v, i=i, par=par: v.memset(Vp2[par][:, i * 130 + 129:i * 130 + 130], 1.0),
                   writes=[r_Vp2[par]])

    def prep_gen(hp):
        par = hp % 2
        c0 = hp * 128
        ws = wstage[:, par * NCH * 512:(par + 1) * NCH * 512]
        r_w = r_ws[par]
        QT, KT, Vp = QT2[par], KT2[par], Vp2[par]
        r_QT, r_KT, r_Vp = r_QT2[par], r_KT2[par], r_Vp2[par]
        wq = ws[:, 0:NCH * 128]
        wk = ws[:, NCH * 128:2 * NCH * 128]
        wv = ws[:, 2 * NCH * 128:3 * NCH * 128]
        for nm, dsts in (("w_qT", wq), ("w_kT", wk), ("w_vT", wv)):
            bf = W[nm + "_bf"]
            sap = bf[:, c0:c0 + 128].rearrange("(c p) m -> p c m", p=128)
            E.dma("act", lambda g, dsts=dsts, sap=sap: g.dma_start(
                      out=dsts.rearrange("p (c m) -> p c m", m=128), in_=sap),
                  reads=[W["r_" + nm + "_bf"]], writes=[r_w], dest=r_w)
        for j in range(NT):
            tb, r_tb = C.tslab[j % 2], C.r_tslab[j % 2]
            E.dma("sync", lambda s, j=j: s.dma_start(out=sb_slab(tb[:]), in_=ap_slab(tT_hbm, j)),
                  reads=[r_tT], writes=[r_tb])
            for wi, (wsb, dst, r_dst, sro) in enumerate((
                    (wq, QT, r_QT, 0), (wk, KT, r_KT, 64))):
                smb, r_smb = qksum2[par], r_qksum2[par]
                E.emit("pe", lambda t, wsb=wsb, tb=tb: [t.matmul(
                           PL[:, 0:512], wsb[:, c * 128:(c + 1) * 128],
                           tb[:, c * 512:(c + 1) * 512],
                           start=(c == 0), stop=(c == NCH - 1)) for c in range(NCH)],
                       reads=[r_w, r_tb], writes=[C.rPL])
                E.emit("dve", lambda v, dst=dst, j=j: v.tensor_copy(
                           out=dst[:, j * 512:(j + 1) * 512], in_=PL[:, 0:512]),
                       reads=[C.rPL], writes=[r_dst])
                E.emit("act", lambda s, dst=dst, j=j: s.activation(
                           out=rmsq[:], in_=dst[:, j * 512:(j + 1) * 512], func=AF.Square),
                       reads=[r_dst], writes=[r_rmsq])
                E.emit("pe", lambda t: t.matmul(PL[0:2, 0:512], C.hsel[:, 0:2], rmsq[:],
                                                start=True, stop=True),
                       reads=[r_rmsq, C.r_hsel], writes=[C.rPL])
                E.emit("dve", lambda v, smb=smb, j=j, sro=sro: v.tensor_copy(
                           out=smb[sro:sro + 2, j * 512:(j + 1) * 512], in_=PL[0:2, 0:512]),
                       reads=[C.rPL], writes=[r_smb])
                yield
            E.emit("pe", lambda t, tb=tb: [t.matmul(
                       PL[:, 0:512], wv[:, c * 128:(c + 1) * 128], tb[:, c * 512:(c + 1) * 512],
                       start=(c == 0), stop=(c == NCH - 1)) for c in range(NCH)],
                   reads=[r_w, r_tb], writes=[C.rPL])
            E.emit("dve", lambda v: v.tensor_copy(out=VTs[:], in_=PL[:, 0:512]),
                   reads=[C.rPL], writes=[r_VTs])
            for k in range(4):
                i = j * 4 + k
                E.emit("pe", lambda t, k=k: t.transpose(PT[:], VTs[:, k * 128:(k + 1) * 128], C.ident_bf[:]),
                       reads=[r_VTs, C.r_ident_b], writes=[C.rPT])
                E.emit("dve", lambda v, i=i, Vp=Vp: v.tensor_copy(out=Vp[:, i * 130:i * 130 + 64],
                                                           in_=PT[:, 0:64]),
                       reads=[C.rPT], writes=[r_Vp])
                E.emit("dve", lambda v, i=i, Vp=Vp: v.tensor_copy(out=Vp[:, i * 130 + 65:i * 130 + 129],
                                                           in_=PT[:, 64:128]),
                       reads=[C.rPT], writes=[r_Vp])
            yield
        smb, r_smb = qksum2[par], r_qksum2[par]
        E.emit("act", lambda s, smb=smb: s.activation(out=smb[:], in_=smb[:], func=AF.Ln,
                                                      scale=CT.f32(1.0 / 64.0), bias=CT.f32(EPS)),
               reads=[r_smb], writes=[r_smb])
        E.emit("act", lambda s, smb=smb: s.activation(out=smb[:], in_=smb[:], func=AF.Exp,
                                                      scale=CT.f32(-0.5)),
               reads=[r_smb], writes=[r_smb])
        yield
        for j in range(NT):
            for (dst, r_dst, sro) in ((QT, r_QT, 0), (KT, r_KT, 64)):
                smb, r_smb = qksum2[par], r_qksum2[par]
                E.emit("pe", lambda t, smb=smb, j=j, sro=sro: t.matmul(
                           PL[:, 0:512], C.hsel2[0:2, :] if sro == 0 else C.hsel2b[64:66, :],
                           smb[sro:sro + 2, j * 512:(j + 1) * 512],
                           start=True, stop=True),
                       reads=[r_smb, C.r_hsel], writes=[C.rPL])
                E.emit("dve", lambda v: v.tensor_copy(out=rmsbc[:], in_=PL[:, 0:512]),
                       reads=[C.rPL], writes=[r_rmsbc])
                E.emit("dve", lambda v, dst=dst, j=j: v.tensor_tensor(
                           out=dst[:, j * 512:(j + 1) * 512], in0=dst[:, j * 512:(j + 1) * 512],
                           in1=rmsbc[:], op=ALU.mult),
                       reads=[r_rmsbc], writes=[r_dst])
            yield

    def attention(hp, feeder):
        par = hp % 2
        QT, KT, Vp = QT2[par], KT2[par], Vp2[par]
        r_QT, r_KT, r_Vp = r_QT2[par], r_KT2[par], r_Vp2[par]
        for qb in range(NQ):
            ntk = (qb + 1) * 4
            qc = slice(qb * 512, (qb + 1) * 512)
            def emit_S(i, qc=qc):
                ib = i % 2
                Ps = PB if ib == 0 else PC
                rS = C.rPB if ib == 0 else C.rPC
                E.emit("pe", lambda t, i=i, Ps=Ps, qc=qc: [t.matmul(
                           Ps[:, 0:512], KT[0:64, i * 128:(i + 1) * 128], QT[0:64, qc],
                           start=True, stop=True, tile_position=(0, 0)),
                       t.matmul(
                           Ps[:, 512:1024], KT[64:128, i * 128:(i + 1) * 128], QT[64:128, qc],
                           start=True, stop=True, tile_position=(64, 0))],
                       reads=[r_KT, r_QT], writes=[rS[0], rS[1]])
            def emit_elem(i, qb=qb):
                ib = i % 2
                Ps = PB if ib == 0 else PC
                rS = C.rPB if ib == 0 else C.rPC
                thb = th if ib == 0 else th2
                r_thb = r_th if ib == 0 else r_th2
                E.emit("act", lambda s, Ps=Ps, thb=thb: s.activation(
                           out=thb[:], in_=Ps[:, 0:1024], func=AF.Tanh, scale=CT.f32(1.0 / 400.0)),
                       reads=[rS[0], rS[1], C.r_const], writes=[r_thb])
                E.emit("act", lambda s, ib=ib, thb=thb: s.activation(
                           out=Pt[ib][:], in_=thb[:], func=AF.Exp,
                           scale=CT.f32(50.0), bias=CT.f32(-SHIFT)),
                       reads=[r_thb, C.r_const], writes=[r_Pt[ib]])
                coff = i * 128 - qb * 512
                if coff >= 0:
                    ci = coff // 128
                    for half in range(2):
                        E.emit("dve", lambda v, ib=ib, ci=ci, half=half: v.tensor_tensor(
                                   out=Pt[ib][:, half * 512:(half + 1) * 512],
                                   in0=Pt[ib][:, half * 512:(half + 1) * 512],
                                   in1=C.masks[:, ci * 512:(ci + 1) * 512], op=ALU.mult),
                               reads=[C.r_masks], writes=[r_Pt[ib]])
            def emit_PV(i, ntk=ntk):
                ib = i % 2
                E.emit("pe", lambda t, i=i, ib=ib, ntk=ntk: t.matmul(
                           PA[0:65, 0:512], Vp[:, i * 130:i * 130 + 65], Pt[ib][:, 0:512],
                           start=(i == 0), stop=(i == ntk - 1)),
                       reads=[r_Vp, r_Pt[ib]], writes=[C.rPA[0]])
                E.emit("pe", lambda t, i=i, ib=ib, ntk=ntk: t.matmul(
                           PA[0:65, 512:1024], Vp[:, i * 130 + 65:i * 130 + 130], Pt[ib][:, 512:1024],
                           start=(i == 0), stop=(i == ntk - 1)),
                       reads=[r_Vp, r_Pt[ib]], writes=[C.rPA[1]])
            for i in range(ntk):
                emit_S(i)
                if i > 0:
                    emit_elem(i - 1)
                    emit_PV(i - 1)
                if i % 2 == 0:
                    feeder()
            emit_elem(ntk - 1)
            emit_PV(ntk - 1)
            E.emit("dve", lambda v: v.reciprocal(out=recrow[:], in_=PA[64:65, 0:1024]),
                   reads=[C.rPA[0], C.rPA[1]], writes=[r_rec])
            E.emit("pe", lambda t: t.matmul(PB[:, 0:512], C.ones_row_f32[0:1, :], recrow[0:1, 0:512],
                                            start=True, stop=True),
                   reads=[r_rec, C.r_ones_row], writes=[C.rPB[0]])
            E.emit("pe", lambda t: t.matmul(PB[:, 512:1024], C.ones_row_f32[0:1, :], recrow[0:1, 512:1024],
                                            start=True, stop=True),
                   reads=[r_rec, C.r_ones_row], writes=[C.rPB[1]])
            E.emit("dve", lambda v: v.tensor_copy(out=rb[:], in_=PB[:, 0:1024]),
                   reads=[C.rPB[0], C.rPB[1]], writes=[r_rb])
            E.emit("dve", lambda v: v.tensor_tensor(
                       out=aOs[0:64, :], in0=PA[0:64, 0:512], in1=rb[0:64, 0:512], op=ALU.mult),
                   reads=[C.rPA[0], r_rb], writes=[r_aOs])
            E.emit("dve", lambda v: v.tensor_tensor(
                       out=aOs[64:128, :], in0=PA[0:64, 512:1024], in1=rb[0:64, 512:1024], op=ALU.mult),
                   reads=[C.rPA[1], r_rb], writes=[r_aOs])
            E.dma("pool", lambda g, hp=hp, qb=qb: g.dma_start(
                      out=aO_hbm[hp * 128:(hp + 1) * 128, qb * 512:(qb + 1) * 512], in_=aOs[:]),
                  reads=[r_aOs], writes=[r_aOh], dest=r_aOh)

    g = prep_gen(0)
    for _ in g:
        pass
    for hp in range(NH // 2):
        gnext = prep_gen(hp + 1) if hp + 1 < NH // 2 else None
        state = {"cnt": 0}
        def feeder():
            state["cnt"] += 1
            if gnext is not None and state["cnt"] % 3 == 0:
                next(gnext, None)
        attention(hp, feeder)
        if gnext is not None:
            for _ in gnext:
                pass

    load_w(C, W, "w_oT", wstage, r_ws[0], slice(0, D), "act")
    r_ws[0].readers.extend(r_ws[1].readers)
    ybuf, r_ybuf = C.slabA, C.r_slabA
    for j in range(NT):
        E.dma("sync", lambda s, j=j: s.dma_start(out=sb_slab(ybuf[:]), in_=ap_slab(xT_in, j)),
              reads=[r_xin], writes=[r_ybuf])
        aos2, r_aos2 = C.tslab[j % 2], C.r_tslab[j % 2]
        E.dma("sync", lambda s, j=j, aos2=aos2: s.dma_start(out=sb_slab(aos2[:]), in_=ap_slab(aO_hbm, j)),
              reads=[r_aOh], writes=[r_aos2])
        for dc in range(NCH):
            h = dc % 2
            E.emit("pe", lambda t, dc=dc, h=h, aos2=aos2: [t.matmul(
                       PC[:, h * 512:(h + 1) * 512],
                       wstage[:, c * D + dc * 128: c * D + (dc + 1) * 128],
                       aos2[:, c * 512:(c + 1) * 512],
                       start=(c == 0), stop=(c == NCH - 1)) for c in range(NCH)],
                   reads=[r_ws[0], r_aos2], writes=[C.rPC[h]])
            E.emit("dve", lambda v, dc=dc, h=h: v.tensor_tensor(
                       out=ybuf[:, dc * 512:(dc + 1) * 512], in0=ybuf[:, dc * 512:(dc + 1) * 512],
                       in1=PC[:, h * 512:(h + 1) * 512], op=ALU.add),
                   reads=[C.rPC[h]], writes=[r_ybuf])
        E.dma("pool", lambda g, j=j: g.dma_start(out=ap_slab(xT_out, j), in_=sb_slab(ybuf[:])),
              reads=[r_ybuf], writes=[r_xout], dest=r_xout)
        emit_ep_ln(C, ybuf, r_ybuf, j, tT_next, r_tTn)


def emit_attn_a1(C, xT_in, r_xin, xT_out, r_xout, tT_hbm, r_tT, W, tT_next=None, r_tTn=None, skip_ln=False):
    E, nc, CT, T, NT = C.E, C.nc, C.CT, C.T, C.NT
    NQ = T // 512
    NTK = T // 128
    EPS = 1e-4
    PA, PB, PC, PL, PT = C.PA, C.PB, C.PC, C.PL, C.PT
    C.ensure_hsel()
    precast(C, W, "W_in")
    precast(C, W, "W_out")
    if not skip_ln:
        emit_ln_to_tT(C, xT_in, r_xin, tT_hbm, r_tT)

    KTp2 = [C.sb(f"a1_KTp{i}", [128, T], BF16) for i in range(2)]
    r_KTp2 = [E.res(f"a1_KTp{i}") for i in range(2)]
    QTp2 = [C.sb(f"a1_QTp{i}", [128, T], BF16) for i in range(2)]
    r_QTp2 = [E.res(f"a1_QTp{i}") for i in range(2)]
    G_hbm = nc.dram_tensor("a1_G_hbm", [D, T], BF16, kind="Internal").ap()
    r_Gh = E.dma_res("a1_Gh")
    gsl2 = [C.sb(f"a1_gsl{i}", [128, 512], BF16) for i in range(2)]
    r_gsl2 = [E.dma_res(f"a1_gsl{i}") for i in range(2)]
    Vp2 = [C.sb(f"a1_Vp{i}", [128, NTK * 130], BF16) for i in range(2)]
    r_Vp2 = [E.res(f"a1_Vp{i}") for i in range(2)]
    q2A2 = [C.sb(f"a1_q2A{i}", [1, T], BF16) for i in range(2)]
    r_q2A2 = [E.res(f"a1_q2A{i}") for i in range(2)]
    q2B2 = [C.sb(f"a1_q2B{i}", [1, T], BF16) for i in range(2)]
    r_q2B2 = [E.res(f"a1_q2B{i}") for i in range(2)]
    k2m2 = [C.sb(f"a1_k2m{i}", [128, 2 * (T // 128)], F32) for i in range(2)]
    r_k2m2 = [E.res(f"a1_k2m{i}") for i in range(2)]
    q2bc = C.sb("a1_q2bc", [128, 1024], BF16); r_q2bc = E.res("a1_q2bc")
    ktok = C.sb("a1_ktok", [128, 128], BF16); r_ktok = E.res("a1_ktok")
    k2sq = C.sb("a1_k2sq", [128, 128], BF16); r_k2sq = E.res("a1_k2sq")
    qkslab = C.sb("a1_qks", [128, 512], BF16); r_qks = E.res("a1_qks")
    sqs = C.sb("a1_sqs", [128, 512], F32); r_sqs = E.res("a1_sqs")
    HO_hbm = nc.dram_tensor("a1_HO_hbm", [D, T], BF16, kind="Internal").ap()
    r_HOh = E.dma_res("a1_HOh")
    HO = C.sb("a1_HO", [128, 512], BF16); r_HO = E.res("a1_HO")

    Pt = [C.sb(f"a1_P{i}", [128, 2048], BF16) for i in range(2)]
    r_Pt = [E.res(f"a1_P{i}") for i in range(2)]
    d2c = [C.sb(f"a1_d2c{i}", [128, 2048], BF16) for i in range(2)]
    r_d2c = [E.res(f"a1_d2c{i}") for i in range(2)]
    Lb = C.sb("a1_Lb", [128, 2048], F32); r_Lb = E.res("a1_Lb")
    Db = C.sb("a1_Db", [128, 512], BF16); r_Db = E.res("a1_Db")
    recrow = C.sb("a1_recrow", [1, 512], F32); r_rec = E.res("a1_recrow")
    rb = C.sb("a1_rb", [128, 1024], F32); r_rb = E.res("a1_rb")
    wstage = C.sb("wstage", [128, NCH * D], BF16)
    r_ws = [E.dma_res("a1_ws0"), E.dma_res("a1_ws1")]
    binc = C.sb("a1_binc", [128, 32], F32); r_binc = E.dma_res("a1_binc")
    boutc = C.sb("a1_boutc", [128, NCH], F32); r_boutc = E.dma_res("a1_boutc")
    binc_in = nc.dram_tensor("a1_binc_in", [128, 32], F32, kind="ExternalInput").ap()
    boutc_in = nc.dram_tensor("a1_boutc_in", [128, NCH], F32, kind="ExternalInput").ap()
    E.dma("sync", lambda s: s.dma_start(out=binc[:], in_=binc_in), writes=[r_binc])
    E.dma("sync", lambda s: s.dma_start(out=boutc[:], in_=boutc_in), writes=[r_boutc])

    for par in range(2):
        for i in range(NTK):
            E.emit("dve", lambda v, i=i, par=par: v.memset(Vp2[par][:, i * 130 + 64:i * 130 + 65], 1.0),
                   writes=[r_Vp2[par]])
            E.emit("dve", lambda v, i=i, par=par: v.memset(Vp2[par][:, i * 130 + 129:i * 130 + 130], 1.0),
                   writes=[r_Vp2[par]])

    W_in, r_Win = W["W_in"], W["r_W_in"]

    def prep_gen(hp):
        par = hp % 2
        ws = wstage[:, par * NCH * 512:(par + 1) * NCH * 512]
        r_w = r_ws[par]
        KTp, QTp, Vp = KTp2[par], QTp2[par], Vp2[par]
        r_KTp, r_QTp, r_Vp = r_KTp2[par], r_QTp2[par], r_Vp2[par]
        q2A, q2B, k2m = q2A2[par], q2B2[par], k2m2[par]
        r_q2A, r_q2B, r_k2m = r_q2A2[par], r_q2B2[par], r_k2m2[par]
        src_ap = W["W_in_bf"][:, 512 * hp:512 * hp + 512].rearrange("(c p) m -> p c m", p=128)
        E.dma("act", lambda g: g.dma_start(out=ws.rearrange("p (c m) -> p c m", m=512), in_=src_ap),
              reads=[W["r_W_in_bf"]], writes=[r_w])
        for j in range(NT):
            tb, r_tb = C.tslab[j % 2], C.r_tslab[j % 2]
            E.dma("sync", lambda s, j=j: s.dma_start(out=sb_slab(tb[:]), in_=ap_slab(tT_hbm, j)),
                  reads=[r_tT], writes=[r_tb])
            for sub in range(4):
                cg = 4 * hp + sub
                hb = 0 if sub < 2 else 64
                E.emit("pe", lambda t, tb=tb, sub=sub, ws=ws: [t.matmul(
                           PL[:, 0:512], ws[:, c * 512 + sub * 128: c * 512 + (sub + 1) * 128],
                           tb[:, c * 512:(c + 1) * 512],
                           start=(c == 0), stop=(c == NCH - 1)) for c in range(NCH)],
                       reads=[r_w, r_tb], writes=[C.rPL])
                E.emit("dve", lambda v, cg=cg: v.tensor_scalar(
                           out=qkslab[:], in0=PL[:, 0:512], scalar1=binc[:, cg:cg + 1],
                           scalar2=None, op0=ALU.add),
                       reads=[C.rPL, r_binc], writes=[r_qks])
                if sub % 2 == 0:
                    E.emit("dve", lambda v, hb=hb, j=j, QTp=QTp: v.tensor_copy(
                               out=QTp[hb:hb + 64, j * 512:(j + 1) * 512], in_=qkslab[0:64, :]),
                           reads=[r_qks], writes=[r_QTp])
                    E.emit("dve", lambda v, hb=hb, j=j, KTp=KTp: v.tensor_copy(
                               out=KTp[hb:hb + 64, j * 512:(j + 1) * 512], in_=qkslab[64:128, :]),
                           reads=[r_qks], writes=[r_KTp])
                    E.emit("act", lambda s: s.activation(out=sqs[0:64, :], in_=qkslab[0:64, :], func=AF.Square),
                           reads=[r_qks], writes=[r_sqs])
                    E.emit("pe", lambda t: t.matmul(PL[0:1, 0:512], C.hsel[0:64, 0:1], sqs[0:64, :],
                                                    start=True, stop=True),
                           reads=[r_sqs, C.r_hsel], writes=[C.rPL])
                    q2buf, r_q2x = (q2A, r_q2A) if hb == 0 else (q2B, r_q2B)
                    E.emit("dve", lambda v, j=j, q2buf=q2buf: v.tensor_scalar(
                               out=q2buf[0:1, j * 512:(j + 1) * 512], in0=PL[0:1, 0:512],
                               scalar1=-0.5, scalar2=None, op0=ALU.mult),
                           reads=[C.rPL], writes=[r_q2x])
                    for k in range(4):
                        i = j * 4 + k
                        kcol = (0 if hb == 0 else (T // 128)) + i
                        E.emit("pe", lambda t, k=k: t.transpose(PT[:, 0:64],
                                                                qkslab[64:128, k * 128:(k + 1) * 128],
                                                                C.ident_bf[64:128, 64:128]),
                               reads=[r_qks, C.r_ident_b], writes=[C.rPT])
                        E.emit("dve", lambda v: v.tensor_copy(out=ktok[:, 0:64], in_=PT[:, 0:64]),
                               reads=[C.rPT], writes=[r_ktok])
                        E.emit("dve", lambda v: v.scalar_tensor_tensor(out=k2sq[:, 0:64], in0=ktok[:, 0:64],
                                                                scalar=0.5, in1=ktok[:, 0:64],
                                                                op0=ALU.mult, op1=ALU.mult),
                               reads=[r_ktok], writes=[r_k2sq])
                        E.emit("dve", lambda v, kcol=kcol, k2m=k2m: v.tensor_reduce(
                                   out=k2m[:, kcol:kcol + 1],
                                   in_=k2sq[:, 0:64], axis=AX.X, op=ALU.add, negate=True),
                               reads=[r_k2sq], writes=[r_k2m])
                else:
                    E.dma("pool", lambda g, hb=hb, j=j, hp=hp: g.dma_start(
                              out=G_hbm[hp * 128 + hb:hp * 128 + hb + 64, j * 512:(j + 1) * 512],
                              in_=qkslab[64:128, :]),
                          reads=[r_qks], writes=[r_Gh], dest=r_Gh)
                    for k in range(4):
                        i = j * 4 + k
                        off = 0 if hb == 0 else 65
                        E.emit("pe", lambda t, k=k: t.transpose(PT[:, 0:64],
                                                                qkslab[0:64, k * 128:(k + 1) * 128],
                                                                C.ident_bf[0:64, 0:64]),
                               reads=[r_qks, C.r_ident_b], writes=[C.rPT])
                        E.emit("dve", lambda v, i=i, off=off, Vp=Vp: v.tensor_copy(
                                   out=Vp[:, i * 130 + off:i * 130 + off + 64], in_=PT[:, 0:64]),
                               reads=[C.rPT], writes=[r_Vp])
                yield

    def attention(hp, feeder):
        par = hp % 2
        KTp, QTp, Vp = KTp2[par], QTp2[par], Vp2[par]
        r_KTp, r_QTp, r_Vp = r_KTp2[par], r_QTp2[par], r_Vp2[par]
        q2A, q2B, k2m = q2A2[par], q2B2[par], k2m2[par]
        r_q2A, r_q2B, r_k2m = r_q2A2[par], r_q2B2[par], r_k2m2[par]
        NP = NTK // 2

        def q2head(qb):
            qc = slice(qb * 512, (qb + 1) * 512)
            gb, r_gb = gsl2[qb % 2], r_gsl2[qb % 2]
            E.dma("sync", lambda s, hp=hp, qb=qb, gb=gb: s.dma_start(
                      out=gb[:], in_=G_hbm[hp * 128:(hp + 1) * 128, qb * 512:(qb + 1) * 512]),
                  reads=[r_Gh], writes=[r_gb])
            E.emit("pe", lambda t, qc=qc: t.matmul(PL[:, 0:512], C.ones_row_bf[0:1, :], q2A[0:1, qc],
                                                   start=True, stop=True),
                   reads=[r_q2A, C.r_ones_row_b], writes=[C.rPL])
            E.emit("dve", lambda v: v.tensor_copy(out=q2bc[:, 0:512], in_=PL[:, 0:512]),
                   reads=[C.rPL], writes=[r_q2bc])
            E.emit("pe", lambda t, qc=qc: t.matmul(PL[:, 0:512], C.ones_row_bf[0:1, :], q2B[0:1, qc],
                                                   start=True, stop=True),
                   reads=[r_q2B, C.r_ones_row_b], writes=[C.rPL])
            E.emit("dve", lambda v: v.tensor_copy(out=q2bc[:, 512:1024], in_=PL[:, 0:512]),
                   reads=[C.rPL], writes=[r_q2bc])

        def emit_S(qb, i):
            qc = slice(qb * 512, (qb + 1) * 512)
            ib = i % 2
            Ps = PB if ib == 0 else PC
            rS = C.rPB if ib == 0 else C.rPC
            E.emit("pe", lambda t, i=i, Ps=Ps, qc=qc: [
                       t.matmul(Ps[:, 0:512], KTp[0:64, i * 128:(i + 1) * 128], QTp[0:64, qc],
                                start=True, stop=True, tile_position=(0, 0)),
                       t.matmul(Ps[:, 512:1024], KTp[64:128, i * 128:(i + 1) * 128], QTp[64:128, qc],
                                start=True, stop=True, tile_position=(64, 0))],
                   reads=[r_KTp, r_QTp], writes=[rS[0], rS[1]])

        def emit_clamp(n, i):
            ib = i % 2
            ip = n % 2
            quarter = (i % 2) * 1024
            Ps = PB if ib == 0 else PC
            rS = C.rPB if ib == 0 else C.rPC
            for half in range(2):
                kcol = (0 if half == 0 else (T // 128)) + i
                E.emit("dve", lambda v, Ps=Ps, half=half, kcol=kcol:
                           v.scalar_tensor_tensor(
                           out=Db[:, 0:512],
                           in0=Ps[:, half * 512:(half + 1) * 512], scalar=k2m[:, kcol:kcol + 1],
                           in1=q2bc[:, half * 512:(half + 1) * 512], op0=ALU.add, op1=ALU.add),
                       reads=[rS[half], r_k2m, r_q2bc], writes=[r_Db])
                E.emit("dve", lambda v, half=half, ip=ip, quarter=quarter: v.tensor_scalar(
                           out=d2c[ip][:, quarter + half * 512: quarter + (half + 1) * 512],
                           in0=Db[:, 0:512],
                           scalar1=-2.0, scalar2=0.0, op0=ALU.mult, op1=ALU.max),
                       reads=[r_Db], writes=[r_d2c[ip]])

        def emit_chain_block(k):
            for j in (k, k + 1):
                ip = j % 2
                E.emit("act", lambda s, ip=ip: s.activation(out=d2c[ip][:], in_=d2c[ip][:],
                                                            func=AF.Sqrt),
                       reads=[r_d2c[ip]], writes=[r_d2c[ip]])
            for j in (k, k + 1):
                ip = j % 2
                E.emit("act", lambda s, ip=ip: s.activation(out=Lb[:], in_=d2c[ip][:], func=AF.Ln,
                                                            bias=CT.f32(EPS)),
                       reads=[r_d2c[ip], C.r_const], writes=[r_Lb])
                E.emit("act", lambda s, ip=ip: s.activation(out=Pt[ip][:], in_=Lb[:], func=AF.Exp,
                                                            scale=CT.f32(-2.0)),
                       reads=[r_Lb, C.r_const], writes=[r_Pt[ip]])

        def emit_PVpair(n):
            ip = n % 2
            for k in range(2):
                i = (n % NP) * 2 + k
                quarter = k * 1024
                E.emit("pe", lambda t, i=i, ip=ip, quarter=quarter: t.matmul(
                           PA[0:65, 0:512], Vp[:, i * 130:i * 130 + 65],
                           Pt[ip][:, quarter:quarter + 512],
                           start=(i == 0), stop=(i == NTK - 1)),
                       reads=[r_Vp, r_Pt[ip]], writes=[C.rPA[0]])
                E.emit("pe", lambda t, i=i, ip=ip, quarter=quarter: t.matmul(
                           PA[0:65, 512:1024], Vp[:, i * 130 + 65:i * 130 + 130],
                           Pt[ip][:, quarter + 512:quarter + 1024],
                           start=(i == 0), stop=(i == NTK - 1)),
                       reads=[r_Vp, r_Pt[ip]], writes=[C.rPA[1]])

        def norm_tail(qb):
            for hh in range(2):
                E.emit("act", lambda s, hh=hh: s.activation(out=recrow[:], in_=PA[64:65, hh * 512:(hh + 1) * 512],
                                                            func=AF.Ln, bias=CT.f32(EPS)),
                       reads=[C.rPA[hh], C.r_const], writes=[r_rec])
                E.emit("act", lambda s: s.activation(out=recrow[:], in_=recrow[:], func=AF.Exp,
                                                     scale=CT.f32(-1.0)),
                       reads=[r_rec, C.r_const], writes=[r_rec])
                E.emit("pe", lambda t, hh=hh: t.matmul(PB[:, hh * 512:(hh + 1) * 512], C.ones_row_f32[0:1, :],
                                                       recrow[0:1, 0:512], start=True, stop=True),
                       reads=[r_rec, C.r_ones_row], writes=[C.rPB[hh]])
            E.emit("dve", lambda v: v.tensor_copy(out=rb[:], in_=PB[:, 0:1024]),
                   reads=[C.rPB[0], C.rPB[1]], writes=[r_rb])
            E.emit("dve", lambda v: v.tensor_tensor(
                       out=Db[0:64, 0:512],
                       in0=PA[0:64, 0:512], in1=rb[0:64, 0:512], op=ALU.mult),
                   reads=[C.rPA[0], r_rb], writes=[r_Db])
            E.emit("dve", lambda v: v.tensor_tensor(
                       out=Db[64:128, 0:512],
                       in0=PA[0:64, 512:1024], in1=rb[0:64, 512:1024], op=ALU.mult),
                   reads=[C.rPA[1], r_rb], writes=[r_Db])
            gb, r_gb = gsl2[qb % 2], r_gsl2[qb % 2]
            E.emit("dve", lambda v, gb=gb: v.tensor_tensor(
                       out=HO[0:64, :],
                       in0=Db[0:64, 0:512], in1=gb[0:64, :], op=ALU.mult),
                   reads=[r_Db, r_gb], writes=[r_HO])
            E.emit("dve", lambda v, gb=gb: v.tensor_tensor(
                       out=HO[64:128, :],
                       in0=Db[64:128, 0:512], in1=gb[64:128, :], op=ALU.mult),
                   reads=[r_Db, r_gb], writes=[r_HO])
            E.dma("pool", lambda g, hp=hp, qb=qb: g.dma_start(
                      out=HO_hbm[hp * 128:(hp + 1) * 128, qb * 512:(qb + 1) * 512], in_=HO[:]),
                  reads=[r_HO], writes=[r_HOh], dest=r_HOh)

        NTOT = NQ * NP
        for n in range(NTOT + 2):
            if n >= 2 and n % 2 == 0 and n - 2 < NTOT:
                emit_chain_block(n - 2)
            if n < NTOT:
                qb = n // NP
                if n % NP == 0:
                    q2head(qb)
                emit_S(qb, (n % NP) * 2)
                emit_clamp(n, (n % NP) * 2)
                emit_S(qb, (n % NP) * 2 + 1)
                emit_clamp(n, (n % NP) * 2 + 1)
                feeder()
            if n >= 2 and n - 2 < NTOT:
                emit_PVpair(n - 2)
                if (n - 2) % NP == NP - 1:
                    norm_tail((n - 2) // NP)

    g = prep_gen(0)
    for _ in g:
        pass
    for hp in range(NH // 2):
        gnext = prep_gen(hp + 1) if hp + 1 < NH // 2 else None
        state = {"cnt": 0}
        def feeder():
            state["cnt"] += 1
            if gnext is not None and state["cnt"] % 4 == 0:
                next(gnext, None)
        attention(hp, feeder)
        if gnext is not None:
            for _ in gnext:
                pass

    load_w(C, W, "W_out", wstage, r_ws[0], slice(0, D), "act")
    r_ws[0].readers.extend(r_ws[1].readers)
    ybuf, r_ybuf = C.slabA, C.r_slabA
    for j in range(NT):
        E.dma("sync", lambda s, j=j: s.dma_start(out=sb_slab(ybuf[:]), in_=ap_slab(xT_in, j)),
              reads=[r_xin], writes=[r_ybuf])
        hos, r_hos = C.tslab[j % 2], C.r_tslab[j % 2]
        E.dma("sync", lambda s, j=j, hos=hos: s.dma_start(out=sb_slab(hos[:]), in_=ap_slab(HO_hbm, j)),
              reads=[r_HOh], writes=[r_hos])
        for dc in range(NCH):
            h = dc % 2
            E.emit("pe", lambda t, dc=dc, j=j, h=h: [t.matmul(
                       PC[:, h * 512:(h + 1) * 512],
                       wstage[:, c * D + dc * 128: c * D + (dc + 1) * 128],
                       hos[:, c * 512:(c + 1) * 512],
                       start=(c == 0), stop=(c == NCH - 1)) for c in range(NCH)],
                   reads=[r_ws[0], r_hos], writes=[C.rPC[h]])
            E.emit("dve", lambda v, dc=dc, h=h: v.scalar_tensor_tensor(
                       out=ybuf[:, dc * 512:(dc + 1) * 512], in0=PC[:, h * 512:(h + 1) * 512],
                       scalar=boutc[:, dc:dc + 1], in1=ybuf[:, dc * 512:(dc + 1) * 512],
                       op0=ALU.add, op1=ALU.add),
                   reads=[C.rPC[h], r_boutc], writes=[r_ybuf])
        E.dma("pool", lambda g, j=j: g.dma_start(out=ap_slab(xT_out, j), in_=sb_slab(ybuf[:])),
              reads=[r_ybuf], writes=[r_xout], dest=r_xout)
        emit_ep_ln(C, ybuf, r_ybuf, j, tT_next, r_tTn)


def emit_l1ffn(C, xT_in, r_xin, xT_out, r_xout, tT_hbm, r_tT, W, tag, tT_next=None, r_tTn=None, skip_ln=False):
    E, nc, CT, T, NT = C.E, C.nc, C.CT, C.T, C.NT
    PA, PB, PC, PL = C.PA, C.PB, C.PC, C.PL
    if not skip_ln:
        emit_ln_to_tT(C, xT_in, r_xin, tT_hbm, r_tT)

    out = C.sb(tag + "out", [128, NCH * T], BF16); r_out = E.dma_res(tag + "out")
    gcb = C.sb(tag + "gcb", [128, 16], BF16); r_outb = E.res(tag + "gcb")
    outb = gcb
    h1b = C.sb(tag + "h1", [128, NCH * 512], BF16); r_h1 = E.res(tag + "h1")
    h2b = C.sb(tag + "h2", [128, 4 * 512], BF16); r_h2 = E.res(tag + "h2")
    h3b = C.sb(tag + "h3", [128, 2 * 512], BF16); r_h3 = E.res(tag + "h3")
    tfrow = C.sb(tag + "tf", [1, T], F32); r_tf = E.res(tag + "tf")
    crow = C.sb(tag + "crow", [1, 512], F32); r_crow = E.res(tag + "crow")
    gcacc = C.sb(tag + "gcacc", [128, NCH + 2], F32); r_gcacc = E.res(tag + "gcacc")
    gfbuf = C.sb(tag + "gf", [128, 8], F32); r_gf = E.res(tag + "gf")
    wstage = C.sb("wstage", [128, NCH * D], BF16); r_wstage = E.dma_res("wstage")
    mtmp = C.sb(tag + "mtmp", [128, 512], F32); r_mtmp = E.res(tag + "mtmp")
    ttmp = C.sb(tag + "ttmp", [128, 512], F32); r_ttmp = E.res(tag + "ttmp")
    bvec = C.sb(tag + "bvec", [128, 24], F32); r_bvec = E.dma_res(tag + "bvec")
    bvec_in = nc.dram_tensor(tag + "bvec_in", [128, 24], F32, kind="ExternalInput").ap()
    E.dma("sync", lambda s: s.dma_start(out=bvec[:], in_=bvec_in), writes=[r_bvec])
    B1, B2, B3, G1, G2, SC = 0, 8, 12, 14, 18, 20

    E.emit("dve", lambda v: v.memset(gcacc[:, 0:NCH], 0.0), writes=[r_gcacc])
    for j in range(NT):
        tb, r_tb = C.tslab[j % 2], C.r_tslab[j % 2]
        E.dma("sync", lambda s, j=j: s.dma_start(out=sb_slab(tb[:]), in_=ap_slab(tT_hbm, j)),
              reads=[r_tT], writes=[r_tb])
        for c in range(NCH):
            E.emit("dve", lambda v, c=c, j=j, tb=tb: v.tensor_copy(
                       out=out[:, c * T + j * 512: c * T + (j + 1) * 512],
                       in_=tb[:, c * 512:(c + 1) * 512]),
                   reads=[r_tb], writes=[r_out])
            E.emit("dve", lambda v, c=c, tb=tb: v.tensor_reduce(
                       out=gcacc[:, NCH:NCH + 1], in_=tb[:, c * 512:(c + 1) * 512],
                       axis=AX.X, op=ALU.add),
                   reads=[r_tb], writes=[r_gcacc])
            E.emit("dve", lambda v, c=c: v.tensor_tensor(
                       out=gcacc[:, c:c + 1], in0=gcacc[:, c:c + 1], in1=gcacc[:, NCH:NCH + 1],
                       op=ALU.add) if True else None,
                   reads=[r_gcacc], writes=[r_gcacc])

    E.emit("dve", lambda v: v.tensor_scalar(out=gcacc[:, 0:NCH], in0=gcacc[:, 0:NCH],
                                            scalar1=1.0 / T, scalar2=None, op0=ALU.mult),
           reads=[r_gcacc], writes=[r_gcacc])
    E.emit("dve", lambda v: v.tensor_copy(out=gcb[:, 0:NCH], in_=gcacc[:, 0:NCH]),
           reads=[r_gcacc], writes=[r_outb])
    load_weight_lhsT(C, W["gc_w1"], W["r_gc_w1"], wstage[:, 0:NCH * 512], r_wstage, slice(0, 512), "pool")
    for dc in range(4):
        E.emit("pe", lambda t, dc=dc: [t.matmul(
                   PL[:, dc:dc + 1], wstage[:, c * 512 + dc * 128: c * 512 + (dc + 1) * 128],
                   gcb[:, c:c + 1], start=(c == 0), stop=(c == NCH - 1)) for c in range(NCH)],
               reads=[r_wstage, r_outb], writes=[C.rPL])
    for dc in range(4):
        E.emit("act", lambda s, dc=dc: s.activation(out=gfbuf[:, dc:dc + 1], in_=PL[:, dc:dc + 1],
                                                    func=AF.Gelu, bias=bvec[:, G1 + dc:G1 + dc + 1]),
               reads=[C.rPL, r_bvec], writes=[r_gf])
    load_weight_lhsT(C, W["gc_w2"], W["r_gc_w2"], wstage[:, 0:4 * 256], r_wstage, slice(0, 256), "pool")
    E.emit("dve", lambda v: v.tensor_copy(out=gcb[:, 8:12], in_=gfbuf[:, 0:4]),
           reads=[r_gf], writes=[r_outb])
    for dc in range(2):
        E.emit("pe", lambda t, dc=dc: [t.matmul(
                   PL[:, 8 + dc:9 + dc], wstage[:, c * 256 + dc * 128: c * 256 + (dc + 1) * 128],
                   gcb[:, 8 + c:9 + c], start=(c == 0), stop=(c == 3)) for c in range(4)],
               reads=[r_wstage, r_outb], writes=[C.rPL])
    for dc in range(2):
        E.emit("act", lambda s, dc=dc: s.activation(out=gfbuf[:, 4 + dc:5 + dc], in_=PL[:, 8 + dc:9 + dc],
                                                    func=AF.Gelu, bias=bvec[:, G2 + dc:G2 + dc + 1]),
               reads=[C.rPL, r_bvec], writes=[r_gf])
    E.dma("pool", lambda g: g.dma_start(out=wstage[:, 0:2], in_=W["gc_w3c"]), reads=[W["r_gc_w3c"]], writes=[r_wstage])
    E.emit("dve", lambda v: v.tensor_copy(out=gcb[:, 12:14], in_=gfbuf[:, 4:6]),
           reads=[r_gf], writes=[r_outb])
    E.emit("pe", lambda t: [t.matmul(PL[0:1, 16:17], wstage[:, c:c + 1], gcb[:, 12 + c:13 + c],
                                     start=(c == 0), stop=(c == 1)) for c in range(2)],
           reads=[r_wstage, r_outb], writes=[C.rPL])
    E.emit("act", lambda s: s.activation(out=gfbuf[0:1, 6:7], in_=PL[0:1, 16:17], func=AF.Sigmoid,
                                         bias=bvec[0:1, SC + 1:SC + 2]),
           reads=[C.rPL, r_bvec], writes=[r_gf])
    E.emit("dve", lambda v: v.tensor_scalar(out=gfbuf[0:1, 7:8], in0=gfbuf[0:1, 6:7],
                                            scalar1=0.7, scalar2=None, op0=ALU.mult),
           reads=[r_gf], writes=[r_gf])

    C.w2stage = C.sb(tag + "w2", [128, NCH * 512], BF16); C.r_w2stage = E.dma_res(tag + "w2")
    C.w3stage = C.sb(tag + "w3", [128, 4 * 256], BF16); C.r_w3stage = E.dma_res(tag + "w3")
    C.w4stage = C.sb(tag + "w4", [128, 2 * 1], BF16); C.r_w4stage = E.dma_res(tag + "w4")
    load_weight_lhsT(C, W["cn_w1"], W["r_cn_w1"], wstage, r_wstage, slice(0, D), "pool")
    load_weight_lhsT(C, W["cn_w2"], W["r_cn_w2"], C.w2stage, C.r_w2stage, slice(0, 512), "pool")
    load_weight_lhsT(C, W["cn_w3"], W["r_cn_w3"], C.w3stage, C.r_w3stage, slice(0, 256), "pool")
    E.dma("pool", lambda g: g.dma_start(out=C.w4stage[:], in_=W["cn_w4c"]), reads=[W["r_cn_w4c"]], writes=[C.r_w4stage])
    for cyc in range(3):
        for j in range(NT):
            for dc in range(NCH):
                h = dc % 2
                E.emit("pe", lambda t, dc=dc, h=h, j=j: [t.matmul(
                           PB[:, h * 512:(h + 1) * 512],
                           wstage[:, c * D + dc * 128: c * D + (dc + 1) * 128],
                           out[:, c * T + j * 512: c * T + (j + 1) * 512],
                           start=(c == 0), stop=(c == NCH - 1)) for c in range(NCH)],
                       reads=[r_wstage, r_out], writes=[C.rPB[h]])
                E.emit("act", lambda s, dc=dc, h=h: s.activation(
                           out=h1b[:, dc * 512:(dc + 1) * 512], in_=PB[:, h * 512:(h + 1) * 512],
                           func=AF.Gelu, bias=bvec[:, B1 + dc:B1 + dc + 1]),
                       reads=[C.rPB[h], r_bvec], writes=[r_h1])
            for dc in range(4):
                h = dc % 2
                E.emit("pe", lambda t, dc=dc, h=h: [t.matmul(
                           PC[:, h * 512:(h + 1) * 512],
                           C.w2stage[:, c * 512 + dc * 128: c * 512 + (dc + 1) * 128],
                           h1b[:, c * 512:(c + 1) * 512],
                           start=(c == 0), stop=(c == NCH - 1)) for c in range(NCH)],
                       reads=[C.r_w2stage, r_h1], writes=[C.rPC[h]])
                E.emit("act", lambda s, dc=dc, h=h: s.activation(
                           out=h2b[:, dc * 512:(dc + 1) * 512], in_=PC[:, h * 512:(h + 1) * 512],
                           func=AF.Gelu, bias=bvec[:, B2 + dc:B2 + dc + 1]),
                       reads=[C.rPC[h], r_bvec], writes=[r_h2])
            for dc in range(2):
                E.emit("pe", lambda t, dc=dc: [t.matmul(
                           PL[:, 0:512],
                           C.w3stage[:, c * 256 + dc * 128: c * 256 + (dc + 1) * 128],
                           h2b[:, c * 512:(c + 1) * 512],
                           start=(c == 0), stop=(c == 3)) for c in range(4)],
                       reads=[C.r_w3stage, r_h2], writes=[C.rPL])
                E.emit("act", lambda s, dc=dc: s.activation(
                           out=h3b[:, dc * 512:(dc + 1) * 512], in_=PL[:, 0:512],
                           func=AF.Gelu, bias=bvec[:, B3 + dc:B3 + dc + 1]),
                       reads=[C.rPL, r_bvec], writes=[r_h3])
            E.emit("pe", lambda t: [t.matmul(PL[0:1, 0:512], C.w4stage[:, c:c + 1],
                                             h3b[:, c * 512:(c + 1) * 512],
                                             start=(c == 0), stop=(c == 1)) for c in range(2)],
                   reads=[C.r_w4stage, r_h3], writes=[C.rPL])
            E.emit("act", lambda s, j=j: s.activation(out=tfrow[0:1, j * 512:(j + 1) * 512],
                                                      in_=PL[0:1, 0:512], func=AF.Sigmoid,
                                                      bias=bvec[0:1, SC:SC + 1]),
                   reads=[C.rPL, r_bvec], writes=[r_tf])
            E.emit("dve", lambda v, j=j: v.tensor_scalar(
                       out=crow[:], in0=tfrow[0:1, j * 512:(j + 1) * 512],
                       scalar1=0.3, scalar2=gfbuf[0:1, 7:8], op0=ALU.mult, op1=ALU.add),
                   reads=[r_tf, r_gf], writes=[r_crow])
            E.emit("dve", lambda v: v.tensor_scalar(
                       out=crow[:], in0=crow[:], scalar1=0.5, scalar2=bvec[0:1, SC + 2:SC + 3],
                       op0=ALU.subtract, op1=ALU.mult),
                   reads=[r_crow, r_bvec], writes=[r_crow])
            E.emit("dve", lambda v: v.tensor_scalar(
                       out=crow[:], in0=crow[:], scalar1=0.05, scalar2=2.0,
                       op0=ALU.add, op1=ALU.mult),
                   reads=[r_crow], writes=[r_crow])
            E.emit("pe", lambda t: t.matmul(PL[:, 0:512], C.ones_row_f32[0:1, :], crow[0:1, :],
                                            start=True, stop=True),
                   reads=[r_crow, C.r_ones_row], writes=[C.rPL])
            for a in range(4):
                b = a + 4
                s_a = +1 if a < 2 else -1
                s_b = -s_a
                sla = slice(a * T + j * 512, a * T + (j + 1) * 512)
                slb = slice(b * T + j * 512, b * T + (j + 1) * 512)
                E.emit("dve", lambda v, sla=sla: v.tensor_copy(out=ttmp[:], in_=out[:, sla]),
                       reads=[r_out], writes=[r_ttmp])
                E.emit("dve", lambda v, slb=slb: v.tensor_tensor(out=mtmp[:], in0=out[:, slb],
                                                                 in1=PL[:, 0:512], op=ALU.mult),
                       reads=[r_out, C.rPL], writes=[r_mtmp])
                E.emit("dve", lambda v, sla=sla, s_a=s_a: v.tensor_tensor(
                           out=out[:, sla], in0=out[:, sla], in1=mtmp[:],
                           op=(ALU.add if s_a > 0 else ALU.subtract)),
                       reads=[r_mtmp], writes=[r_out])
                E.emit("dve", lambda v: v.tensor_tensor(out=mtmp[:], in0=ttmp[:],
                                                        in1=PL[:, 0:512], op=ALU.mult),
                       reads=[r_ttmp, C.rPL], writes=[r_mtmp])
                E.emit("dve", lambda v, slb=slb, s_b=s_b: v.tensor_tensor(
                           out=out[:, slb], in0=out[:, slb], in1=mtmp[:],
                           op=(ALU.add if s_b > 0 else ALU.subtract)),
                       reads=[r_mtmp], writes=[r_out])

    ybuf, r_ybuf = C.slabA, C.r_slabA
    for j in range(NT):
        E.dma("sync", lambda s, j=j: s.dma_start(out=sb_slab(ybuf[:]), in_=ap_slab(xT_in, j)),
              reads=[r_xin], writes=[r_ybuf])
        for c in range(NCH):
            E.emit("dve", lambda v, c=c, j=j: v.tensor_tensor(
                       out=ybuf[:, c * 512:(c + 1) * 512], in0=ybuf[:, c * 512:(c + 1) * 512],
                       in1=out[:, c * T + j * 512: c * T + (j + 1) * 512], op=ALU.add),
                   reads=[r_out], writes=[r_ybuf])
        E.dma("pool", lambda g, j=j: g.dma_start(out=ap_slab(xT_out, j), in_=sb_slab(ybuf[:])),
              reads=[r_ybuf], writes=[r_xout], dest=r_xout)
        emit_ep_ln(C, ybuf, r_ybuf, j, tT_next, r_tTn)


def emit_mamba(C, xT_in, r_xin, xT_out, r_xout, tT_hbm, r_tT, W, tT_next=None, r_tTn=None, skip_ln=False):
    E, nc, CT, T, NT = C.E, C.nc, C.CT, C.T, C.NT
    DI, NDI = 2048, 16
    PA, PB, PC, PL = C.PA, C.PB, C.PC, C.PL
    if not skip_ln:
        emit_ln_to_tT(C, xT_in, r_xin, tT_hbm, r_tT)

    inwT = C.sb("mb_inwT", [128, NCH * DI], BF16); r_inwT = E.dma_res("mb_inwT")
    precast(C, W, "in_wT")
    load_w(C, W, "in_wT", inwT, r_inwT, slice(0, DI), "act")
    xpw = C.sb("mb_xpw", [128, 16 * 65], BF16); r_xpw = E.dma_res("mb_xpw")
    E.dma("pool", lambda g: g.dma_start(out=xpw[:].rearrange("p (c m) -> p c m", m=65),
                                        in_=W["xp_wTa"].rearrange("(c p) m -> p c m", p=128)),
          reads=[W["r_xp_wTa"]], writes=[r_xpw])
    outwT = C.sb("mb_outwT", [16, NCH * 128], BF16); r_outwT = E.dma_res("mb_outwT")
    E.dma("pool", lambda g: g.dma_start(out=outwT[:].rearrange("p (c m) -> p c m", m=128),
                                        in_=W["out_wT"].rearrange("p (c m) -> p c m", m=128)),
          reads=[W["r_out_wT"]], writes=[r_outwT])
    vecs = C.sb("mb_vecs", [128, 16 * 3 + 64 + 2], F32); r_vecs = E.dma_res("mb_vecs")
    vecs_in = nc.dram_tensor("mb_vecs_in", [128, 114], F32, kind="ExternalInput").ap()
    E.dma("sync", lambda s: s.dma_start(out=vecs[:], in_=vecs_in), writes=[r_vecs])
    IB, CB, OB, CW, XB = 0, 16, 32, 48, 112
    hraw2 = [C.sb(f"mb_hraw{i}", [128, T], BF16) for i in range(2)]
    r_hraw2 = [E.res(f"mb_hraw{i}") for i in range(2)]
    acc = C.sb("mb_acc", [128, T], BF16); r_acc = E.res("mb_acc")
    xpacc = C.sb("mb_xpacc", [65, T], F32); r_xpacc = E.res("mb_xpacc")
    r16a = C.sb("mb_r16a", [16, T], F32); r_r16a = E.res("mb_r16a")
    r16b = C.sb("mb_r16b", [16, T], F32); r_r16b = E.res("mb_r16b")
    ysb = C.sb("mb_ysb", [16, T], BF16); r_ysb = E.res("mb_ysb")
    hsrow = C.sb("mb_hsrow", [1, T], BF16); r_hsrow = E.res("mb_hsrow")

    E.emit("dve", lambda v: v.memset(xpacc[:], 0.0), writes=[r_xpacc])
    for di in range(NDI):
        hraw, r_hraw = hraw2[di % 2], r_hraw2[di % 2]
        for j in range(NT):
            tb, r_tb = C.tslab[j % 2], C.r_tslab[j % 2]
            if di == 0 or True:
                pass
            E.dma("sync", lambda s, j=j: s.dma_start(out=sb_slab(tb[:]), in_=ap_slab(tT_hbm, j)),
                  reads=[r_tT], writes=[r_tb])
            h2 = j % 2
            Ps = PB if h2 == 0 else PC
            rPs = C.rPB if h2 == 0 else C.rPC
            E.emit("pe", lambda t, di=di, tb=tb, Ps=Ps: [t.matmul(
                       Ps[:, 0:512], inwT[:, c * DI + di * 128: c * DI + (di + 1) * 128],
                       tb[:, c * 512:(c + 1) * 512],
                       start=(c == 0), stop=(c == NCH - 1)) for c in range(NCH)],
                   reads=[r_inwT, r_tb], writes=[rPs[0]])
            E.emit("dve", lambda v, j=j, di=di, Ps=Ps: v.tensor_scalar(
                       out=hraw[:, j * 512:(j + 1) * 512], in0=Ps[:, 0:512],
                       scalar1=vecs[:, IB + di:IB + di + 1], scalar2=None, op0=ALU.add),
                   reads=[rPs[0], r_vecs], writes=[r_hraw])
        E.emit("dve", lambda v, di=di: v.tensor_scalar(
                   out=acc[:], in0=hraw[:], scalar1=vecs[:, CW + di * 4 + 3:CW + di * 4 + 4],
                   scalar2=None, op0=ALU.mult),
               reads=[r_hraw, r_vecs], writes=[r_acc])
        for s_ in range(1, 4):
            E.emit("dve", lambda v, di=di, s_=s_: v.scalar_tensor_tensor(
                       out=acc[:, s_:T], in0=hraw[:, 0:T - s_],
                       scalar=vecs[:, CW + di * 4 + 3 - s_:CW + di * 4 + 4 - s_],
                       in1=acc[:, s_:T], op0=ALU.mult, op1=ALU.add),
                   reads=[r_hraw, r_vecs, r_acc], writes=[r_acc])
        E.emit("act", lambda s, di=di: s.activation(out=hraw[:], in_=acc[:], func=AF.Silu,
                                                    bias=vecs[:, CB + di:CB + di + 1]),
               reads=[r_acc, r_vecs], writes=[r_hraw])
        for j in range(NT):
            E.emit("pe", lambda t, j=j, di=di: t.matmul(
                       PL[0:65, 0:512], xpw[:, di * 65:(di + 1) * 65], hraw[:, j * 512:(j + 1) * 512],
                       start=True, stop=True),
                   reads=[r_xpw, r_hraw], writes=[C.rPL])
            E.emit("dve", lambda v, j=j: v.tensor_tensor(
                       out=xpacc[:, j * 512:(j + 1) * 512], in0=xpacc[:, j * 512:(j + 1) * 512],
                       in1=PL[0:65, 0:512], op=ALU.add),
                   reads=[C.rPL], writes=[r_xpacc])
    E.emit("act", lambda s: s.activation(out=r16a[:], in_=xpacc[0:16, :], func=AF.Exp,
                                         bias=vecs[0:16, XB:XB + 1]),
           reads=[r_xpacc, r_vecs], writes=[r_r16a])
    E.emit("act", lambda s: s.activation(out=r16a[:], in_=r16a[:], func=AF.Exp,
                                         scale=CT.f32(-1.0)),
           reads=[r_r16a, C.r_const], writes=[r_r16a])
    E.emit("act", lambda s: s.activation(out=r16b[:], in_=xpacc[32:48, :], func=AF.Silu,
                                         bias=vecs[32:48, XB + 1:XB + 2]),
           reads=[r_xpacc, r_vecs], writes=[r_r16b])
    E.emit("dve", lambda v: v.tensor_copy(out=hsrow[:], in_=xpacc[64:65, :]),
           reads=[r_xpacc], writes=[r_hsrow])
    for j in range(NT):
        E.emit("pe", lambda t, j=j: t.matmul(
                   PL[0:16, 0:512], C.ones_row_bf[0:1, 0:16], hsrow[0:1, j * 512:(j + 1) * 512],
                   start=True, stop=True),
               reads=[r_hsrow, C.r_ones_row_b], writes=[C.rPL])
        E.emit("dve", lambda v, j=j: v.tensor_tensor(
                   out=r16b[:, j * 512:(j + 1) * 512], in0=r16b[:, j * 512:(j + 1) * 512],
                   in1=PL[0:16, 0:512], op=ALU.mult),
               reads=[C.rPL], writes=[r_r16b])
    E.emit("dve", lambda v: v.tensor_tensor_scan(out=ysb[:], data0=r16a[:], data1=r16b[:],
                                                 initial=0.0, op0=ALU.mult, op1=ALU.add),
           reads=[r_r16a, r_r16b], writes=[r_ysb])
    ybuf, r_ybuf = C.slabA, C.r_slabA
    for j in range(NT):
        E.dma("sync", lambda s, j=j: s.dma_start(out=sb_slab(ybuf[:]), in_=ap_slab(xT_in, j)),
              reads=[r_xin], writes=[r_ybuf])
        for dc in range(NCH):
            h2 = dc % 2
            Ps = PB if h2 == 0 else PC
            rPs = C.rPB if h2 == 0 else C.rPC
            E.emit("pe", lambda t, dc=dc, j=j, Ps=Ps: t.matmul(
                       Ps[:, 0:512], outwT[:, dc * 128:(dc + 1) * 128],
                       ysb[:, j * 512:(j + 1) * 512], start=True, stop=True),
                   reads=[r_outwT, r_ysb], writes=[rPs[0]])
            E.emit("dve", lambda v, dc=dc, Ps=Ps: v.scalar_tensor_tensor(
                       out=ybuf[:, dc * 512:(dc + 1) * 512], in0=Ps[:, 0:512],
                       scalar=vecs[:, OB + dc:OB + dc + 1], in1=ybuf[:, dc * 512:(dc + 1) * 512],
                       op0=ALU.add, op1=ALU.add),
                   reads=[rPs[0], r_vecs], writes=[r_ybuf])
        E.dma("pool", lambda g, j=j: g.dma_start(out=ap_slab(xT_out, j), in_=sb_slab(ybuf[:])),
              reads=[r_ybuf], writes=[r_xout], dest=r_xout)
        emit_ep_ln(C, ybuf, r_ybuf, j, tT_next, r_tTn)


def emit_moe_prologue(C, W):
    E, nc = C.E, C.nc
    M = 1408
    aps = {}
    for nm, shape in (("w_g", (8, D, M)), ("w_u", (8, D, M)), ("w_d", (8, M, D))):
        bf = nc.dram_tensor(nm + "_bfs", list(shape), BF16, kind="Internal").ap()
        r = E.dma_res(nm + "_bf")
        W[nm + "_bf"] = bf
        W["r_" + nm + "_bf"] = r
        for e in range(8):
            q = "pool"
            E.dma(q, lambda g, nm=nm, e=e, bf=bf: g.dma_start(out=bf[e], in_=W[nm][e]),
                  reads=[W["r_" + nm]], writes=[r], dest=r)
    return W



T_FULL = 4096
N_CORES = 8
_FUSED_CACHE = {}


def mk_fused(T):
    C = Ctx(T)
    nc = C.nc
    C.open()
    E = C.E
    xT_in = nc.dram_tensor("xT_in", [D, T], F32, kind="ExternalInput").ap()
    xT_out = nc.dram_tensor("xT_out", [D, T], F32, kind="ExternalOutput").ap()
    scrA = nc.dram_tensor("scrA", [D, T], F32, kind="Internal").ap()
    scrB = nc.dram_tensor("scrB", [D, T], F32, kind="Internal").ap()
    tTs = [nc.dram_tensor(f"tT_scratch{i}", [D, T], BF16, kind="Internal").ap() for i in range(2)]
    r_xin = E.res("xT_in")
    r_A = E.dma_res("scrA")
    r_B = E.dma_res("scrB")
    r_out = E.dma_res("xT_out")
    r_tTs = [E.dma_res(f"tT{i}") for i in range(2)]

    def decl(name, shape):
        ap = nc.dram_tensor(name, list(shape), F32, kind="ExternalInput").ap()
        return ap, E.res(name)

    WA = {}
    for n, shape in (("w_qkv", (D, 3 * D)), ("w_proj", (D, D))):
        WA[n], WA["r_" + n] = decl(n, shape)
    for n in ("bpc", "gammac", "betac"):
        sb_t = C.sb("v_" + n, [128, NCH], F32)
        r = E.dma_res("v_" + n)
        inp = nc.dram_tensor("in_" + n, [128, NCH], F32, kind="ExternalInput").ap()
        E.dma("sync", lambda s, sb_t=sb_t, inp=inp: s.dma_start(out=sb_t[:], in_=inp), writes=[r])
        WA[n] = sb_t
        WA["r_" + n] = r
    WM = {}
    for n, shape in (("w_gate", (D, 8)), ("w_g", (8, D, 1408)), ("w_u", (8, D, 1408)),
                     ("w_d", (8, 1408, D))):
        WM[n], WM["r_" + n] = decl(n, shape)
    WB = {}
    for n in ("w_qT", "w_kT", "w_vT", "w_oT"):
        WB[n], WB["r_" + n] = decl(n, (D, D))
    W5 = {}
    for n, shape in (("W_in", (D, 4 * D)), ("W_out", (D, D))):
        W5[n], W5["r_" + n] = decl(n, shape)
    W6 = {}
    for n, shape in (("cn_w1", (D, D)), ("cn_w2", (D, 512)), ("cn_w3", (512, 256)),
                     ("gc_w1", (D, 512)), ("gc_w2", (512, 256))):
        W6[n], W6["r_" + n] = decl(n, shape)
    for n in ("cn_w4c", "gc_w3c"):
        W6[n], W6["r_" + n] = decl(n, (128, 2))
    W7 = {}
    for n, shape in (("in_wT", (D, 2048)), ("xp_wTa", (2048, 65)), ("out_wT", (16, D))):
        W7[n], W7["r_" + n] = decl(n, shape)

    emit_moe_prologue(C, WM)
    chain = [(xT_in, r_xin), (scrA, r_A), (scrB, r_B)]
    seq = [("attn_a0", None), ("moe", "m2_"), ("attn_b0", None), ("moe", "m4_"),
           ("attn_a1", None), ("l1ffn", "f5_"), ("mamba", None), ("l1ffn", "f7_")]
    cur = chain[0]
    for si, (kind, tag) in enumerate(seq):
        nxt = (xT_out, r_out) if si == len(seq) - 1 else chain[1 + (si % 2)]
        tT, r_tT = tTs[si % 2], r_tTs[si % 2]
        if si < len(seq) - 1:
            tT_next, r_tTn = tTs[(si + 1) % 2], r_tTs[(si + 1) % 2]
        else:
            tT_next, r_tTn = None, None
        kw = dict(tT_next=tT_next, r_tTn=r_tTn, skip_ln=(si > 0))
        C.push_scope(f"s{si}_")
        if kind == "attn_a0":
            emit_attn_a0(C, cur[0], cur[1], nxt[0], nxt[1], tT, r_tT, WA, **kw)
        elif kind == "moe":
            emit_moe(C, cur[0], cur[1], nxt[0], nxt[1], tT, r_tT, WM, tag, **kw)
        elif kind == "attn_b0":
            emit_attn_b0(C, cur[0], cur[1], nxt[0], nxt[1], tT, r_tT, WB, **kw)
        elif kind == "attn_a1":
            emit_attn_a1(C, cur[0], cur[1], nxt[0], nxt[1], tT, r_tT, W5, **kw)
        elif kind == "l1ffn":
            emit_l1ffn(C, cur[0], cur[1], nxt[0], nxt[1], tT, r_tT, W6, tag, **kw)
        elif kind == "mamba":
            emit_mamba(C, cur[0], cur[1], nxt[0], nxt[1], tT, r_tT, W7, **kw)
        C.barrier()
        C.pop_scope()
        cur = nxt
    E.wait_all("sync")
    C.finish()
    return C


def _fused_inputs(C, P):
    d = dict(C.host_consts)
    d["w_qkv"] = P["L0_attn_a_w_qkv"]
    d["w_proj"] = P["L0_attn_a_w_proj"]
    d["in_bpc"] = chunked_vec(P["L0_attn_a_b_proj"])
    d["in_gammac"] = chunked_vec(P["L0_attn_a_ln_gamma"])
    d["in_betac"] = chunked_vec(P["L0_attn_a_ln_beta"])
    d["w_gate"] = P["L0_ffn_w_gate"]
    d["w_g"] = P["L0_ffn_w_g"]
    d["w_u"] = P["L0_ffn_w_u"]
    d["w_d"] = P["L0_ffn_w_d"]
    d["w_qT"] = np.ascontiguousarray(P["L0_attn_b_w_q"].T)
    d["w_kT"] = np.ascontiguousarray(P["L0_attn_b_w_k"].T)
    d["w_vT"] = np.ascontiguousarray(P["L0_attn_b_w_v"].T)
    d["w_oT"] = np.ascontiguousarray(P["L0_attn_b_w_o"].T)
    d["W_in"] = P["L1_attn_a_W_in"]
    d["W_out"] = P["L1_attn_a_W_out"]
    d["a1_binc_in"] = chunked_vec(P["L1_attn_a_b_in"])
    d["a1_boutc_in"] = chunked_vec(P["L1_attn_a_b_out"])
    for n in ("cn_w1", "cn_w2", "cn_w3", "gc_w1", "gc_w2"):
        d[n] = P["L1_ffn_" + n]
    d["cn_w4c"] = chunked_vec(P["L1_ffn_cn_w4"][:, 0])
    d["gc_w3c"] = chunked_vec(P["L1_ffn_gc_w3"][:, 0])
    bvec = np.zeros((128, 24), np.float32)
    bvec[:, 0:8] = chunked_vec(P["L1_ffn_cn_b1"])
    bvec[:, 8:12] = chunked_vec(P["L1_ffn_cn_b2"])
    bvec[:, 12:14] = chunked_vec(P["L1_ffn_cn_b3"])
    bvec[:, 14:18] = chunked_vec(P["L1_ffn_gc_b1"])
    bvec[:, 18:20] = chunked_vec(P["L1_ffn_gc_b2"])
    bvec[:, 20] = P["L1_ffn_cn_b4"][0]
    bvec[:, 21] = P["L1_ffn_gc_b3"][0]
    bvec[:, 22] = P["L1_ffn_adaptive_range"][0]
    d["f5_bvec_in"] = bvec
    d["f7_bvec_in"] = bvec
    DI = 2048
    d["in_wT"] = np.ascontiguousarray(P["L1_attn_b_in_w"].T)
    xp_w = P["L1_attn_b_xp_w"]
    xpa = np.zeros((DI, 65), np.float32)
    xpa[:, 0:16] = xp_w[0:16].T
    xpa[:, 32:48] = xp_w[16:32].T
    xpa[:, 64] = 1.0
    d["xp_wTa"] = xpa
    d["out_wT"] = np.ascontiguousarray(P["L1_attn_b_out_w"].T)
    vecs = np.zeros((128, 114), np.float32)
    vecs[:, 0:16] = chunked_vec(P["L1_attn_b_in_b"])
    vecs[:, 16:32] = chunked_vec(P["L1_attn_b_conv_b"])
    vecs[:, 32:40] = chunked_vec(P["L1_attn_b_out_b"])
    cw = P["L1_attn_b_conv_w"][:, 0, :]
    vecs[:, 48:112] = cw.reshape(16, 128, 4).transpose(1, 0, 2).reshape(128, 64)
    vecs[0:16, 112] = P["L1_attn_b_xp_b"][0:16]
    vecs[32:48, 113] = P["L1_attn_b_xp_b"][16:32]
    d["mb_vecs_in"] = vecs
    return d


def kernel(**P):
    from concourse.bass_utils import run_bass_kernel_spmd
    if T_FULL not in _FUSED_CACHE:
        _FUSED_CACHE[T_FULL] = mk_fused(T_FULL)
    C = _FUSED_CACHE[T_FULL]
    x = np.asarray(P["L0_attn_a_target"])
    B = x.shape[0]
    shared = _fused_inputs(C, {k: np.asarray(v) for k, v in P.items()})
    in_maps = [dict(shared, xT_in=np.ascontiguousarray(x[b].T).astype(np.float32))
               for b in range(B)]
    res = run_bass_kernel_spmd(C.nc, in_maps, core_ids=list(range(N_CORES)))
    out = np.stack([np.asarray(res.results[b]["xT_out"]).T for b in range(B)], axis=0)
    return out.astype(np.float32)
